# version 18
# baseline (speedup 1.0000x reference)
import numpy as np
from contextlib import ExitStack
import concourse.bass as bass
import concourse.mybir as mybir
from concourse.bass_utils import run_bass_kernel_spmd

F32 = mybir.dt.float32
BF16 = mybir.dt.bfloat16
AF = mybir.ActivationFunctionType
ALU = mybir.AluOpType
AX = mybir.AxisListType

L = 2048
D = 1024
DI = 2048
NS = 16
NT = 16
NCORES = 8
EPS = 1e-5
DMA_INC = 16


class KB:

    def __init__(self):
        self.plan = []
        self.bufs = {}

    def _norm(self, spec):
        if isinstance(spec, str):
            return (spec, 0, 1 << 30)
        return spec

    PSUM_KEYS = frozenset(["psA", "psB", "psC", "psD", "psE", "psV", "psT", "psU"])

    def op(self, eng, fn, r=(), w=(), kind="compute", sem=None):
        deps = set()
        r = [self._norm(s) for s in r]
        w = [self._norm(s) for s in w]
        w = w + [s for s in r if s[0] in self.PSUM_KEYS]
        r = [s for s in r if s[0] not in self.PSUM_KEYS]
        for (name, lo, hi) in r:
            for ent in self.bufs.get(name, ()):
                if ent[0] < hi and lo < ent[1] and ent[2] is not None:
                    deps.add(ent[2])
        for (name, lo, hi) in w:
            for ent in self.bufs.get(name, ()):
                if ent[0] < hi and lo < ent[1]:
                    if ent[2] is not None:
                        deps.add(ent[2])
                    deps.update(ent[3])
        i = len(self.plan)
        self.plan.append(dict(eng=eng, fn=fn, deps=sorted(deps), kind=kind,
                              sem=sem, marked=False, token=None, waits=None))
        for (name, lo, hi) in r:
            lst = self.bufs.setdefault(name, [])
            for ent in lst:
                if ent[0] < hi and lo < ent[1]:
                    ent[3].add(i)
        for (name, lo, hi) in w:
            lst = self.bufs.setdefault(name, [])
            lst[:] = [ent for ent in lst if not (lo <= ent[0] and ent[1] <= hi)]
            lst.append([lo, hi, i, set()])
        return i

    def _queue(self, o):
        if o["sem"] is not None:
            return o["sem"]
        return "dma_s" if o["eng"] == "sync" else "dma_g"

    def resolve(self):
        POOLS = {"dma_s": ["dsa", "dsb", "dsc", "dsd"],
                 "dma_g": ["dga", "dgb", "dgc"],
                 "dout": ["dout"]}
        semcum = {}
        qcur = {q: None for q in POOLS}
        qopen = {q: [] for q in POOLS}
        qsi = {q: -1 for q in POOLS}
        group_close = {}
        needed = [[] for _ in self.plan]
        prewaits = [[] for _ in self.plan]

        def close_group(q):
            sem = qcur[q]
            if sem is None:
                return
            v = semcum.get(sem, 0)
            for j in qopen[q]:
                group_close[j] = (sem, v)
            qopen[q] = []
            qcur[q] = None

        for i, o in enumerate(self.plan):
            for d in o["deps"]:
                p = self.plan[d]
                if p["kind"] == "compute":
                    p["marked"] = True
                    needed[i].append(d)
                else:
                    q = self._queue(p)
                    if d in qopen[q]:
                        close_group(q)
                    needed[i].append(d)
            if o["kind"] == "dma":
                q = self._queue(o)
                if qcur[q] is None:
                    qsi[q] += 1
                    sem = POOLS[q][qsi[q] % len(POOLS[q])]
                    if semcum.get(sem, 0) > 0:
                        prewaits[i].append((sem, semcum[sem]))
                    qcur[q] = sem
                sem = qcur[q]
                semcum[sem] = semcum.get(sem, 0) + DMA_INC
                o["marked"] = True
                o["token"] = (sem, semcum[sem], DMA_INC)
                qopen[q].append(i)
        for q in POOLS:
            close_group(q)

        counters = dict(semcum)
        for o in self.plan:
            if o["marked"] and o["kind"] == "compute":
                s = o["eng"]
                counters[s] = counters.get(s, 0) + 1
                o["token"] = (s, counters[s], 1)

        marks = {}
        for i, o in enumerate(self.plan):
            wl = {}
            for (s, v) in prewaits[i]:
                if wl.get(s, 0) < v:
                    wl[s] = v
            for d in needed[i]:
                p = self.plan[d]
                if p["kind"] == "dma":
                    s, v = group_close[d]
                else:
                    s, v, _ = p["token"]
                if wl.get(s, 0) < v:
                    wl[s] = v
            final = []
            for s, v in wl.items():
                key = (o["eng"], s)
                if marks.get(key, 0) < v:
                    marks[key] = v
                    final.append((s, v))
            o["waits"] = final
        return counters

    def emit(self, block, sems, final_waits):
        for name in ["sync", "gpsimd", "tensor", "vector", "scalar"]:
            ops = [o for o in self.plan if o["eng"] == name]

            def make(eng_ops, engname):
                def body(eng):
                    for o in eng_ops:
                        for (s, v) in o["waits"]:
                            eng.wait_ge(sems[s], v)
                        inst = o["fn"](eng)
                        if o["marked"]:
                            s, v, inc = o["token"]
                            inst.then_inc(sems[s], inc)
                    if engname == "sync":
                        for (s, v) in final_waits:
                            eng.wait_ge(sems[s], v)
                return body

            getattr(block, name)(make(ops, name))


_PHASES = [7]


def _fin(nc, kb, sems, es, dy, bigf, T0):
    for lt in range(NT):
        def f_out(e, lt=lt):
            return e.dma_start(out=dy[lt * 128:(lt + 1) * 128, :],
                               in_=bigf[:, T0:T0 + 1024])
        kb.op("sync", f_out, kind="dma", r=[("bigf", T0, T0 + 1024)],
              w=[("dy", lt, lt + 1)], sem="dout")
    counters = kb.resolve()
    with nc.Block() as block:
        kb.emit(block, sems, final_waits=[("dout", counters.get("dout", 0))])
    es.close()
    return nc



def build_nc():
    nc = bass.Bass("TRN2", target_bir_lowering=False, debug=False)

    dx = nc.dram_tensor("x", [L, D], F32, kind="ExternalInput").ap()
    dw_in = nc.dram_tensor("w_in", [128, 32, 8, 128], F32, kind="ExternalInput").ap()
    dw_x = nc.dram_tensor("w_x", [DI, D + 2 * NS], F32, kind="ExternalInput").ap()
    dw_dt = nc.dram_tensor("w_dt", [128, 16, 8, 128], F32, kind="ExternalInput").ap()
    dw_out = nc.dram_tensor("w_out", [DI, D], F32, kind="ExternalInput").ap()
    dffn_w = nc.dram_tensor("ffn_w", [8, D, D], F32, kind="ExternalInput").ap()
    dffn_b = nc.dram_tensor("ffn_b", [8, D], F32, kind="ExternalInput").ap()
    dconv_wt = nc.dram_tensor("conv_wt", [128, 64], F32, kind="ExternalInput").ap()
    dconv_b2 = nc.dram_tensor("conv_b2", [128, 16], F32, kind="ExternalInput").ap()
    db_dt2 = nc.dram_tensor("b_dt2", [128, 16], F32, kind="ExternalInput").ap()
    da_log2 = nc.dram_tensor("a_log2", [128, 256], F32, kind="ExternalInput").ap()
    dlogits = nc.dram_tensor("logits1", [1, 8], F32, kind="ExternalInput").ap()
    du1 = nc.dram_tensor("u1", [1, 8], F32, kind="ExternalInput").ap()
    dident = nc.dram_tensor("ident", [128, 128], F32, kind="ExternalInput").ap()
    dsel = nc.dram_tensor("sel16", [NS, 2048], F32, kind="ExternalInput").ap()
    dy = nc.dram_tensor("y", [L, D], F32, kind="ExternalOutput").ap()

    d_xconv = nc.dram_tensor("d_xconv", [NT, 128, L], BF16, kind="Internal").ap()
    d_sz = nc.dram_tensor("d_sz", [NT, 128, L], BF16, kind="Internal").ap()
    d_yg = nc.dram_tensor("d_yg", [NT, 128, L], BF16, kind="Internal").ap()
    d_weff = nc.dram_tensor("d_weff", [128, 8 * D], BF16, kind="Internal").ap()
    d_mix = nc.dram_tensor("d_mix", [8, 128, D], BF16, kind="Internal").ap()
    d_mask = nc.dram_tensor("d_mask", [1, 8], F32, kind="Internal").ap()

    kb = KB()

    with ExitStack() as es:
        sb = lambda n, shp, dt: es.enter_context(nc.sbuf_tensor(n, shp, dt))
        pt = lambda n, shp, dt: es.enter_context(nc.psum_tensor(n, shp, dt))

        arena = sb("arena", [128, 8 * L], BF16)
        big_a = sb("big_a", [128, 16 * L], BF16)
        big_b = sb("big_b", [128, 27392], BF16)
        T0, T1, T2, T3 = 0, 2052, 4132, 6180
        bigf = sb("bigf", [128, 8228], F32)
        bflat = big_b[0:1, 18432:20480]
        identb = big_b[:, 27136:27264]
        ones1 = big_b[0:1, 27264:27392]
        beffrow = big_b[0:1, 20480:21504]
        sm = sb("sm", [128, 7 * 1024], BF16)
        pa = sb("pa", [128, 256], F32)
        wcol = sb("wcol", [128, 64], F32)
        cbcol = sb("cbcol", [128, 16], F32)
        bdtcol = sb("bdtcol", [128, 16], F32)
        ones8 = sb("ones8", [8, 1], BF16)
        lgu = sb("lgu", [1, 32], F32)
        maskr = sb("maskr", [1, 8], F32)
        maskrb = sb("maskrb", [1, 8], BF16)
        maskt = sb("maskt", [8, 1], F32)
        maskrep = sb("maskrep", [128, 8], F32)
        cpool = sb("cpool", [128, 8], F32)
        hcol = sb("hcol", [128, 256], F32)
        convh = sb("convh", [128, 2051], F32)
        tails = sb("tails", [128, 48], F32)

        psA = pt("psA", [128, 512], F32)
        psB = pt("psB", [128, 512], F32)
        psC = pt("psC", [128, 512], F32)
        psD = pt("psD", [128, 512], F32)
        psE = pt("psE", [128, 512], F32)
        psV = pt("psV", [128, 512], F32)
        psT = pt("psT", [128, 1024], BF16)
        psU = pt("psU", [128, 1024], BF16)

        sems = {n: es.enter_context(nc.semaphore(n)) for n in
                ["dsa", "dsb", "dsc", "dsd", "dga", "dgb", "dgc",
                 "tensor", "vector", "scalar", "gpsimd", "dout"]}

        kb.op("vector", lambda e: e.memset(cpool[:, 0:1], EPS), w=["cpool"])
        kb.op("vector", lambda e: e.memset(cpool[:, 1:2], 1.0), w=["cpool"])
        kb.op("sync", lambda e: e.dma_start(out=pa[:, 0:256], in_=da_log2), kind="dma",
              w=[("pa", 0, 256)])
        kb.op("scalar", lambda e: e.activation(pa[:, 0:256], pa[:, 0:256], AF.Exp),
              r=[("pa", 0, 256)], w=[("pa", 0, 256)])
        kb.op("vector", lambda e: e.tensor_scalar(pa[:, 0:256], pa[:, 0:256], -1.0,
                                                  None, ALU.mult),
              r=[("pa", 0, 256)], w=[("pa", 0, 256)])
        kb.op("sync", lambda e: e.dma_start(out=wcol[:], in_=dconv_wt),
              kind="dma", w=["wcol"])
        kb.op("sync", lambda e: e.dma_start(out=cbcol[:], in_=dconv_b2), kind="dma", w=["cbcol"])
        kb.op("sync", lambda e: e.dma_start(out=bdtcol[:], in_=db_dt2), kind="dma", w=["bdtcol"])
        kb.op("gpsimd", lambda e: e.dma_start(out=identb, in_=dident), kind="dma", w=[("big_b", 27136, 27264)])
        kb.op("vector", lambda e: e.memset(ones1, 1.0), w=[("big_b", 27264, 27392)])
        kb.op("vector", lambda e: e.memset(ones8[:], 1.0), w=["ones8"])
        kb.op("vector", lambda e: e.memset(convh[:, 0:3], 0.0), w=[("convh", 0, 3)])

        kb.op("sync", lambda e: e.dma_start(out=lgu[:, 0:8], in_=dlogits), kind="dma",
              w=[("lgu", 0, 8)])
        kb.op("sync", lambda e: e.dma_start(out=lgu[:, 8:16], in_=du1), kind="dma",
              w=[("lgu", 8, 16)])
        kb.op("scalar", lambda e: e.activation(lgu[:, 16:24], lgu[:, 8:16], AF.Ln),
              r=[("lgu", 8, 16)], w=[("lgu", 16, 24)])
        kb.op("vector", lambda e: e.tensor_scalar(lgu[:, 16:24], lgu[:, 16:24], -1.0, None,
                                                  ALU.mult),
              r=[("lgu", 16, 24)], w=[("lgu", 16, 24)])
        kb.op("scalar", lambda e: e.activation(lgu[:, 16:24], lgu[:, 16:24], AF.Ln),
              r=[("lgu", 16, 24)], w=[("lgu", 16, 24)])
        kb.op("vector", lambda e: e.tensor_sub(lgu[:, 24:32], lgu[:, 0:8], lgu[:, 16:24]),
              r=[("lgu", 0, 24)], w=[("lgu", 24, 32)])
        kb.op("vector", lambda e: e.reduce_max(lgu[:, 16:17], lgu[:, 24:32], axis=AX.X),
              r=[("lgu", 24, 32)], w=[("lgu", 16, 17)])
        kb.op("vector", lambda e: e.tensor_scalar(maskr[:], lgu[:, 24:32], lgu[:, 16:17],
                                                  None, ALU.is_ge),
              r=[("lgu", 16, 32)], w=["maskr"])
        kb.op("vector", lambda e: e.tensor_copy(maskrb[:], maskr[:]), r=["maskr"], w=["maskrb"])
        kb.op("sync", lambda e: e.dma_start(out=d_mask, in_=maskr[:]), kind="dma",
              r=["maskr"], w=["d_mask"])
        kb.op("sync", lambda e: e.dma_start(out=maskt[:], in_=d_mask.rearrange("1 k -> k 1")),
              kind="dma", r=["d_mask"], w=["maskt"])
        kb.op("tensor", lambda e: e.matmul(psE[:, 0:8], ones1, maskrb[:], start=True,
                                           stop=True),
              r=[("big_b", 27264, 27392), "maskrb"], w=["psE"])
        kb.op("vector", lambda e: e.tensor_copy(maskrep[:], psE[:, 0:8]), r=["psE"],
              w=["maskrep"])

        MD0 = 24064
        for k in range(8):
            def f_md(e, k=k):
                return e.tensor_scalar(big_b[:, MD0 + k * 128:MD0 + (k + 1) * 128],
                                       identb, maskrep[:, k:k + 1], None, ALU.mult)
            kb.op("vector", f_md,
                  r=[("big_b", 27136, 27264), "maskrep"],
                  w=[("big_b", MD0 + k * 128, MD0 + (k + 1) * 128)])
        weff_steps = []

        def _weff_step(dt_, k):
            def go():
                lane = 4 + k % 2
                def f_load(e, k=k, dt_=dt_, lane=lane):
                    return e.dma_start(out=sm[:, lane * 1024:(lane + 1) * 1024],
                                       in_=dffn_w[k, dt_ * 128:(dt_ + 1) * 128, :])
                kb.op("gpsimd", f_load, kind="dma",
                      w=[("sm", lane * 1024, (lane + 1) * 1024)])
                for n2 in range(2):
                    ps, pk = (psC, "psC") if n2 == 0 else (psD, "psD")
                    def f_mm(e, k=k, n2=n2, ps=ps, lane=lane):
                        return e.matmul(ps[:, 0:512],
                                        big_b[:, MD0 + k * 128:MD0 + (k + 1) * 128],
                                        sm[:, lane * 1024 + n2 * 512:lane * 1024 + n2 * 512 + 512],
                                        start=(k == 0), stop=(k == 7))
                    kb.op("tensor", f_mm,
                          r=[("big_b", MD0 + k * 128, MD0 + (k + 1) * 128),
                             ("sm", lane * 1024, (lane + 1) * 1024)], w=[pk])
                if k == 7:
                    for n2 in range(2):
                        ps, pk = (psC, "psC") if n2 == 0 else (psD, "psD")
                        def f_ev(e, dt_=dt_, n2=n2, ps=ps):
                            return e.copy(
                                sm[:, 6144 + n2 * 512:6144 + n2 * 512 + 512],
                                ps[:, 0:512])
                        kb.op("scalar", f_ev, r=[pk],
                              w=[("sm", 6144 + n2 * 512, 6144 + n2 * 512 + 512)])
                    def f_sp(e, dt_=dt_):
                        return e.dma_start(
                            out=d_weff[:, dt_ * 1024:(dt_ + 1) * 1024],
                            in_=sm[:, 6144:6144 + 1024])
                    kb.op("sync", f_sp, kind="dma", r=[("sm", 6144, 7168)],
                          w=[("d_weff", dt_ * 1024, (dt_ + 1) * 1024)])
            return go

        for dt_ in range(8):
            for k in range(8):
                weff_steps.append(_weff_step(dt_, k))

        pscnt = [0]

        def next_ps():
            pscnt[0] += 1
            return [(psA, "psA"), (psB, "psB"), (psE, "psE")][pscnt[0] % 3]

        LN1T = ("arena", 0, 8192)

        def emit_p1_lt(h, j):
            lt = h * 8 + j
            stage_lo = None
            def f_ldx(e, lt=lt):
                return e.dma_start(out=convh[:, 0:1024],
                                   in_=dx[lt * 128:(lt + 1) * 128, :])
            kb.op("sync", f_ldx, kind="dma", w=[("convh", 0, 1024)])
            XT = ("convh", 0, 1024)
            SQ = ("convh", 1024, 2048)
            sb0 = 112
            kb.op("scalar", lambda e: e.activation(convh[:, 1024:2048], convh[:, 0:1024],
                                                   AF.Copy,
                                                   accum_out=statn[:, sb0:sb0 + 1]),
                  r=[XT], w=[SQ, ("statn", sb0, sb0 + 1)])
            kb.op("scalar", lambda e: e.activation(convh[:, 1024:2048], convh[:, 0:1024],
                                                   AF.Square,
                                                   accum_out=statn[:, sb0 + 1:sb0 + 2]),
                  r=[XT], w=[SQ, ("statn", sb0 + 1, sb0 + 2)])
            kb.op("vector", lambda e: e.tensor_scalar(statn[:, sb0 + 2:sb0 + 3], statn[:, sb0:sb0 + 1],
                                                      1.0 / D, None, ALU.mult),
                  r=[("statn", sb0, sb0 + 1)], w=[("statn", sb0 + 2, sb0 + 3)])
            kb.op("vector", lambda e: e.tensor_scalar(statn[:, sb0 + 3:sb0 + 4], statn[:, sb0 + 1:sb0 + 2],
                                                      1.0 / D, None, ALU.mult),
                  r=[("statn", sb0 + 1, sb0 + 2)], w=[("statn", sb0 + 3, sb0 + 4)])
            kb.op("vector", lambda e: e.tensor_mul(statn[:, sb0 + 4:sb0 + 5], statn[:, sb0 + 2:sb0 + 3],
                                                   statn[:, sb0 + 2:sb0 + 3]),
                  r=[("statn", sb0 + 2, sb0 + 3)], w=[("statn", sb0 + 4, sb0 + 5)])
            kb.op("vector", lambda e: e.tensor_sub(statn[:, sb0 + 5:sb0 + 6], statn[:, sb0 + 3:sb0 + 4],
                                                   statn[:, sb0 + 4:sb0 + 5]),
                  r=[("statn", sb0 + 3, sb0 + 5)], w=[("statn", sb0 + 5, sb0 + 6)])
            kb.op("scalar", lambda e: e.activation(statn[:, sb0 + 6:sb0 + 7], statn[:, sb0 + 5:sb0 + 6],
                                                   AF.Ln, bias=cpool[:, 0:1]),
                  r=[("statn", sb0 + 5, sb0 + 6), "cpool"], w=[("statn", sb0 + 6, sb0 + 7)])
            kb.op("scalar", lambda e: e.activation(statn[:, sb0 + 7:sb0 + 8], statn[:, sb0 + 6:sb0 + 7],
                                                   AF.Exp, scale=-0.5),
                  r=[("statn", sb0 + 6, sb0 + 7)], w=[("statn", sb0 + 7, sb0 + 8)])
            kb.op("vector", lambda e: e.tensor_scalar(statn[:, sb0 + 8:sb0 + 9],
                                                      statn[:, sb0 + 2:sb0 + 3],
                                                      statn[:, sb0 + 7:sb0 + 8], -1.0,
                                                      ALU.mult, ALU.mult),
                  r=[("statn", sb0 + 2, sb0 + 3), ("statn", sb0 + 7, sb0 + 8)],
                  w=[("statn", sb0 + 8, sb0 + 9)])
            if h == 0:
                stg_ap, STG = sm[:, 2048:3072], ("sm", 2048, 3072)
            else:
                stg_ap, STG = big_b[:, 24064:25088], ("big_b", 24064, 25088)
            def f_norm(e, stg_ap=stg_ap):
                return e.activation(stg_ap, convh[:, 0:1024], AF.Identity,
                                    bias=statn[:, sb0 + 8:sb0 + 9],
                                    scale=statn[:, sb0 + 7:sb0 + 8])
            kb.op("scalar", f_norm, r=[XT, ("statn", sb0 + 7, sb0 + 9)], w=[STG])
            for dt_ in range(8):
                bank, bk = (psT, "psT") if dt_ % 2 == 0 else (psU, "psU")
                def f_tr(e, dt_=dt_, bank=bank, stg_ap=stg_ap):
                    return e.transpose(bank[:, 0:128],
                                       stg_ap[:, dt_ * 128:dt_ * 128 + 128], identb)
                kb.op("tensor", f_tr, r=[STG, ("big_b", 27136, 27264)], w=[bk])
                def f_cp(e, j=j, dt_=dt_, bank=bank):
                    return e.copy(
                        arena[:, dt_ * 1024 + j * 128:dt_ * 1024 + j * 128 + 128],
                        bank[:, 0:128])
                kb.op("scalar", f_cp, r=[bk],
                      w=[("arena", dt_ * 1024 + j * 128, dt_ * 1024 + j * 128 + 128)])

        WINB = 16896

        def emit_p2_ct(h, cti, ct_order):
            ct = ct_order[cti]
            if h == 0:
                nring, rbase = 4, WINB
            else:
                nring, rbase = 2, 25088
            lane = cti % nring
            WI = ("big_b", rbase + lane * 1024, rbase + (lane + 1) * 1024)
            if cti < nring:
                def f_ldw0(e, ct=ct, lane=lane, rbase=rbase):
                    return e.dma_start(
                        out=big_b[:, rbase + lane * 1024:rbase + lane * 1024 + 1024],
                        in_=dw_in[:, ct].rearrange("p dt c -> p (dt c)"))
                kb.op("gpsimd", f_ldw0, kind="dma", w=[WI])
            for lc2 in range(2):
                ps, pk = next_ps()
                for dt_ in range(8):
                    def f_mm(e, lc2=lc2, dt_=dt_, ps=ps, lane=lane, rbase=rbase):
                        return e.matmul(
                            ps[:, 0:512],
                            big_b[:, rbase + lane * 1024 + dt_ * 128:rbase + lane * 1024 + dt_ * 128 + 128],
                            arena[:, dt_ * 1024 + lc2 * 512:dt_ * 1024 + lc2 * 512 + 512],
                            start=(dt_ == 0), stop=(dt_ == 7))
                    kb.op("tensor", f_mm,
                          r=[WI, ("arena", dt_ * 1024 + lc2 * 512,
                                  dt_ * 1024 + lc2 * 512 + 512)], w=[pk])
                if ct < 16:
                    def f_ev(e, lc2=lc2, ps=ps):
                        return e.copy(convh[:, 3 + lc2 * 512:3 + lc2 * 512 + 512],
                                      ps[:, 0:512])
                    kb.op("scalar", f_ev, r=[pk],
                          w=[("convh", 3 + lc2 * 512, 3 + lc2 * 512 + 512)])
                else:
                    if h == 0:
                        zstg, ZS = sm[:, 2048 + lc2 * 512:2048 + lc2 * 512 + 512], \
                            ("sm", 2048 + lc2 * 512, 2048 + lc2 * 512 + 512)
                    else:
                        zstg, ZS = big_b[:, 24064 + lc2 * 512:24064 + lc2 * 512 + 512], \
                            ("big_b", 24064 + lc2 * 512, 24064 + lc2 * 512 + 512)
                    def f_silu(e, ps=ps, zstg=zstg):
                        return e.activation(zstg, ps[:, 0:512], AF.Silu)
                    kb.op("scalar", f_silu, r=[pk], w=[ZS])
            if ct < 16:
                if h == 1:
                    def f_pad(e, ct=ct):
                        return e.tensor_copy(convh[:, 0:3], tails[:, ct * 3:ct * 3 + 3])
                    kb.op("vector", f_pad, r=[("tails", ct * 3, ct * 3 + 3)],
                          w=[("convh", 0, 3)])
                else:
                    def f_tail(e, ct=ct):
                        return e.tensor_copy(tails[:, ct * 3:ct * 3 + 3], convh[:, 1024:1027])
                    kb.op("vector", f_tail, r=[("convh", 1024, 1027)],
                          w=[("tails", ct * 3, ct * 3 + 3)])
                XP = ("convh", 0, 1027)
                def f_c0(e, ct=ct):
                    return e.tensor_scalar(convh[:, 1027:1027 + 1024], convh[:, 0:1024],
                                           wcol[:, ct * 4:ct * 4 + 1], None, ALU.mult)
                kb.op("vector", f_c0, r=[XP, "wcol"], w=[("convh", 1027, 2051)])
                for k in range(1, 4):
                    def f_ck(e, ct=ct, k=k):
                        return e.scalar_tensor_tensor(
                            convh[:, 1027:1027 + 1024], convh[:, k:k + 1024],
                            wcol[:, ct * 4 + k:ct * 4 + k + 1],
                            convh[:, 1027:1027 + 1024], ALU.mult, ALU.add)
                    kb.op("vector", f_ck, r=[XP, "wcol", ("convh", 1027, 2051)],
                          w=[("convh", 1027, 2051)])
                if h == 0:
                    def f_si(e, ct=ct):
                        return e.activation(big_a[:, ct * 1024:(ct + 1) * 1024],
                                            convh[:, 1027:1027 + 1024],
                                            AF.Silu, bias=cbcol[:, ct:ct + 1])
                    kb.op("scalar", f_si, r=[("convh", 1027, 2051), "cbcol"],
                          w=[("big_a", ct * 1024, (ct + 1) * 1024)])
                    def f_sp(e, ct=ct):
                        return e.dma_start(out=d_xconv[ct][:, 0:1024],
                                           in_=big_a[:, ct * 1024:(ct + 1) * 1024])
                    kb.op("sync", f_sp, kind="dma", r=[("big_a", ct * 1024, (ct + 1) * 1024)],
                          w=[("d_xconv", ct * 2, ct * 2 + 1)])
                else:
                    def f_si(e, ct=ct):
                        return e.activation(big_b[:, 24064:25088],
                                            convh[:, 1027:1027 + 1024],
                                            AF.Silu, bias=cbcol[:, ct:ct + 1])
                    kb.op("scalar", f_si, r=[("convh", 1027, 2051), "cbcol"],
                          w=[("big_b", 24064, 25088)])
                    def f_sp(e, ct=ct):
                        return e.dma_start(out=d_xconv[ct][:, 1024:2048],
                                           in_=big_b[:, 24064:25088])
                    kb.op("sync", f_sp, kind="dma", r=[("big_b", 24064, 25088)],
                          w=[("d_xconv", ct * 2 + 1, ct * 2 + 2)])
            else:
                if h == 0:
                    def f_spz(e, ct=ct):
                        return e.dma_start(out=d_sz[ct - 16][:, 0:1024],
                                           in_=sm[:, 2048:3072])
                    kb.op("sync", f_spz, kind="dma", r=[("sm", 2048, 3072)],
                          w=[("d_sz", (ct - 16) * 2, (ct - 16) * 2 + 1)])
                else:
                    def f_spz(e, ct=ct):
                        return e.dma_start(out=d_sz[ct - 16][:, 1024:2048],
                                           in_=big_b[:, 24064:25088])
                    kb.op("sync", f_spz, kind="dma", r=[("big_b", 24064, 25088)],
                          w=[("d_sz", (ct - 16) * 2 + 1, (ct - 16) * 2 + 2)])
            if cti + nring - 1 < 32 and cti + nring - 1 >= nring:
                nct = ct_order[cti + nring - 1]
                sl = (cti + nring - 1) % nring
                def f_ldw(e, nct=nct, sl=sl, rbase=rbase):
                    return e.dma_start(
                        out=big_b[:, rbase + sl * 1024:rbase + sl * 1024 + 1024],
                        in_=dw_in[:, nct].rearrange("p dt c -> p (dt c)"))
                kb.op("gpsimd", f_ldw, kind="dma",
                      w=[("big_b", rbase + sl * 1024, rbase + sl * 1024 + 1024)])

        WX_ALL = ("big_b", 0, 16 * 1056)

        REP_STG = {0: 24064, 1: 23040}
        SEL0 = 18432

        repc = [0]

        def rep_unit(which, s):
            bb = 0 if which == 0 else 16384
            for lc2 in range(2):
                repc[0] += 1
                ps, pk = (psC, "psC") if repc[0] % 2 == 0 else (psD, "psD")
                def f_mm(e, which=which, s=s, lc2=lc2, ps=ps):
                    return e.matmul(
                        ps[:, 0:512],
                        big_b[0:16, SEL0 + s * 128:SEL0 + (s + 1) * 128],
                        big_b[0:16, REP_STG[which] + lc2 * 512:REP_STG[which] + lc2 * 512 + 512],
                        start=True, stop=True)
                kb.op("tensor", f_mm,
                      r=[("big_b", SEL0 + s * 128, SEL0 + (s + 1) * 128),
                         ("big_b", REP_STG[which] + lc2 * 512,
                          REP_STG[which] + lc2 * 512 + 512)],
                      w=[pk])
                def f_ev(e, bb=bb, s=s, lc2=lc2, ps=ps):
                    return e.copy(
                        big_a[:, bb + s * HF + lc2 * 512:bb + s * HF + lc2 * 512 + 512],
                        ps[:, 0:512])
                kb.op("scalar", f_ev, r=[pk],
                      w=[("big_a", bb + s * HF + lc2 * 512,
                          bb + s * HF + lc2 * 512 + 512)])

        def emit_p3a_half(h):
            if h == 1:
                for ct in range(16):
                    def f_rx(e, ct=ct):
                        return e.dma_start(out=big_a[:, ct * 1024:(ct + 1) * 1024],
                                           in_=d_xconv[ct][:, 1024:2048])
                    kb.op("sync" if ct % 2 == 0 else "gpsimd", f_rx, kind="dma",
                          r=[("d_xconv", ct * 2 + 1, ct * 2 + 2)],
                          w=[("big_a", ct * 1024, (ct + 1) * 1024)])
            kb.op("gpsimd", lambda e: e.dma_start(out=big_b[0:16, SEL0:SEL0 + 2048],
                                                  in_=dsel), kind="dma",
                  w=[("big_b", SEL0, SEL0 + 2048)])
            for which in (1, 0):
                col0 = 1024 if which == 0 else 1040
                ps, pk = (psD, "psD") if which == 0 else (psC, "psC")
                for lc2 in range(2):
                    for ct in range(16):
                        def f_mm(e, col0=col0, lc2=lc2, ct=ct, ps=ps):
                            return e.matmul(
                                ps[0:16, 0:512],
                                big_b[:, ct * 1056 + col0:ct * 1056 + col0 + 16],
                                big_a[:, ct * 1024 + lc2 * 512:ct * 1024 + lc2 * 512 + 512],
                                start=(ct == 0), stop=(ct == 15))
                        kb.op("tensor", f_mm,
                              r=[WX_ALL, ("big_a", ct * 1024 + lc2 * 512,
                                          ct * 1024 + lc2 * 512 + 512)], w=[pk])
                    def f_st(e, ps=ps, which=which, lc2=lc2):
                        return e.copy(
                            big_b[0:16, REP_STG[which] + lc2 * 512:REP_STG[which] + lc2 * 512 + 512],
                            ps[0:16, 0:512])
                    kb.op("scalar", f_st, r=[pk],
                          w=[("big_b", REP_STG[which] + lc2 * 512,
                              REP_STG[which] + lc2 * 512 + 512)])
            emit_p3b_loads(h, 0)
            base = 8192 if h == 0 else 0
            for dp in range(8):
                for lc2 in range(2):
                    ps, pk = next_ps()
                    for ct in range(16):
                        def f_mm(e, dp=dp, lc2=lc2, ct=ct, ps=ps):
                            return e.matmul(
                                ps[:, 0:512],
                                big_b[:, ct * 1056 + dp * 128:ct * 1056 + dp * 128 + 128],
                                big_a[:, ct * 1024 + lc2 * 512:ct * 1024 + lc2 * 512 + 512],
                                start=(ct == 0), stop=(ct == 15))
                        kb.op("tensor", f_mm,
                              r=[WX_ALL, ("big_a", ct * 1024 + lc2 * 512,
                                          ct * 1024 + lc2 * 512 + 512)], w=[pk])
                    def f_ev(e, dp=dp, lc2=lc2, ps=ps, base=base):
                        return e.copy(
                            arena[:, base + dp * 1024 + lc2 * 512:base + dp * 1024 + lc2 * 512 + 512],
                            ps[:, 0:512])
                    kb.op("scalar", f_ev, r=[pk],
                          w=[("arena", base + dp * 1024 + lc2 * 512,
                              base + dp * 1024 + lc2 * 512 + 512)])
            emit_p3b(h, 0, do_loads=False)
            for s in (0, 1):
                rep_unit(0, s)
                rep_unit(1, s)

        HLn, L0n, L1n, L2n = 16896, 17920, 18944, 19968
        WDTn, YGn = 20992, 23040
        HK = ("big_b", HLn, HLn + 1024)
        LK = [("big_b", c, c + 1024) for c in (L0n, L1n, L2n)]
        HF = 1024


        def emit_p3b_loads(half, it):
            par = it % 2
            WDL = ("big_b", WDTn + par * 1024, WDTn + (par + 1) * 1024)
            def f_lw(e, it=it, par=par):
                return e.dma_start(
                    out=big_b[:, WDTn + par * 1024:WDTn + (par + 1) * 1024],
                    in_=dw_dt[:, it].rearrange("p dt i -> p (dt i)"))
            kb.op("gpsimd", f_lw, kind="dma", w=[WDL])
            def f_lxc(e, it=it, half=half, par=par):
                return e.dma_start(out=sm[:, (2 + par) * 1024:(3 + par) * 1024],
                                   in_=d_xconv[it][:, half * HF:(half + 1) * HF])
            kb.op("sync", f_lxc, kind="dma", r=[("d_xconv", it * 2 + half, it * 2 + half + 1)],
                  w=[("sm", (2 + par) * 1024, (3 + par) * 1024)])
            def f_lsz(e, it=it, half=half, par=par):
                return e.dma_start(out=sm[:, (4 + par) * 1024:(5 + par) * 1024],
                                   in_=d_sz[it][:, half * HF:(half + 1) * HF])
            kb.op("sync", f_lsz, kind="dma", r=[("d_sz", it * 2 + half, it * 2 + half + 1)],
                  w=[("sm", (4 + par) * 1024, (5 + par) * 1024)])

        def emit_p3b(half, it, do_loads=True):
            par = it % 2
            WDL = ("big_b", WDTn + par * 1024, WDTn + (par + 1) * 1024)
            if do_loads:
                emit_p3b_loads(half, it)
            XCL = ("sm", (2 + par) * 1024, (3 + par) * 1024)
            SZL = ("sm", (4 + par) * 1024, (5 + par) * 1024)
            DTB = T3 + par * 1024
            DTK = ("bigf", DTB, DTB + 1024)
            for lc2 in range(2):
                ps, pk = next_ps()
                base_h = 8192 if half == 0 else 0
                for dp in range(8):
                    def f_mm(e, lc2=lc2, dp=dp, ps=ps, par=par, base_h=base_h):
                        return e.matmul(
                            ps[:, 0:512],
                            big_b[:, WDTn + par * 1024 + dp * 128:WDTn + par * 1024 + dp * 128 + 128],
                            arena[:, base_h + dp * 1024 + lc2 * 512:base_h + dp * 1024 + lc2 * 512 + 512],
                            start=(dp == 0), stop=(dp == 7))
                    kb.op("tensor", f_mm, r=[WDL, ("arena", base_h, base_h + 8192)], w=[pk])
                R_ = ("bigf", T2 + 1024 + lc2 * 512, T2 + 1024 + lc2 * 512 + 512)
                A_ = ("bigf", T0 + 1024 + lc2 * 512, T0 + 1024 + lc2 * 512 + 512)
                def f_r(e, it=it, lc2=lc2, ps=ps):
                    return e.activation(
                        bigf[:, T2 + 1024 + lc2 * 512:T2 + 1024 + lc2 * 512 + 512],
                        ps[:, 0:512], AF.Relu, bias=bdtcol[:, it:it + 1])
                kb.op("scalar", f_r, r=[pk, "bdtcol"], w=[R_])
                def f_a(e, it=it, lc2=lc2, ps=ps):
                    return e.activation(
                        bigf[:, T0 + 1024 + lc2 * 512:T0 + 1024 + lc2 * 512 + 512],
                        ps[:, 0:512], AF.Abs, bias=bdtcol[:, it:it + 1])
                kb.op("scalar", f_a, r=[pk, "bdtcol"], w=[A_])
                def f_e(e, lc2=lc2):
                    return e.activation(
                        bigf[:, T0 + 1024 + lc2 * 512:T0 + 1024 + lc2 * 512 + 512],
                        bigf[:, T0 + 1024 + lc2 * 512:T0 + 1024 + lc2 * 512 + 512],
                        AF.Exp, scale=-1.0)
                kb.op("scalar", f_e, r=[A_], w=[A_])
                def f_l(e, lc2=lc2):
                    return e.activation(
                        bigf[:, T0 + 1024 + lc2 * 512:T0 + 1024 + lc2 * 512 + 512],
                        bigf[:, T0 + 1024 + lc2 * 512:T0 + 1024 + lc2 * 512 + 512],
                        AF.Ln, bias=cpool[:, 1:2])
                kb.op("scalar", f_l, r=[A_, "cpool"], w=[A_])
                def f_d(e, lc2=lc2, DTB=DTB):
                    return e.tensor_add(bigf[:, DTB + lc2 * 512:DTB + lc2 * 512 + 512],
                                        bigf[:, T2 + 1024 + lc2 * 512:T2 + 1024 + lc2 * 512 + 512],
                                        bigf[:, T0 + 1024 + lc2 * 512:T0 + 1024 + lc2 * 512 + 512])
                kb.op("vector", f_d, r=[R_, A_],
                      w=[("bigf", DTB + lc2 * 512, DTB + lc2 * 512 + 512)])
            def f_dtx(e, DTB=DTB, par=par):
                return e.tensor_mul(sm[:, par * 1024:(par + 1) * 1024],
                                    bigf[:, DTB:DTB + 1024],
                                    sm[:, (2 + par) * 1024:(3 + par) * 1024])
            kb.op("vector", f_dtx, r=[DTK, XCL], w=[("sm", par * 1024, (par + 1) * 1024)])

        def emit_p4(half, it, lead_reps=False):
            par = it % 2
            DTB = T3 + par * 1024
            DTK = ("bigf", DTB, DTB + 1024)
            DTXL = ("sm", par * 1024, (par + 1) * 1024)
            XCL = ("sm", (2 + par) * 1024, (3 + par) * 1024)
            SZL = ("sm", (4 + par) * 1024, (5 + par) * 1024)
            YA = ("bigf", T2, T2 + 1024)
            for s in range(NS):
                sp = s % 2
                DA = ("bigf", T1 + sp * 1024, T1 + (sp + 1) * 1024)
                def f_da(e, it=it, s=s, sp=sp, DTB=DTB):
                    return e.activation(bigf[:, T1 + sp * 1024:T1 + (sp + 1) * 1024],
                                        bigf[:, DTB:DTB + 1024], AF.Exp,
                                        scale=pa[:, it * 16 + s:it * 16 + s + 1])
                kb.op("scalar", f_da, r=[DTK, ("pa", 0, 256)], w=[DA])
                DBU = ("sm", 6144, 7168)
                def f_dbu(e, s=s, par=par):
                    return e.tensor_mul(sm[:, 6144:7168],
                                        sm[:, par * 1024:(par + 1) * 1024],
                                        big_a[:, s * HF:(s + 1) * HF])
                kb.op("vector", f_dbu, r=[DTXL, ("big_a", s * HF, (s + 1) * HF)],
                      w=[DBU])
                if half == 0:
                    def f_scan(e, sp=sp):
                        return e.tensor_tensor_scan(
                            big_b[:, HLn:HLn + 1024],
                            bigf[:, T1 + sp * 1024:T1 + (sp + 1) * 1024],
                            sm[:, 6144:7168],
                            0.0, ALU.mult, ALU.add)
                    kb.op("vector", f_scan, r=[DA, DBU], w=[HK])
                    ic = it * 16 + s
                    def f_hc(e, ic=ic):
                        return e.tensor_copy(hcol[:, ic:ic + 1],
                                             big_b[:, HLn + 1023:HLn + 1024])
                    kb.op("vector", f_hc, r=[HK], w=[("hcol", ic, ic + 1)])
                else:
                    ic = it * 16 + s
                    def f_scan(e, sp=sp, ic=ic):
                        return e.tensor_tensor_scan(
                            big_b[:, HLn:HLn + 1024],
                            bigf[:, T1 + sp * 1024:T1 + (sp + 1) * 1024],
                            sm[:, 6144:7168],
                            hcol[:, ic:ic + 1], ALU.mult, ALU.add)
                    kb.op("vector", f_scan, r=[DA, DBU, ("hcol", ic, ic + 1)], w=[HK])
                leaf = L0n if s % 2 == 0 else L1n
                def f_ch(e, s=s, leaf=leaf):
                    return e.tensor_mul(big_b[:, leaf:leaf + 1024],
                                        big_b[:, HLn:HLn + 1024],
                                        big_a[:, 16384 + s * HF:16384 + (s + 1) * HF])
                kb.op("vector", f_ch,
                      r=[HK, ("big_a", 16384 + s * HF, 16384 + (s + 1) * HF)],
                      w=[("big_b", leaf, leaf + 1024)])
                if s % 4 == 1:
                    def f_pair(e):
                        return e.tensor_add(big_b[:, L2n:L2n + 1024],
                                            big_b[:, L0n:L0n + 1024],
                                            big_b[:, L1n:L1n + 1024])
                    kb.op("vector", f_pair, r=[LK[0], LK[1]], w=[LK[2]])
                if s % 4 == 3:
                    def f_pair1(e):
                        return e.tensor_add(big_b[:, L0n:L0n + 1024],
                                            big_b[:, L0n:L0n + 1024],
                                            big_b[:, L1n:L1n + 1024])
                    kb.op("vector", f_pair1, r=[LK[0], LK[1]], w=[LK[0]])
                    def f_quad(e):
                        return e.tensor_add(big_b[:, L0n:L0n + 1024],
                                            big_b[:, L2n:L2n + 1024],
                                            big_b[:, L0n:L0n + 1024])
                    kb.op("vector", f_quad, r=[LK[2], LK[0]], w=[LK[0]])
                    if s == 3:
                        kb.op("vector", lambda e: e.tensor_copy(bigf[:, T2:T2 + 1024],
                                                                big_b[:, L0n:L0n + 1024]),
                              r=[LK[0]], w=[YA])
                    else:
                        kb.op("vector", lambda e: e.tensor_add(bigf[:, T2:T2 + 1024],
                                                               bigf[:, T2:T2 + 1024],
                                                               big_b[:, L0n:L0n + 1024]),
                              r=[LK[0], YA], w=[YA])
                if lead_reps and s + 2 < NS:
                    rep_unit(0, s + 2)
                    rep_unit(1, s + 2)
            YF = ("bigf", T0, T0 + 1024)
            def f_yf(e, par=par):
                return e.tensor_add(bigf[:, T0:T0 + 1024], bigf[:, T2:T2 + 1024],
                                    sm[:, (2 + par) * 1024:(3 + par) * 1024])
            kb.op("vector", f_yf, r=[YA, XCL], w=[YF])
            def f_yg(e, par=par):
                return e.tensor_mul(big_b[:, YGn:YGn + 1024], bigf[:, T0:T0 + 1024],
                                    sm[:, (4 + par) * 1024:(5 + par) * 1024])
            kb.op("vector", f_yg, r=[YF, SZL], w=[("big_b", YGn, YGn + 1024)])
            def f_spyg(e, it=it, half=half):
                return e.dma_start(out=d_yg[it][:, half * HF:(half + 1) * HF],
                                   in_=big_b[:, YGn:YGn + 1024])
            kb.op("sync", f_spyg, kind="dma", r=[("big_b", YGn, YGn + 1024)],
                  w=[("d_yg", it * 2 + half, it * 2 + half + 1)])

        ct_order = []
        for k in range(16):
            ct_order += [k, 16 + k]
        for j in range(8):
            emit_p1_lt(0, j)
        kb.op("vector", lambda e: e.memset(convh[:, 0:3], 0.0), w=[("convh", 0, 3)])
        for cti in range(32):
            emit_p2_ct(0, cti, ct_order)
            for _ in range(2):
                if weff_steps:
                    weff_steps.pop(0)()
        while weff_steps:
            weff_steps.pop(0)()
        for ct in range(16):
            def f_lwx(e, ct=ct):
                return e.dma_start(out=big_b[:, ct * 1056:(ct + 1) * 1056],
                                   in_=dw_x[ct * 128:(ct + 1) * 128, :])
            kb.op("gpsimd", f_lwx, kind="dma",
                  w=[("big_b", ct * 1056, (ct + 1) * 1056)])
        emit_p3a_half(0)
        for it in range(NT):
            emit_p4(0, it, lead_reps=(it == 0))
            if it + 1 < NT:
                emit_p3b(0, it + 1)
            if it < 8:
                emit_p1_lt(1, it)
            else:
                for cti in range((it - 8) * 4, (it - 8) * 4 + 4):
                    emit_p2_ct(1, cti, ct_order)
        def emit_mix_early_loads(lt):
            ab = 8192 + (lt % 4) * 2048
            for it in range(NT):
                def f_ly(e, lt=lt, it=it, ab=ab):
                    return e.dma_start(
                        out=arena[:, ab + it * 128:ab + (it + 1) * 128],
                        in_=d_yg[it][:, lt * 128:(lt + 1) * 128])
                kb.op("gpsimd", f_ly, kind="dma", r=[("d_yg", it * 2, it * 2 + 1)],
                      w=[("arena", ab + it * 128, ab + (it + 1) * 128)])

        def emit_mix_early(lt):
            ab = 8192 + (lt % 4) * 2048
            for n2 in range(2):
                ps, pk = (psC, "psC") if n2 == 0 else (psD, "psD")
                for it in range(NT):
                    def f_mm(e, n2=n2, it=it, ps=ps, ab=ab):
                        return e.matmul(
                            ps[:, 0:512],
                            arena[:, ab + it * 128:ab + (it + 1) * 128],
                            big_b[:, it * 1024 + n2 * 512:it * 1024 + n2 * 512 + 512],
                            start=(it == 0), stop=(it == 15))
                    kb.op("tensor", f_mm,
                          r=[("arena", ab, ab + 2048), ("big_b", 0, 16384)], w=[pk])
                def f_ev(e, n2=n2, ps=ps):
                    return e.copy(big_b[:, 16384 + n2 * 512:16384 + n2 * 512 + 512],
                                  ps[:, 0:512])
                kb.op("scalar", f_ev, r=[pk],
                      w=[("big_b", 16384 + n2 * 512, 16384 + n2 * 512 + 512)])
            def f_sp(e, lt=lt):
                return e.dma_start(out=d_mix[lt], in_=big_b[:, 16384:17408])
            kb.op("sync", f_sp, kind="dma", r=[("big_b", 16384, 17408)],
                  w=[("d_mix", lt, lt + 1)])

        emit_p3a_half(1)
        for it in range(NT):
            emit_p4(1, it, lead_reps=(it == 0))
            if it + 1 < NT:
                emit_p3b(1, it + 1)
            if it < 8:
                for k in (it * 2, it * 2 + 1):
                    def f_lwo(e, k=k):
                        return e.dma_start(out=big_b[:, k * 1024:(k + 1) * 1024],
                                           in_=dw_out[k * 128:(k + 1) * 128, :])
                    kb.op("gpsimd", f_lwo, kind="dma",
                          w=[("big_b", k * 1024, (k + 1) * 1024)])
            if it >= 7 and it < 15:
                emit_mix_early_loads(it - 7)
            if it >= 8:
                emit_mix_early(it - 8)

        for lt in (0, 1):
            mb0 = T2 + (lt % 2) * 1024
            def f_lm0(e, lt=lt, mb0=mb0):
                return e.dma_start(out=bigf[:, mb0:mb0 + 1024], in_=d_mix[lt])
            kb.op("gpsimd", f_lm0, kind="dma", r=[("d_mix", lt, lt + 1)],
                  w=[("bigf", mb0, mb0 + 1024)])
        for it in range(NT):
            def f_lyg(e, it=it):
                return e.dma_start(out=big_a[:, it * L + 1024:(it + 1) * L],
                                   in_=d_yg[it][:, 1024:2048])
            kb.op("sync", f_lyg, kind="dma", r=[("d_yg", it * 2 + 1, it * 2 + 2)],
                  w=[("big_a", it * L + 1024, (it + 1) * L)])
        kb.op("gpsimd", lambda e: e.dma_start(out=sm[:, 0:7168], in_=d_weff[:, 0:7168]),
              kind="dma", r=[("d_weff", 0, 7168)], w=["sm"])
        kb.op("gpsimd", lambda e: e.dma_start(out=big_b[:, 24064:25088],
                                              in_=d_weff[:, 7168:8192]),
              kind="dma", r=[("d_weff", 7168, 8192)], w=[("big_b", 24064, 25088)])
        YG_ALL = ("big_a", 0, 16 * L)
        WO_ALL = ("big_b", 0, 16 * 1024)
        WEFF = ("sm", 0, 7168)

        kb.op("sync", lambda e: e.dma_start(out=bigf[0:8, T1:T1 + 1024], in_=dffn_b),
              kind="dma", w=[("bigf", T1, T1 + 1024)])
        kb.op("vector", lambda e: e.tensor_scalar(bigf[0:8, T1:T1 + 1024],
                                                  bigf[0:8, T1:T1 + 1024],
                                                  maskt[:], None, ALU.mult),
              r=[("bigf", T1, T1 + 1024), "maskt"], w=[("bigf", T1, T1 + 1024)])
        kb.op("vector", lambda e: e.tensor_copy(big_b[0:8, 16384:16384 + 1024],
                                                bigf[0:8, T1:T1 + 1024]),
              r=[("bigf", T1, T1 + 1024)], w=[("big_b", 16384, 17408)])
        for n2 in range(2):
            def f_mm(e, n2=n2):
                return e.matmul(psE[0:1, 0:512], ones8[:],
                                big_b[0:8, 16384 + n2 * 512:16384 + n2 * 512 + 512],
                                start=True, stop=True)
            kb.op("tensor", f_mm, r=[("big_b", 16384, 17408), "ones8"], w=["psE"])
            def f_ev(e, n2=n2):
                return e.tensor_copy(beffrow[:, n2 * 512:n2 * 512 + 512], psE[0:1, 0:512])
            kb.op("vector", f_ev, r=["psE"], w=[("big_b", 20480 + n2 * 512, 20480 + n2 * 512 + 512)])
        for n2 in range(2):
            def f_mm(e, n2=n2):
                return e.matmul(psE[:, 0:512], ones1,
                                beffrow[:, n2 * 512:n2 * 512 + 512], start=True,
                                stop=True)
            kb.op("tensor", f_mm, r=[("big_b", 20480, 21504), ("big_b", 27264, 27392)], w=["psE"])
            def f_ev(e, n2=n2):
                return e.tensor_copy(
                    convh[:, n2 * 512:n2 * 512 + 512], psE[:, 0:512])
            kb.op("vector", f_ev, r=["psE"],
                  w=[("convh", n2 * 512, n2 * 512 + 512)])
        BEFF = ("convh", 0, 1024)

        statn = sb("statn", [128, 128], F32)

        def ln_ops(src_lo, dst_ap_fn, dst_w, sb_, sq_lo):
            SRC = ("bigf", src_lo, src_lo + 1024)
            SQ5 = ("bigf", sq_lo, sq_lo + 1024)
            c = lambda a, b: ("statn", sb_ + a, sb_ + b)
            s_ = statn
            kb.op("scalar", lambda e: e.activation(bigf[:, sq_lo:sq_lo + 1024],
                                                   bigf[:, src_lo:src_lo + 1024],
                                                   AF.Copy,
                                                   accum_out=s_[:, sb_:sb_ + 1]),
                  r=[SRC], w=[SQ5, c(0, 1)])
            kb.op("scalar", lambda e: e.activation(bigf[:, sq_lo:sq_lo + 1024],
                                                   bigf[:, src_lo:src_lo + 1024],
                                                   AF.Square,
                                                   accum_out=s_[:, sb_ + 1:sb_ + 2]),
                  r=[SRC], w=[SQ5, c(1, 2)])
            kb.op("vector", lambda e: e.tensor_scalar(s_[:, sb_ + 2:sb_ + 3],
                                                      s_[:, sb_:sb_ + 1], 1.0 / D,
                                                      None, ALU.mult),
                  r=[c(0, 1)], w=[c(2, 3)])
            kb.op("vector", lambda e: e.tensor_scalar(s_[:, sb_ + 3:sb_ + 4],
                                                      s_[:, sb_ + 1:sb_ + 2], 1.0 / D,
                                                      None, ALU.mult),
                  r=[c(1, 2)], w=[c(3, 4)])
            kb.op("vector", lambda e: e.tensor_mul(s_[:, sb_ + 4:sb_ + 5],
                                                   s_[:, sb_ + 2:sb_ + 3],
                                                   s_[:, sb_ + 2:sb_ + 3]),
                  r=[c(2, 3)], w=[c(4, 5)])
            kb.op("vector", lambda e: e.tensor_sub(s_[:, sb_ + 5:sb_ + 6],
                                                   s_[:, sb_ + 3:sb_ + 4],
                                                   s_[:, sb_ + 4:sb_ + 5]),
                  r=[c(3, 5)], w=[c(5, 6)])
            kb.op("scalar", lambda e: e.activation(s_[:, sb_ + 6:sb_ + 7],
                                                   s_[:, sb_ + 5:sb_ + 6], AF.Ln,
                                                   bias=cpool[:, 0:1]),
                  r=[c(5, 6), "cpool"], w=[c(6, 7)])
            kb.op("scalar", lambda e: e.activation(s_[:, sb_ + 7:sb_ + 8],
                                                   s_[:, sb_ + 6:sb_ + 7], AF.Exp,
                                                   scale=-0.5),
                  r=[c(6, 7)], w=[c(7, 8)])
            kb.op("vector", lambda e: e.tensor_scalar(s_[:, sb_ + 8:sb_ + 9],
                                                      s_[:, sb_ + 2:sb_ + 3],
                                                      s_[:, sb_ + 7:sb_ + 8], -1.0,
                                                      ALU.mult, ALU.mult),
                  r=[c(2, 3), c(7, 8)], w=[c(8, 9)])
            def f_dst(e):
                return e.activation(dst_ap_fn(), bigf[:, src_lo:src_lo + 1024],
                                    AF.Identity, bias=s_[:, sb_ + 8:sb_ + 9],
                                    scale=s_[:, sb_ + 7:sb_ + 8])
            kb.op("scalar", f_dst, r=[SRC, c(7, 9)], w=[dst_w])


        def emit_mix(lt):
            mb = T2 + (lt % 2) * 1024
            if lt < 8:
                def f_lm(e, lt=lt, mb=mb):
                    return e.dma_start(out=bigf[:, mb:mb + 1024], in_=d_mix[lt])
                kb.op("gpsimd", f_lm, kind="dma", r=[("d_mix", lt, lt + 1)],
                      w=[("bigf", mb, mb + 1024)])
                return
            for n2 in range(2):
                ps, pk = (psA, "psA") if n2 == 0 else (psB, "psB")
                for it in range(NT):
                    def f_mm(e, lt=lt, n2=n2, it=it, ps=ps):
                        return e.matmul(
                            ps[:, 0:512],
                            big_a[:, it * L + lt * 128:it * L + lt * 128 + 128],
                            big_b[:, it * 1024 + n2 * 512:it * 1024 + n2 * 512 + 512],
                            start=(it == 0), stop=(it == 15))
                    kb.op("tensor", f_mm, r=[YG_ALL, WO_ALL], w=[pk])
                def f_ev(e, n2=n2, ps=ps, mb=mb):
                    return e.copy(bigf[:, mb + n2 * 512:mb + n2 * 512 + 512],
                                  ps[:, 0:512])
                kb.op("scalar", f_ev, r=[pk],
                      w=[("bigf", mb + n2 * 512, mb + n2 * 512 + 512)])

        def emit_tail_a(lt):
            p = lt % 2
            mb = T2 + p * 1024
            xs = p * 1024
            x2s = 2048 + p * 1024
            sqs = 6180 + p * 1024
            l2s = 16384 if p == 0 else 25088
            l2t = 17408 if p == 0 else 26112
            def f_ldx(e, lt=lt, xs=xs):
                return e.dma_start(out=bigf[:, xs:xs + 1024],
                                   in_=dx[lt * 128:(lt + 1) * 128, :])
            kb.op("sync", f_ldx, kind="dma", w=[("bigf", xs, xs + 1024)])
            XT5 = ("bigf", xs, xs + 1024)

            sb0 = (lt % 4) * 32
            ln_ops(mb, lambda sqs=sqs: bigf[:, sqs:sqs + 1024],
                   ("bigf", sqs, sqs + 1024), sb0, sq_lo=sqs)
            kb.op("vector", lambda e, xs=xs, x2s=x2s, sqs=sqs: e.tensor_add(
                                                   bigf[:, x2s:x2s + 1024],
                                                   bigf[:, xs:xs + 1024],
                                                   bigf[:, sqs:sqs + 1024]),
                  r=[("bigf", sqs, sqs + 1024), XT5], w=[("bigf", x2s, x2s + 1024)])
            X2 = ("bigf", x2s, x2s + 1024)
            ln_ops(x2s, lambda l2s=l2s: big_b[:, l2s:l2s + 1024],
                   ("big_b", l2s, l2s + 1024), sb0 + 10, sq_lo=sqs)
            for dt_ in range(8):
                bank, bk = (psT, "psT") if dt_ % 2 == 0 else (psU, "psU")
                def f_tr(e, dt_=dt_, bank=bank, l2s=l2s):
                    return e.transpose(bank[:, 0:128],
                                       big_b[:, l2s + dt_ * 128:l2s + dt_ * 128 + 128],
                                       identb)
                kb.op("tensor", f_tr, r=[("big_b", l2s, l2s + 1024), ("big_b", 27136, 27264)],
                      w=[bk])
                def f_cp(e, dt_=dt_, bank=bank, l2t=l2t):
                    return e.copy(
                        big_b[:, l2t + dt_ * 128:l2t + dt_ * 128 + 128],
                        bank[:, 0:128])
                kb.op("scalar", f_cp, r=[bk],
                      w=[("big_b", l2t + dt_ * 128, l2t + dt_ * 128 + 128)])
            LN2T = ("big_b", l2t, l2t + 1024)
            for n2 in range(2):
                if p == 0:
                    ps, pk = (psC, "psC") if n2 == 0 else (psD, "psD")
                else:
                    ps, pk = (psE, "psE") if n2 == 0 else (psV, "psV")
                for dt_ in range(8):
                    if dt_ < 7:
                        rhs_ap = sm[:, dt_ * 1024 + n2 * 512:dt_ * 1024 + n2 * 512 + 512]
                        rhs_k = WEFF
                    else:
                        rhs_ap = big_b[:, 24064 + n2 * 512:24064 + n2 * 512 + 512]
                        rhs_k = ("big_b", 24064, 25088)
                    def f_mm(e, n2=n2, dt_=dt_, ps=ps, rhs_ap=rhs_ap, l2t=l2t):
                        return e.matmul(
                            ps[:, 0:512],
                            big_b[:, l2t + dt_ * 128:l2t + dt_ * 128 + 128],
                            rhs_ap,
                            start=(dt_ == 0), stop=(dt_ == 7))
                    kb.op("tensor", f_mm, r=[LN2T, rhs_k], w=[pk])

        def emit_tail_b(lt):
            p = lt % 2
            xs = p * 1024
            x2s = 2048 + p * 1024
            sqs = 6180 + p * 1024
            sb0 = (lt % 4) * 32
            X2 = ("bigf", x2s, x2s + 1024)
            for n2 in range(2):
                if p == 0:
                    ps, pk = (psC, "psC") if n2 == 0 else (psD, "psD")
                else:
                    ps, pk = (psE, "psE") if n2 == 0 else (psV, "psV")
                def f_ev(e, n2=n2, ps=ps, xs=xs):
                    return e.tensor_add(
                        bigf[:, xs + n2 * 512:xs + n2 * 512 + 512], ps[:, 0:512],
                        convh[:, n2 * 512:n2 * 512 + 512])
                kb.op("vector", f_ev, r=[pk, BEFF],
                      w=[("bigf", xs + n2 * 512, xs + n2 * 512 + 512)])
            ln_ops(xs, lambda sqs=sqs: bigf[:, sqs:sqs + 1024],
                   ("bigf", sqs, sqs + 1024), sb0 + 20, sq_lo=sqs)
            kb.op("vector", lambda e, sqs=sqs, x2s=x2s: e.tensor_add(
                                                   bigf[:, sqs:sqs + 1024],
                                                   bigf[:, x2s:x2s + 1024],
                                                   bigf[:, sqs:sqs + 1024]),
                  r=[("bigf", sqs, sqs + 1024), X2], w=[("bigf", sqs, sqs + 1024)])
            def f_out(e, lt=lt, sqs=sqs):
                return e.dma_start(out=dy[lt * 128:(lt + 1) * 128, :],
                                   in_=bigf[:, sqs:sqs + 1024])
            kb.op("sync", f_out, kind="dma", r=[("bigf", sqs, sqs + 1024)],
                  w=[("dy", lt, lt + 1)], sem="dout")

        emit_tail_a(0)
        for lt in range(NT):
            emit_tail_b(lt)
            if lt + 2 < NT:
                emit_mix(lt + 2)
            if lt + 1 < NT:
                emit_tail_a(lt + 1)

        counters = kb.resolve()
        with nc.Block() as block:
            kb.emit(block, sems, final_waits=[("dout", counters.get("dout", 0))])

    return nc


_NC_CACHE = [None]


def kernel(**inputs):
    x = np.ascontiguousarray(inputs["L0_attn_inputs"], dtype=np.float32)
    W_in = np.ascontiguousarray(inputs["L0_attn_W_in"], dtype=np.float32)
    conv_w = np.ascontiguousarray(inputs["L0_attn_conv_w"], dtype=np.float32)
    conv_b = np.ascontiguousarray(inputs["L0_attn_conv_b"], dtype=np.float32)
    W_x = np.ascontiguousarray(inputs["L0_attn_W_x"], dtype=np.float32)
    W_dt = np.ascontiguousarray(inputs["L0_attn_W_dt"], dtype=np.float32)
    b_dt = np.ascontiguousarray(inputs["L0_attn_b_dt"], dtype=np.float32)
    A_log = np.ascontiguousarray(inputs["L0_attn_A_log"], dtype=np.float32)
    W_out = np.ascontiguousarray(inputs["L0_attn_W_out"], dtype=np.float32)
    ffn_W = np.ascontiguousarray(inputs["L0_ffn_W"], dtype=np.float32)
    ffn_b = np.ascontiguousarray(inputs["L0_ffn_b"], dtype=np.float32)
    logits = np.ascontiguousarray(inputs["L0_ffn_logits"], dtype=np.float32)
    u = np.ascontiguousarray(inputs["L0_ffn_u"], dtype=np.float32)

    W_in2 = np.ascontiguousarray(
        W_in.reshape(8, 128, 32, 128).transpose(1, 2, 0, 3))
    W_dt2 = np.ascontiguousarray(
        W_dt.reshape(8, 128, 16, 128).transpose(1, 2, 0, 3))
    conv_wt = np.ascontiguousarray(
        conv_w.reshape(4, 16, 128).transpose(2, 1, 0).reshape(128, 64))
    conv_b2 = np.ascontiguousarray(conv_b.reshape(16, 128).T)
    b_dt2 = np.ascontiguousarray(b_dt.reshape(16, 128).T)
    a_log2 = np.ascontiguousarray(
        A_log.reshape(16, 128, 16).transpose(1, 0, 2).reshape(128, 256))
    logits1 = np.ascontiguousarray(logits.reshape(1, 8))
    u1 = np.ascontiguousarray(u.reshape(1, 8))
    ident = np.eye(128, dtype=np.float32)
    sel16 = np.zeros((16, 2048), dtype=np.float32)
    for s in range(16):
        sel16[s, s * 128:(s + 1) * 128] = 1.0

    if _NC_CACHE[0] is None:
        _NC_CACHE[0] = build_nc()
    nc = _NC_CACHE[0]

    shared = {"w_in": W_in2, "w_x": W_x, "w_dt": W_dt2, "w_out": W_out,
              "ffn_w": ffn_W, "ffn_b": ffn_b, "conv_wt": conv_wt,
              "conv_b2": conv_b2, "b_dt2": b_dt2, "a_log2": a_log2,
              "logits1": logits1, "u1": u1, "ident": ident, "sel16": sel16}
    in_maps = [dict(shared, x=np.ascontiguousarray(x[b])) for b in range(NCORES)]
    res = run_bass_kernel_spmd(nc, in_maps, core_ids=list(range(NCORES)))
    return np.stack([np.asarray(res.results[b]["y"], dtype=np.float32)
                     for b in range(NCORES)], axis=0)


# version 19
# speedup vs baseline: 1.0084x; 1.0084x over previous
import numpy as np
from contextlib import ExitStack
import concourse.bass as bass
import concourse.mybir as mybir
from concourse.bass_utils import run_bass_kernel_spmd

F32 = mybir.dt.float32
BF16 = mybir.dt.bfloat16
AF = mybir.ActivationFunctionType
ALU = mybir.AluOpType
AX = mybir.AxisListType

L = 2048
D = 1024
DI = 2048
NS = 16
NT = 16
NCORES = 8
EPS = 1e-5
DMA_INC = 16


class KB:

    def __init__(self):
        self.plan = []
        self.bufs = {}

    def _norm(self, spec):
        if isinstance(spec, str):
            return (spec, 0, 1 << 30)
        return spec

    PSUM_KEYS = frozenset(["psA", "psB", "psC", "psD", "psE", "psV", "psT", "psU"])

    def op(self, eng, fn, r=(), w=(), kind="compute", sem=None):
        deps = set()
        r = [self._norm(s) for s in r]
        w = [self._norm(s) for s in w]
        w = w + [s for s in r if s[0] in self.PSUM_KEYS]
        r = [s for s in r if s[0] not in self.PSUM_KEYS]
        for (name, lo, hi) in r:
            for ent in self.bufs.get(name, ()):
                if ent[0] < hi and lo < ent[1] and ent[2] is not None:
                    deps.add(ent[2])
        for (name, lo, hi) in w:
            for ent in self.bufs.get(name, ()):
                if ent[0] < hi and lo < ent[1]:
                    if ent[2] is not None:
                        deps.add(ent[2])
                    deps.update(ent[3])
        i = len(self.plan)
        self.plan.append(dict(eng=eng, fn=fn, deps=sorted(deps), kind=kind,
                              sem=sem, marked=False, token=None, waits=None))
        for (name, lo, hi) in r:
            lst = self.bufs.setdefault(name, [])
            for ent in lst:
                if ent[0] < hi and lo < ent[1]:
                    ent[3].add(i)
        for (name, lo, hi) in w:
            lst = self.bufs.setdefault(name, [])
            lst[:] = [ent for ent in lst if not (lo <= ent[0] and ent[1] <= hi)]
            lst.append([lo, hi, i, set()])
        return i

    def _queue(self, o):
        if o["sem"] is not None:
            return o["sem"]
        return "dma_s" if o["eng"] == "sync" else "dma_g"

    def resolve(self):
        POOLS = {"dma_s": ["dsa", "dsb", "dsc", "dsd"],
                 "dma_g": ["dga", "dgb", "dgc"],
                 "dout": ["dout"]}
        semcum = {}
        qcur = {q: None for q in POOLS}
        qopen = {q: [] for q in POOLS}
        qsi = {q: -1 for q in POOLS}
        group_close = {}
        needed = [[] for _ in self.plan]
        prewaits = [[] for _ in self.plan]

        def close_group(q):
            sem = qcur[q]
            if sem is None:
                return
            v = semcum.get(sem, 0)
            for j in qopen[q]:
                group_close[j] = (sem, v)
            qopen[q] = []
            qcur[q] = None

        for i, o in enumerate(self.plan):
            for d in o["deps"]:
                p = self.plan[d]
                if p["kind"] == "compute":
                    p["marked"] = True
                    needed[i].append(d)
                else:
                    q = self._queue(p)
                    if d in qopen[q]:
                        close_group(q)
                    needed[i].append(d)
            if o["kind"] == "dma":
                q = self._queue(o)
                if qcur[q] is None:
                    qsi[q] += 1
                    sem = POOLS[q][qsi[q] % len(POOLS[q])]
                    if semcum.get(sem, 0) > 0:
                        prewaits[i].append((sem, semcum[sem]))
                    qcur[q] = sem
                sem = qcur[q]
                semcum[sem] = semcum.get(sem, 0) + DMA_INC
                o["marked"] = True
                o["token"] = (sem, semcum[sem], DMA_INC)
                qopen[q].append(i)
        for q in POOLS:
            close_group(q)

        counters = dict(semcum)
        for o in self.plan:
            if o["marked"] and o["kind"] == "compute":
                s = o["eng"]
                counters[s] = counters.get(s, 0) + 1
                o["token"] = (s, counters[s], 1)

        marks = {}
        for i, o in enumerate(self.plan):
            wl = {}
            for (s, v) in prewaits[i]:
                if wl.get(s, 0) < v:
                    wl[s] = v
            for d in needed[i]:
                p = self.plan[d]
                if p["kind"] == "dma":
                    s, v = group_close[d]
                else:
                    s, v, _ = p["token"]
                if wl.get(s, 0) < v:
                    wl[s] = v
            final = []
            for s, v in wl.items():
                key = (o["eng"], s)
                if marks.get(key, 0) < v:
                    marks[key] = v
                    final.append((s, v))
            o["waits"] = final
        return counters

    def emit(self, block, sems, final_waits):
        for name in ["sync", "gpsimd", "tensor", "vector", "scalar"]:
            ops = [o for o in self.plan if o["eng"] == name]

            def make(eng_ops, engname):
                def body(eng):
                    for o in eng_ops:
                        for (s, v) in o["waits"]:
                            eng.wait_ge(sems[s], v)
                        inst = o["fn"](eng)
                        if o["marked"]:
                            s, v, inc = o["token"]
                            inst.then_inc(sems[s], inc)
                    if engname == "sync":
                        for (s, v) in final_waits:
                            eng.wait_ge(sems[s], v)
                return body

            getattr(block, name)(make(ops, name))


_PHASES = [7]


def _fin(nc, kb, sems, es, dy, bigf, T0):
    for lt in range(NT):
        def f_out(e, lt=lt):
            return e.dma_start(out=dy[lt * 128:(lt + 1) * 128, :],
                               in_=bigf[:, T0:T0 + 1024])
        kb.op("sync", f_out, kind="dma", r=[("bigf", T0, T0 + 1024)],
              w=[("dy", lt, lt + 1)], sem="dout")
    counters = kb.resolve()
    with nc.Block() as block:
        kb.emit(block, sems, final_waits=[("dout", counters.get("dout", 0))])
    es.close()
    return nc



def build_nc():
    nc = bass.Bass("TRN2", target_bir_lowering=False, debug=False)

    dx = nc.dram_tensor("x", [L, D], F32, kind="ExternalInput").ap()
    dw_in = nc.dram_tensor("w_in", [128, 32, 8, 128], F32, kind="ExternalInput").ap()
    dw_x = nc.dram_tensor("w_x", [DI, D + 2 * NS], F32, kind="ExternalInput").ap()
    dw_dt = nc.dram_tensor("w_dt", [128, 16, 8, 128], F32, kind="ExternalInput").ap()
    dw_out = nc.dram_tensor("w_out", [DI, D], F32, kind="ExternalInput").ap()
    dffn_w = nc.dram_tensor("ffn_w", [8, D, D], F32, kind="ExternalInput").ap()
    dffn_b = nc.dram_tensor("ffn_b", [8, D], F32, kind="ExternalInput").ap()
    dconv_wt = nc.dram_tensor("conv_wt", [128, 64], F32, kind="ExternalInput").ap()
    dconv_b2 = nc.dram_tensor("conv_b2", [128, 16], F32, kind="ExternalInput").ap()
    db_dt2 = nc.dram_tensor("b_dt2", [128, 16], F32, kind="ExternalInput").ap()
    da_log2 = nc.dram_tensor("a_log2", [128, 256], F32, kind="ExternalInput").ap()
    dlogits = nc.dram_tensor("logits1", [1, 8], F32, kind="ExternalInput").ap()
    du1 = nc.dram_tensor("u1", [1, 8], F32, kind="ExternalInput").ap()
    dident = nc.dram_tensor("ident", [128, 128], F32, kind="ExternalInput").ap()
    dsel = nc.dram_tensor("sel16", [NS, 2048], F32, kind="ExternalInput").ap()
    dy = nc.dram_tensor("y", [L, D], F32, kind="ExternalOutput").ap()

    d_xconv = nc.dram_tensor("d_xconv", [NT, 128, L], BF16, kind="Internal").ap()
    d_sz = nc.dram_tensor("d_sz", [NT, 128, L], BF16, kind="Internal").ap()
    d_yg = nc.dram_tensor("d_yg", [NT, 128, L], BF16, kind="Internal").ap()
    d_weff = nc.dram_tensor("d_weff", [128, 8 * D], BF16, kind="Internal").ap()
    d_mix = nc.dram_tensor("d_mix", [16, 128, D], BF16, kind="Internal").ap()
    d_mask = nc.dram_tensor("d_mask", [1, 8], F32, kind="Internal").ap()

    kb = KB()

    with ExitStack() as es:
        sb = lambda n, shp, dt: es.enter_context(nc.sbuf_tensor(n, shp, dt))
        pt = lambda n, shp, dt: es.enter_context(nc.psum_tensor(n, shp, dt))

        arena = sb("arena", [128, 8 * L], BF16)
        big_a = sb("big_a", [128, 16 * L], BF16)
        big_b = sb("big_b", [128, 27392], BF16)
        T0, T1, T2, T3 = 0, 2052, 4132, 6180
        bigf = sb("bigf", [128, 8228], F32)
        bflat = big_b[0:1, 18432:20480]
        identb = big_b[:, 27136:27264]
        ones1 = big_b[0:1, 27264:27392]
        beffrow = big_b[0:1, 20480:21504]
        sm = sb("sm", [128, 7 * 1024], BF16)
        pa = sb("pa", [128, 256], F32)
        wcol = sb("wcol", [128, 64], F32)
        cbcol = sb("cbcol", [128, 16], F32)
        bdtcol = sb("bdtcol", [128, 16], F32)
        ones8 = sb("ones8", [8, 1], BF16)
        lgu = sb("lgu", [1, 32], F32)
        maskr = sb("maskr", [1, 8], F32)
        maskrb = sb("maskrb", [1, 8], BF16)
        maskt = sb("maskt", [8, 1], F32)
        maskrep = sb("maskrep", [128, 8], F32)
        cpool = sb("cpool", [128, 8], F32)
        hcol = sb("hcol", [128, 256], F32)
        convh = sb("convh", [128, 2051], F32)
        tails = sb("tails", [128, 48], F32)

        psA = pt("psA", [128, 512], F32)
        psB = pt("psB", [128, 512], F32)
        psC = pt("psC", [128, 512], F32)
        psD = pt("psD", [128, 512], F32)
        psE = pt("psE", [128, 512], F32)
        psV = pt("psV", [128, 512], F32)
        psT = pt("psT", [128, 1024], BF16)
        psU = pt("psU", [128, 1024], BF16)

        sems = {n: es.enter_context(nc.semaphore(n)) for n in
                ["dsa", "dsb", "dsc", "dsd", "dga", "dgb", "dgc",
                 "tensor", "vector", "scalar", "gpsimd", "dout"]}

        kb.op("vector", lambda e: e.memset(cpool[:, 0:1], EPS), w=["cpool"])
        kb.op("vector", lambda e: e.memset(cpool[:, 1:2], 1.0), w=["cpool"])
        kb.op("sync", lambda e: e.dma_start(out=pa[:, 0:256], in_=da_log2), kind="dma",
              w=[("pa", 0, 256)])
        kb.op("scalar", lambda e: e.activation(pa[:, 0:256], pa[:, 0:256], AF.Exp),
              r=[("pa", 0, 256)], w=[("pa", 0, 256)])
        kb.op("vector", lambda e: e.tensor_scalar(pa[:, 0:256], pa[:, 0:256], -1.0,
                                                  None, ALU.mult),
              r=[("pa", 0, 256)], w=[("pa", 0, 256)])
        kb.op("sync", lambda e: e.dma_start(out=wcol[:], in_=dconv_wt),
              kind="dma", w=["wcol"])
        kb.op("sync", lambda e: e.dma_start(out=cbcol[:], in_=dconv_b2), kind="dma", w=["cbcol"])
        kb.op("sync", lambda e: e.dma_start(out=bdtcol[:], in_=db_dt2), kind="dma", w=["bdtcol"])
        kb.op("gpsimd", lambda e: e.dma_start(out=identb, in_=dident), kind="dma", w=[("big_b", 27136, 27264)])
        kb.op("vector", lambda e: e.memset(ones1, 1.0), w=[("big_b", 27264, 27392)])
        kb.op("vector", lambda e: e.memset(ones8[:], 1.0), w=["ones8"])
        kb.op("vector", lambda e: e.memset(convh[:, 0:3], 0.0), w=[("convh", 0, 3)])

        kb.op("sync", lambda e: e.dma_start(out=lgu[:, 0:8], in_=dlogits), kind="dma",
              w=[("lgu", 0, 8)])
        kb.op("sync", lambda e: e.dma_start(out=lgu[:, 8:16], in_=du1), kind="dma",
              w=[("lgu", 8, 16)])
        kb.op("scalar", lambda e: e.activation(lgu[:, 16:24], lgu[:, 8:16], AF.Ln),
              r=[("lgu", 8, 16)], w=[("lgu", 16, 24)])
        kb.op("vector", lambda e: e.tensor_scalar(lgu[:, 16:24], lgu[:, 16:24], -1.0, None,
                                                  ALU.mult),
              r=[("lgu", 16, 24)], w=[("lgu", 16, 24)])
        kb.op("scalar", lambda e: e.activation(lgu[:, 16:24], lgu[:, 16:24], AF.Ln),
              r=[("lgu", 16, 24)], w=[("lgu", 16, 24)])
        kb.op("vector", lambda e: e.tensor_sub(lgu[:, 24:32], lgu[:, 0:8], lgu[:, 16:24]),
              r=[("lgu", 0, 24)], w=[("lgu", 24, 32)])
        kb.op("vector", lambda e: e.reduce_max(lgu[:, 16:17], lgu[:, 24:32], axis=AX.X),
              r=[("lgu", 24, 32)], w=[("lgu", 16, 17)])
        kb.op("vector", lambda e: e.tensor_scalar(maskr[:], lgu[:, 24:32], lgu[:, 16:17],
                                                  None, ALU.is_ge),
              r=[("lgu", 16, 32)], w=["maskr"])
        kb.op("vector", lambda e: e.tensor_copy(maskrb[:], maskr[:]), r=["maskr"], w=["maskrb"])
        kb.op("sync", lambda e: e.dma_start(out=d_mask, in_=maskr[:]), kind="dma",
              r=["maskr"], w=["d_mask"])
        kb.op("sync", lambda e: e.dma_start(out=maskt[:], in_=d_mask.rearrange("1 k -> k 1")),
              kind="dma", r=["d_mask"], w=["maskt"])
        kb.op("tensor", lambda e: e.matmul(psE[:, 0:8], ones1, maskrb[:], start=True,
                                           stop=True),
              r=[("big_b", 27264, 27392), "maskrb"], w=["psE"])
        kb.op("vector", lambda e: e.tensor_copy(maskrep[:], psE[:, 0:8]), r=["psE"],
              w=["maskrep"])

        MD0 = 24064
        for k in range(8):
            def f_md(e, k=k):
                return e.tensor_scalar(big_b[:, MD0 + k * 128:MD0 + (k + 1) * 128],
                                       identb, maskrep[:, k:k + 1], None, ALU.mult)
            kb.op("vector", f_md,
                  r=[("big_b", 27136, 27264), "maskrep"],
                  w=[("big_b", MD0 + k * 128, MD0 + (k + 1) * 128)])
        weff_steps = []

        def _weff_step(dt_, k):
            def go():
                lane = 4 + k % 2
                def f_load(e, k=k, dt_=dt_, lane=lane):
                    return e.dma_start(out=sm[:, lane * 1024:(lane + 1) * 1024],
                                       in_=dffn_w[k, dt_ * 128:(dt_ + 1) * 128, :])
                kb.op("gpsimd", f_load, kind="dma",
                      w=[("sm", lane * 1024, (lane + 1) * 1024)])
                for n2 in range(2):
                    ps, pk = (psC, "psC") if n2 == 0 else (psD, "psD")
                    def f_mm(e, k=k, n2=n2, ps=ps, lane=lane):
                        return e.matmul(ps[:, 0:512],
                                        big_b[:, MD0 + k * 128:MD0 + (k + 1) * 128],
                                        sm[:, lane * 1024 + n2 * 512:lane * 1024 + n2 * 512 + 512],
                                        start=(k == 0), stop=(k == 7))
                    kb.op("tensor", f_mm,
                          r=[("big_b", MD0 + k * 128, MD0 + (k + 1) * 128),
                             ("sm", lane * 1024, (lane + 1) * 1024)], w=[pk])
                if k == 7:
                    for n2 in range(2):
                        ps, pk = (psC, "psC") if n2 == 0 else (psD, "psD")
                        def f_ev(e, dt_=dt_, n2=n2, ps=ps):
                            return e.copy(
                                sm[:, 6144 + n2 * 512:6144 + n2 * 512 + 512],
                                ps[:, 0:512])
                        kb.op("scalar", f_ev, r=[pk],
                              w=[("sm", 6144 + n2 * 512, 6144 + n2 * 512 + 512)])
                    def f_sp(e, dt_=dt_):
                        return e.dma_start(
                            out=d_weff[:, dt_ * 1024:(dt_ + 1) * 1024],
                            in_=sm[:, 6144:6144 + 1024])
                    kb.op("sync", f_sp, kind="dma", r=[("sm", 6144, 7168)],
                          w=[("d_weff", dt_ * 1024, (dt_ + 1) * 1024)])
            return go

        for dt_ in range(8):
            for k in range(8):
                weff_steps.append(_weff_step(dt_, k))

        pscnt = [0]

        def next_ps():
            pscnt[0] += 1
            return [(psA, "psA"), (psB, "psB"), (psE, "psE")][pscnt[0] % 3]

        LN1T = ("arena", 0, 8192)

        def emit_p1_lt(h, j):
            lt = h * 8 + j
            stage_lo = None
            def f_ldx(e, lt=lt):
                return e.dma_start(out=convh[:, 0:1024],
                                   in_=dx[lt * 128:(lt + 1) * 128, :])
            kb.op("sync", f_ldx, kind="dma", w=[("convh", 0, 1024)])
            XT = ("convh", 0, 1024)
            SQ = ("convh", 1024, 2048)
            sb0 = 112
            kb.op("vector", lambda e: e.reduce_sum(statn[:, sb0:sb0 + 1], convh[:, 0:1024],
                                                   axis=AX.X), r=[XT], w=[("statn", sb0, sb0 + 1)])
            kb.op("scalar", lambda e: e.activation(convh[:, 1024:2048], convh[:, 0:1024],
                                                   AF.Square,
                                                   accum_out=statn[:, sb0 + 1:sb0 + 2]),
                  r=[XT], w=[SQ, ("statn", sb0 + 1, sb0 + 2)])
            kb.op("vector", lambda e: e.tensor_scalar(statn[:, sb0 + 2:sb0 + 3], statn[:, sb0:sb0 + 1],
                                                      1.0 / D, None, ALU.mult),
                  r=[("statn", sb0, sb0 + 1)], w=[("statn", sb0 + 2, sb0 + 3)])
            kb.op("vector", lambda e: e.tensor_scalar(statn[:, sb0 + 3:sb0 + 4], statn[:, sb0 + 1:sb0 + 2],
                                                      1.0 / D, None, ALU.mult),
                  r=[("statn", sb0 + 1, sb0 + 2)], w=[("statn", sb0 + 3, sb0 + 4)])
            kb.op("vector", lambda e: e.tensor_mul(statn[:, sb0 + 4:sb0 + 5], statn[:, sb0 + 2:sb0 + 3],
                                                   statn[:, sb0 + 2:sb0 + 3]),
                  r=[("statn", sb0 + 2, sb0 + 3)], w=[("statn", sb0 + 4, sb0 + 5)])
            kb.op("vector", lambda e: e.tensor_sub(statn[:, sb0 + 5:sb0 + 6], statn[:, sb0 + 3:sb0 + 4],
                                                   statn[:, sb0 + 4:sb0 + 5]),
                  r=[("statn", sb0 + 3, sb0 + 5)], w=[("statn", sb0 + 5, sb0 + 6)])
            kb.op("scalar", lambda e: e.activation(statn[:, sb0 + 6:sb0 + 7], statn[:, sb0 + 5:sb0 + 6],
                                                   AF.Ln, bias=cpool[:, 0:1]),
                  r=[("statn", sb0 + 5, sb0 + 6), "cpool"], w=[("statn", sb0 + 6, sb0 + 7)])
            kb.op("scalar", lambda e: e.activation(statn[:, sb0 + 7:sb0 + 8], statn[:, sb0 + 6:sb0 + 7],
                                                   AF.Exp, scale=-0.5),
                  r=[("statn", sb0 + 6, sb0 + 7)], w=[("statn", sb0 + 7, sb0 + 8)])
            kb.op("vector", lambda e: e.tensor_scalar(statn[:, sb0 + 8:sb0 + 9],
                                                      statn[:, sb0 + 2:sb0 + 3],
                                                      statn[:, sb0 + 7:sb0 + 8], -1.0,
                                                      ALU.mult, ALU.mult),
                  r=[("statn", sb0 + 2, sb0 + 3), ("statn", sb0 + 7, sb0 + 8)],
                  w=[("statn", sb0 + 8, sb0 + 9)])
            if h == 0:
                stg_ap, STG = sm[:, 2048:3072], ("sm", 2048, 3072)
            else:
                stg_ap, STG = big_b[:, 24064:25088], ("big_b", 24064, 25088)
            def f_norm(e, stg_ap=stg_ap):
                return e.activation(stg_ap, convh[:, 0:1024], AF.Identity,
                                    bias=statn[:, sb0 + 8:sb0 + 9],
                                    scale=statn[:, sb0 + 7:sb0 + 8])
            kb.op("scalar", f_norm, r=[XT, ("statn", sb0 + 7, sb0 + 9)], w=[STG])
            for dt_ in range(8):
                bank, bk = (psT, "psT") if dt_ % 2 == 0 else (psU, "psU")
                def f_tr(e, dt_=dt_, bank=bank, stg_ap=stg_ap):
                    return e.transpose(bank[:, 0:128],
                                       stg_ap[:, dt_ * 128:dt_ * 128 + 128], identb)
                kb.op("tensor", f_tr, r=[STG, ("big_b", 27136, 27264)], w=[bk])
                def f_cp(e, j=j, dt_=dt_, bank=bank):
                    return e.copy(
                        arena[:, dt_ * 1024 + j * 128:dt_ * 1024 + j * 128 + 128],
                        bank[:, 0:128])
                kb.op("scalar", f_cp, r=[bk],
                      w=[("arena", dt_ * 1024 + j * 128, dt_ * 1024 + j * 128 + 128)])

        WINB = 16896

        def emit_p2_ct(h, cti, ct_order):
            ct = ct_order[cti]
            if h == 0:
                nring, rbase = 4, WINB
            else:
                nring, rbase = 2, 25088
            lane = cti % nring
            WI = ("big_b", rbase + lane * 1024, rbase + (lane + 1) * 1024)
            if cti < nring:
                def f_ldw0(e, ct=ct, lane=lane, rbase=rbase):
                    return e.dma_start(
                        out=big_b[:, rbase + lane * 1024:rbase + lane * 1024 + 1024],
                        in_=dw_in[:, ct].rearrange("p dt c -> p (dt c)"))
                kb.op("gpsimd", f_ldw0, kind="dma", w=[WI])
            for lc2 in range(2):
                ps, pk = next_ps()
                for dt_ in range(8):
                    def f_mm(e, lc2=lc2, dt_=dt_, ps=ps, lane=lane, rbase=rbase):
                        return e.matmul(
                            ps[:, 0:512],
                            big_b[:, rbase + lane * 1024 + dt_ * 128:rbase + lane * 1024 + dt_ * 128 + 128],
                            arena[:, dt_ * 1024 + lc2 * 512:dt_ * 1024 + lc2 * 512 + 512],
                            start=(dt_ == 0), stop=(dt_ == 7))
                    kb.op("tensor", f_mm,
                          r=[WI, ("arena", dt_ * 1024 + lc2 * 512,
                                  dt_ * 1024 + lc2 * 512 + 512)], w=[pk])
                if ct < 16:
                    def f_ev(e, lc2=lc2, ps=ps):
                        return e.copy(convh[:, 3 + lc2 * 512:3 + lc2 * 512 + 512],
                                      ps[:, 0:512])
                    kb.op("scalar", f_ev, r=[pk],
                          w=[("convh", 3 + lc2 * 512, 3 + lc2 * 512 + 512)])
                else:
                    if h == 0:
                        zstg, ZS = sm[:, 2048 + lc2 * 512:2048 + lc2 * 512 + 512], \
                            ("sm", 2048 + lc2 * 512, 2048 + lc2 * 512 + 512)
                    else:
                        zstg, ZS = big_b[:, 24064 + lc2 * 512:24064 + lc2 * 512 + 512], \
                            ("big_b", 24064 + lc2 * 512, 24064 + lc2 * 512 + 512)
                    def f_silu(e, ps=ps, zstg=zstg):
                        return e.activation(zstg, ps[:, 0:512], AF.Silu)
                    kb.op("scalar", f_silu, r=[pk], w=[ZS])
            if ct < 16:
                if h == 1:
                    def f_pad(e, ct=ct):
                        return e.tensor_copy(convh[:, 0:3], tails[:, ct * 3:ct * 3 + 3])
                    kb.op("vector", f_pad, r=[("tails", ct * 3, ct * 3 + 3)],
                          w=[("convh", 0, 3)])
                else:
                    def f_tail(e, ct=ct):
                        return e.tensor_copy(tails[:, ct * 3:ct * 3 + 3], convh[:, 1024:1027])
                    kb.op("vector", f_tail, r=[("convh", 1024, 1027)],
                          w=[("tails", ct * 3, ct * 3 + 3)])
                XP = ("convh", 0, 1027)
                def f_c0(e, ct=ct):
                    return e.tensor_scalar(convh[:, 1027:1027 + 1024], convh[:, 0:1024],
                                           wcol[:, ct * 4:ct * 4 + 1], None, ALU.mult)
                kb.op("vector", f_c0, r=[XP, "wcol"], w=[("convh", 1027, 2051)])
                for k in range(1, 4):
                    def f_ck(e, ct=ct, k=k):
                        return e.scalar_tensor_tensor(
                            convh[:, 1027:1027 + 1024], convh[:, k:k + 1024],
                            wcol[:, ct * 4 + k:ct * 4 + k + 1],
                            convh[:, 1027:1027 + 1024], ALU.mult, ALU.add)
                    kb.op("vector", f_ck, r=[XP, "wcol", ("convh", 1027, 2051)],
                          w=[("convh", 1027, 2051)])
                if h == 0:
                    def f_si(e, ct=ct):
                        return e.activation(big_a[:, ct * 1024:(ct + 1) * 1024],
                                            convh[:, 1027:1027 + 1024],
                                            AF.Silu, bias=cbcol[:, ct:ct + 1])
                    kb.op("scalar", f_si, r=[("convh", 1027, 2051), "cbcol"],
                          w=[("big_a", ct * 1024, (ct + 1) * 1024)])
                    def f_sp(e, ct=ct):
                        return e.dma_start(out=d_xconv[ct][:, 0:1024],
                                           in_=big_a[:, ct * 1024:(ct + 1) * 1024])
                    kb.op("sync", f_sp, kind="dma", r=[("big_a", ct * 1024, (ct + 1) * 1024)],
                          w=[("d_xconv", ct * 2, ct * 2 + 1)])
                else:
                    def f_si(e, ct=ct):
                        return e.activation(big_b[:, 24064:25088],
                                            convh[:, 1027:1027 + 1024],
                                            AF.Silu, bias=cbcol[:, ct:ct + 1])
                    kb.op("scalar", f_si, r=[("convh", 1027, 2051), "cbcol"],
                          w=[("big_b", 24064, 25088)])
                    def f_sp(e, ct=ct):
                        return e.dma_start(out=d_xconv[ct][:, 1024:2048],
                                           in_=big_b[:, 24064:25088])
                    kb.op("sync", f_sp, kind="dma", r=[("big_b", 24064, 25088)],
                          w=[("d_xconv", ct * 2 + 1, ct * 2 + 2)])
            else:
                if h == 0:
                    def f_spz(e, ct=ct):
                        return e.dma_start(out=d_sz[ct - 16][:, 0:1024],
                                           in_=sm[:, 2048:3072])
                    kb.op("sync", f_spz, kind="dma", r=[("sm", 2048, 3072)],
                          w=[("d_sz", (ct - 16) * 2, (ct - 16) * 2 + 1)])
                else:
                    def f_spz(e, ct=ct):
                        return e.dma_start(out=d_sz[ct - 16][:, 1024:2048],
                                           in_=big_b[:, 24064:25088])
                    kb.op("sync", f_spz, kind="dma", r=[("big_b", 24064, 25088)],
                          w=[("d_sz", (ct - 16) * 2 + 1, (ct - 16) * 2 + 2)])
            if cti + nring - 1 < 32 and cti + nring - 1 >= nring:
                nct = ct_order[cti + nring - 1]
                sl = (cti + nring - 1) % nring
                def f_ldw(e, nct=nct, sl=sl, rbase=rbase):
                    return e.dma_start(
                        out=big_b[:, rbase + sl * 1024:rbase + sl * 1024 + 1024],
                        in_=dw_in[:, nct].rearrange("p dt c -> p (dt c)"))
                kb.op("gpsimd", f_ldw, kind="dma",
                      w=[("big_b", rbase + sl * 1024, rbase + sl * 1024 + 1024)])

        WX_ALL = ("big_b", 0, 16 * 1056)

        REP_STG = {0: 24064, 1: 23040}
        SEL0 = 18432

        repc = [0]

        def rep_unit(which, s):
            bb = 0 if which == 0 else 16384
            for lc2 in range(2):
                repc[0] += 1
                ps, pk = (psC, "psC") if repc[0] % 2 == 0 else (psD, "psD")
                def f_mm(e, which=which, s=s, lc2=lc2, ps=ps):
                    return e.matmul(
                        ps[:, 0:512],
                        big_b[0:16, SEL0 + s * 128:SEL0 + (s + 1) * 128],
                        big_b[0:16, REP_STG[which] + lc2 * 512:REP_STG[which] + lc2 * 512 + 512],
                        start=True, stop=True)
                kb.op("tensor", f_mm,
                      r=[("big_b", SEL0 + s * 128, SEL0 + (s + 1) * 128),
                         ("big_b", REP_STG[which] + lc2 * 512,
                          REP_STG[which] + lc2 * 512 + 512)],
                      w=[pk])
                def f_ev(e, bb=bb, s=s, lc2=lc2, ps=ps):
                    return e.copy(
                        big_a[:, bb + s * HF + lc2 * 512:bb + s * HF + lc2 * 512 + 512],
                        ps[:, 0:512])
                kb.op("scalar", f_ev, r=[pk],
                      w=[("big_a", bb + s * HF + lc2 * 512,
                          bb + s * HF + lc2 * 512 + 512)])

        def emit_p3a_half(h):
            if h == 1:
                for ct in range(16):
                    def f_rx(e, ct=ct):
                        return e.dma_start(out=big_a[:, ct * 1024:(ct + 1) * 1024],
                                           in_=d_xconv[ct][:, 1024:2048])
                    kb.op("sync" if ct % 2 == 0 else "gpsimd", f_rx, kind="dma",
                          r=[("d_xconv", ct * 2 + 1, ct * 2 + 2)],
                          w=[("big_a", ct * 1024, (ct + 1) * 1024)])
            kb.op("gpsimd", lambda e: e.dma_start(out=big_b[0:16, SEL0:SEL0 + 2048],
                                                  in_=dsel), kind="dma",
                  w=[("big_b", SEL0, SEL0 + 2048)])
            for which in (1, 0):
                col0 = 1024 if which == 0 else 1040
                ps, pk = (psD, "psD") if which == 0 else (psC, "psC")
                for lc2 in range(2):
                    for ct in range(16):
                        def f_mm(e, col0=col0, lc2=lc2, ct=ct, ps=ps):
                            return e.matmul(
                                ps[0:16, 0:512],
                                big_b[:, ct * 1056 + col0:ct * 1056 + col0 + 16],
                                big_a[:, ct * 1024 + lc2 * 512:ct * 1024 + lc2 * 512 + 512],
                                start=(ct == 0), stop=(ct == 15))
                        kb.op("tensor", f_mm,
                              r=[WX_ALL, ("big_a", ct * 1024 + lc2 * 512,
                                          ct * 1024 + lc2 * 512 + 512)], w=[pk])
                    def f_st(e, ps=ps, which=which, lc2=lc2):
                        return e.copy(
                            big_b[0:16, REP_STG[which] + lc2 * 512:REP_STG[which] + lc2 * 512 + 512],
                            ps[0:16, 0:512])
                    kb.op("scalar", f_st, r=[pk],
                          w=[("big_b", REP_STG[which] + lc2 * 512,
                              REP_STG[which] + lc2 * 512 + 512)])
            emit_p3b_loads(h, 0)
            base = 8192 if h == 0 else 0
            for dp in range(8):
                for lc2 in range(2):
                    ps, pk = next_ps()
                    for ct in range(16):
                        def f_mm(e, dp=dp, lc2=lc2, ct=ct, ps=ps):
                            return e.matmul(
                                ps[:, 0:512],
                                big_b[:, ct * 1056 + dp * 128:ct * 1056 + dp * 128 + 128],
                                big_a[:, ct * 1024 + lc2 * 512:ct * 1024 + lc2 * 512 + 512],
                                start=(ct == 0), stop=(ct == 15))
                        kb.op("tensor", f_mm,
                              r=[WX_ALL, ("big_a", ct * 1024 + lc2 * 512,
                                          ct * 1024 + lc2 * 512 + 512)], w=[pk])
                    def f_ev(e, dp=dp, lc2=lc2, ps=ps, base=base):
                        return e.copy(
                            arena[:, base + dp * 1024 + lc2 * 512:base + dp * 1024 + lc2 * 512 + 512],
                            ps[:, 0:512])
                    kb.op("scalar", f_ev, r=[pk],
                          w=[("arena", base + dp * 1024 + lc2 * 512,
                              base + dp * 1024 + lc2 * 512 + 512)])
            emit_p3b(h, 0, do_loads=False)
            for s in (0, 1):
                rep_unit(0, s)
                rep_unit(1, s)

        HLn, L0n, L1n, L2n = 16896, 17920, 18944, 19968
        WDTn, YGn = 20992, 23040
        HK = ("big_b", HLn, HLn + 1024)
        LK = [("big_b", c, c + 1024) for c in (L0n, L1n, L2n)]
        HF = 1024


        def emit_p3b_loads(half, it):
            par = it % 2
            WDL = ("big_b", WDTn + par * 1024, WDTn + (par + 1) * 1024)
            def f_lw(e, it=it, par=par):
                return e.dma_start(
                    out=big_b[:, WDTn + par * 1024:WDTn + (par + 1) * 1024],
                    in_=dw_dt[:, it].rearrange("p dt i -> p (dt i)"))
            kb.op("gpsimd", f_lw, kind="dma", w=[WDL])
            def f_lxc(e, it=it, half=half, par=par):
                return e.dma_start(out=sm[:, (2 + par) * 1024:(3 + par) * 1024],
                                   in_=d_xconv[it][:, half * HF:(half + 1) * HF])
            kb.op("sync", f_lxc, kind="dma", r=[("d_xconv", it * 2 + half, it * 2 + half + 1)],
                  w=[("sm", (2 + par) * 1024, (3 + par) * 1024)])
            def f_lsz(e, it=it, half=half, par=par):
                return e.dma_start(out=sm[:, (4 + par) * 1024:(5 + par) * 1024],
                                   in_=d_sz[it][:, half * HF:(half + 1) * HF])
            kb.op("sync", f_lsz, kind="dma", r=[("d_sz", it * 2 + half, it * 2 + half + 1)],
                  w=[("sm", (4 + par) * 1024, (5 + par) * 1024)])

        def emit_p3b(half, it, do_loads=True):
            par = it % 2
            WDL = ("big_b", WDTn + par * 1024, WDTn + (par + 1) * 1024)
            if do_loads:
                emit_p3b_loads(half, it)
            XCL = ("sm", (2 + par) * 1024, (3 + par) * 1024)
            SZL = ("sm", (4 + par) * 1024, (5 + par) * 1024)
            DTB = T3 + par * 1024
            DTK = ("bigf", DTB, DTB + 1024)
            for lc2 in range(2):
                ps, pk = next_ps()
                base_h = 8192 if half == 0 else 0
                for dp in range(8):
                    def f_mm(e, lc2=lc2, dp=dp, ps=ps, par=par, base_h=base_h):
                        return e.matmul(
                            ps[:, 0:512],
                            big_b[:, WDTn + par * 1024 + dp * 128:WDTn + par * 1024 + dp * 128 + 128],
                            arena[:, base_h + dp * 1024 + lc2 * 512:base_h + dp * 1024 + lc2 * 512 + 512],
                            start=(dp == 0), stop=(dp == 7))
                    kb.op("tensor", f_mm, r=[WDL, ("arena", base_h, base_h + 8192)], w=[pk])
                R_ = ("bigf", T2 + 1024 + lc2 * 512, T2 + 1024 + lc2 * 512 + 512)
                A_ = ("bigf", T0 + 1024 + lc2 * 512, T0 + 1024 + lc2 * 512 + 512)
                def f_r(e, it=it, lc2=lc2, ps=ps):
                    return e.activation(
                        bigf[:, T2 + 1024 + lc2 * 512:T2 + 1024 + lc2 * 512 + 512],
                        ps[:, 0:512], AF.Relu, bias=bdtcol[:, it:it + 1])
                kb.op("scalar", f_r, r=[pk, "bdtcol"], w=[R_])
                def f_a(e, it=it, lc2=lc2, ps=ps):
                    return e.activation(
                        bigf[:, T0 + 1024 + lc2 * 512:T0 + 1024 + lc2 * 512 + 512],
                        ps[:, 0:512], AF.Abs, bias=bdtcol[:, it:it + 1])
                kb.op("scalar", f_a, r=[pk, "bdtcol"], w=[A_])
                def f_e(e, lc2=lc2):
                    return e.activation(
                        bigf[:, T0 + 1024 + lc2 * 512:T0 + 1024 + lc2 * 512 + 512],
                        bigf[:, T0 + 1024 + lc2 * 512:T0 + 1024 + lc2 * 512 + 512],
                        AF.Exp, scale=-1.0)
                kb.op("scalar", f_e, r=[A_], w=[A_])
                def f_l(e, lc2=lc2):
                    return e.activation(
                        bigf[:, T0 + 1024 + lc2 * 512:T0 + 1024 + lc2 * 512 + 512],
                        bigf[:, T0 + 1024 + lc2 * 512:T0 + 1024 + lc2 * 512 + 512],
                        AF.Ln, bias=cpool[:, 1:2])
                kb.op("scalar", f_l, r=[A_, "cpool"], w=[A_])
                def f_d(e, lc2=lc2, DTB=DTB):
                    return e.tensor_add(bigf[:, DTB + lc2 * 512:DTB + lc2 * 512 + 512],
                                        bigf[:, T2 + 1024 + lc2 * 512:T2 + 1024 + lc2 * 512 + 512],
                                        bigf[:, T0 + 1024 + lc2 * 512:T0 + 1024 + lc2 * 512 + 512])
                kb.op("vector", f_d, r=[R_, A_],
                      w=[("bigf", DTB + lc2 * 512, DTB + lc2 * 512 + 512)])
            def f_dtx(e, DTB=DTB, par=par):
                return e.tensor_mul(sm[:, par * 1024:(par + 1) * 1024],
                                    bigf[:, DTB:DTB + 1024],
                                    sm[:, (2 + par) * 1024:(3 + par) * 1024])
            kb.op("vector", f_dtx, r=[DTK, XCL], w=[("sm", par * 1024, (par + 1) * 1024)])

        def emit_p4(half, it, lead_reps=False):
            par = it % 2
            DTB = T3 + par * 1024
            DTK = ("bigf", DTB, DTB + 1024)
            DTXL = ("sm", par * 1024, (par + 1) * 1024)
            XCL = ("sm", (2 + par) * 1024, (3 + par) * 1024)
            SZL = ("sm", (4 + par) * 1024, (5 + par) * 1024)
            YA = ("bigf", T2, T2 + 1024)
            for s in range(NS):
                sp = s % 2
                DA = ("bigf", T1 + sp * 1024, T1 + (sp + 1) * 1024)
                def f_da(e, it=it, s=s, sp=sp, DTB=DTB):
                    return e.activation(bigf[:, T1 + sp * 1024:T1 + (sp + 1) * 1024],
                                        bigf[:, DTB:DTB + 1024], AF.Exp,
                                        scale=pa[:, it * 16 + s:it * 16 + s + 1])
                kb.op("scalar", f_da, r=[DTK, ("pa", 0, 256)], w=[DA])
                DBU = ("sm", 6144, 7168)
                def f_dbu(e, s=s, par=par):
                    return e.tensor_mul(sm[:, 6144:7168],
                                        sm[:, par * 1024:(par + 1) * 1024],
                                        big_a[:, s * HF:(s + 1) * HF])
                kb.op("vector", f_dbu, r=[DTXL, ("big_a", s * HF, (s + 1) * HF)],
                      w=[DBU])
                if half == 0:
                    def f_scan(e, sp=sp):
                        return e.tensor_tensor_scan(
                            big_b[:, HLn:HLn + 1024],
                            bigf[:, T1 + sp * 1024:T1 + (sp + 1) * 1024],
                            sm[:, 6144:7168],
                            0.0, ALU.mult, ALU.add)
                    kb.op("vector", f_scan, r=[DA, DBU], w=[HK])
                    ic = it * 16 + s
                    def f_hc(e, ic=ic):
                        return e.tensor_copy(hcol[:, ic:ic + 1],
                                             big_b[:, HLn + 1023:HLn + 1024])
                    kb.op("vector", f_hc, r=[HK], w=[("hcol", ic, ic + 1)])
                else:
                    ic = it * 16 + s
                    def f_scan(e, sp=sp, ic=ic):
                        return e.tensor_tensor_scan(
                            big_b[:, HLn:HLn + 1024],
                            bigf[:, T1 + sp * 1024:T1 + (sp + 1) * 1024],
                            sm[:, 6144:7168],
                            hcol[:, ic:ic + 1], ALU.mult, ALU.add)
                    kb.op("vector", f_scan, r=[DA, DBU, ("hcol", ic, ic + 1)], w=[HK])
                leaf = L0n if s % 2 == 0 else L1n
                def f_ch(e, s=s, leaf=leaf):
                    return e.tensor_mul(big_b[:, leaf:leaf + 1024],
                                        big_b[:, HLn:HLn + 1024],
                                        big_a[:, 16384 + s * HF:16384 + (s + 1) * HF])
                kb.op("vector", f_ch,
                      r=[HK, ("big_a", 16384 + s * HF, 16384 + (s + 1) * HF)],
                      w=[("big_b", leaf, leaf + 1024)])
                if s % 4 == 1:
                    def f_pair(e):
                        return e.tensor_add(big_b[:, L2n:L2n + 1024],
                                            big_b[:, L0n:L0n + 1024],
                                            big_b[:, L1n:L1n + 1024])
                    kb.op("vector", f_pair, r=[LK[0], LK[1]], w=[LK[2]])
                if s % 4 == 3:
                    def f_pair1(e):
                        return e.tensor_add(big_b[:, L0n:L0n + 1024],
                                            big_b[:, L0n:L0n + 1024],
                                            big_b[:, L1n:L1n + 1024])
                    kb.op("vector", f_pair1, r=[LK[0], LK[1]], w=[LK[0]])
                    def f_quad(e):
                        return e.tensor_add(big_b[:, L0n:L0n + 1024],
                                            big_b[:, L2n:L2n + 1024],
                                            big_b[:, L0n:L0n + 1024])
                    kb.op("vector", f_quad, r=[LK[2], LK[0]], w=[LK[0]])
                    if s == 3:
                        kb.op("vector", lambda e: e.tensor_copy(bigf[:, T2:T2 + 1024],
                                                                big_b[:, L0n:L0n + 1024]),
                              r=[LK[0]], w=[YA])
                    else:
                        kb.op("vector", lambda e: e.tensor_add(bigf[:, T2:T2 + 1024],
                                                               bigf[:, T2:T2 + 1024],
                                                               big_b[:, L0n:L0n + 1024]),
                              r=[LK[0], YA], w=[YA])
                if lead_reps and s + 2 < NS:
                    rep_unit(0, s + 2)
                    rep_unit(1, s + 2)
            YF = ("bigf", T0, T0 + 1024)
            def f_yf(e, par=par):
                return e.tensor_add(bigf[:, T0:T0 + 1024], bigf[:, T2:T2 + 1024],
                                    sm[:, (2 + par) * 1024:(3 + par) * 1024])
            kb.op("vector", f_yf, r=[YA, XCL], w=[YF])
            def f_yg(e, par=par):
                return e.tensor_mul(big_b[:, YGn:YGn + 1024], bigf[:, T0:T0 + 1024],
                                    sm[:, (4 + par) * 1024:(5 + par) * 1024])
            kb.op("vector", f_yg, r=[YF, SZL], w=[("big_b", YGn, YGn + 1024)])
            def f_spyg(e, it=it, half=half):
                return e.dma_start(out=d_yg[it][:, half * HF:(half + 1) * HF],
                                   in_=big_b[:, YGn:YGn + 1024])
            kb.op("sync", f_spyg, kind="dma", r=[("big_b", YGn, YGn + 1024)],
                  w=[("d_yg", it * 2 + half, it * 2 + half + 1)])

        ct_order = []
        for k in range(16):
            ct_order += [k, 16 + k]
        for j in range(8):
            emit_p1_lt(0, j)
        kb.op("vector", lambda e: e.memset(convh[:, 0:3], 0.0), w=[("convh", 0, 3)])
        for cti in range(32):
            emit_p2_ct(0, cti, ct_order)
            for _ in range(2):
                if weff_steps:
                    weff_steps.pop(0)()
        while weff_steps:
            weff_steps.pop(0)()
        for ct in range(16):
            def f_lwx(e, ct=ct):
                return e.dma_start(out=big_b[:, ct * 1056:(ct + 1) * 1056],
                                   in_=dw_x[ct * 128:(ct + 1) * 128, :])
            kb.op("gpsimd", f_lwx, kind="dma",
                  w=[("big_b", ct * 1056, (ct + 1) * 1056)])
        emit_p3a_half(0)
        for it in range(NT):
            emit_p4(0, it, lead_reps=(it == 0))
            if it + 1 < NT:
                emit_p3b(0, it + 1)
            if it < 8:
                emit_p1_lt(1, it)
            else:
                for cti in range((it - 8) * 4, (it - 8) * 4 + 4):
                    emit_p2_ct(1, cti, ct_order)
        def emit_mix_early_loads(lt):
            ab = 8192 + (lt % 4) * 2048
            for it in range(NT):
                def f_ly(e, lt=lt, it=it, ab=ab):
                    return e.dma_start(
                        out=arena[:, ab + it * 128:ab + (it + 1) * 128],
                        in_=d_yg[it][:, lt * 128:(lt + 1) * 128])
                kb.op("gpsimd", f_ly, kind="dma", r=[("d_yg", it * 2, it * 2 + 1)],
                      w=[("arena", ab + it * 128, ab + (it + 1) * 128)])

        def emit_mix_early(lt):
            ab = 8192 + (lt % 4) * 2048
            for n2 in range(2):
                ps, pk = (psC, "psC") if n2 == 0 else (psD, "psD")
                for it in range(NT):
                    def f_mm(e, n2=n2, it=it, ps=ps, ab=ab):
                        return e.matmul(
                            ps[:, 0:512],
                            arena[:, ab + it * 128:ab + (it + 1) * 128],
                            big_b[:, it * 1024 + n2 * 512:it * 1024 + n2 * 512 + 512],
                            start=(it == 0), stop=(it == 15))
                    kb.op("tensor", f_mm,
                          r=[("arena", ab, ab + 2048), ("big_b", 0, 16384)], w=[pk])
                def f_ev(e, n2=n2, ps=ps):
                    return e.copy(big_b[:, 16384 + n2 * 512:16384 + n2 * 512 + 512],
                                  ps[:, 0:512])
                kb.op("scalar", f_ev, r=[pk],
                      w=[("big_b", 16384 + n2 * 512, 16384 + n2 * 512 + 512)])
            def f_sp(e, lt=lt):
                return e.dma_start(out=d_mix[lt], in_=big_b[:, 16384:17408])
            kb.op("sync", f_sp, kind="dma", r=[("big_b", 16384, 17408)],
                  w=[("d_mix", lt, lt + 1)])

        emit_p3a_half(1)
        for it in range(NT):
            emit_p4(1, it, lead_reps=(it == 0))
            if it + 1 < NT:
                emit_p3b(1, it + 1)
            if it < 8:
                for k in (it * 2, it * 2 + 1):
                    def f_lwo(e, k=k):
                        return e.dma_start(out=big_b[:, k * 1024:(k + 1) * 1024],
                                           in_=dw_out[k * 128:(k + 1) * 128, :])
                    kb.op("gpsimd", f_lwo, kind="dma",
                          w=[("big_b", k * 1024, (k + 1) * 1024)])
            if it >= 7 and it < 15:
                emit_mix_early_loads(it - 7)
            if it >= 8:
                emit_mix_early(it - 8)

        for lt in (0, 1):
            mb0 = T2 + (lt % 2) * 1024
            def f_lm0(e, lt=lt, mb0=mb0):
                return e.dma_start(out=bigf[:, mb0:mb0 + 1024], in_=d_mix[lt])
            kb.op("gpsimd", f_lm0, kind="dma", r=[("d_mix", lt, lt + 1)],
                  w=[("bigf", mb0, mb0 + 1024)])
        for it in range(NT):
            def f_lyg(e, it=it):
                return e.dma_start(out=big_a[:, it * L + 1024:(it + 1) * L],
                                   in_=d_yg[it][:, 1024:2048])
            kb.op("sync", f_lyg, kind="dma", r=[("d_yg", it * 2 + 1, it * 2 + 2)],
                  w=[("big_a", it * L + 1024, (it + 1) * L)])
        kb.op("gpsimd", lambda e: e.dma_start(out=sm[:, 0:7168], in_=d_weff[:, 0:7168]),
              kind="dma", r=[("d_weff", 0, 7168)], w=["sm"])
        kb.op("gpsimd", lambda e: e.dma_start(out=big_b[:, 24064:25088],
                                              in_=d_weff[:, 7168:8192]),
              kind="dma", r=[("d_weff", 7168, 8192)], w=[("big_b", 24064, 25088)])
        YG_ALL = ("big_a", 0, 16 * L)
        WO_ALL = ("big_b", 0, 16 * 1024)
        WEFF = ("sm", 0, 7168)

        kb.op("sync", lambda e: e.dma_start(out=bigf[0:8, T1:T1 + 1024], in_=dffn_b),
              kind="dma", w=[("bigf", T1, T1 + 1024)])
        kb.op("vector", lambda e: e.tensor_scalar(bigf[0:8, T1:T1 + 1024],
                                                  bigf[0:8, T1:T1 + 1024],
                                                  maskt[:], None, ALU.mult),
              r=[("bigf", T1, T1 + 1024), "maskt"], w=[("bigf", T1, T1 + 1024)])
        kb.op("vector", lambda e: e.tensor_copy(big_b[0:8, 16384:16384 + 1024],
                                                bigf[0:8, T1:T1 + 1024]),
              r=[("bigf", T1, T1 + 1024)], w=[("big_b", 16384, 17408)])
        for n2 in range(2):
            def f_mm(e, n2=n2):
                return e.matmul(psE[0:1, 0:512], ones8[:],
                                big_b[0:8, 16384 + n2 * 512:16384 + n2 * 512 + 512],
                                start=True, stop=True)
            kb.op("tensor", f_mm, r=[("big_b", 16384, 17408), "ones8"], w=["psE"])
            def f_ev(e, n2=n2):
                return e.tensor_copy(beffrow[:, n2 * 512:n2 * 512 + 512], psE[0:1, 0:512])
            kb.op("vector", f_ev, r=["psE"], w=[("big_b", 20480 + n2 * 512, 20480 + n2 * 512 + 512)])
        for n2 in range(2):
            def f_mm(e, n2=n2):
                return e.matmul(psE[:, 0:512], ones1,
                                beffrow[:, n2 * 512:n2 * 512 + 512], start=True,
                                stop=True)
            kb.op("tensor", f_mm, r=[("big_b", 20480, 21504), ("big_b", 27264, 27392)], w=["psE"])
            def f_ev(e, n2=n2):
                return e.tensor_copy(
                    convh[:, n2 * 512:n2 * 512 + 512], psE[:, 0:512])
            kb.op("vector", f_ev, r=["psE"],
                  w=[("convh", n2 * 512, n2 * 512 + 512)])
        BEFF = ("convh", 0, 1024)

        statn = sb("statn", [128, 128], F32)

        def ln_ops(src_lo, dst_ap_fn, dst_w, sb_, sq_lo):
            SRC = ("bigf", src_lo, src_lo + 1024)
            SQ5 = ("bigf", sq_lo, sq_lo + 1024)
            c = lambda a, b: ("statn", sb_ + a, sb_ + b)
            s_ = statn
            kb.op("vector", lambda e: e.reduce_sum(s_[:, sb_:sb_ + 1],
                                                   bigf[:, src_lo:src_lo + 1024],
                                                   axis=AX.X), r=[SRC], w=[c(0, 1)])
            kb.op("scalar", lambda e: e.activation(bigf[:, sq_lo:sq_lo + 1024],
                                                   bigf[:, src_lo:src_lo + 1024],
                                                   AF.Square,
                                                   accum_out=s_[:, sb_ + 1:sb_ + 2]),
                  r=[SRC], w=[SQ5, c(1, 2)])
            kb.op("vector", lambda e: e.tensor_scalar(s_[:, sb_ + 2:sb_ + 3],
                                                      s_[:, sb_:sb_ + 1], 1.0 / D,
                                                      None, ALU.mult),
                  r=[c(0, 1)], w=[c(2, 3)])
            kb.op("vector", lambda e: e.tensor_scalar(s_[:, sb_ + 3:sb_ + 4],
                                                      s_[:, sb_ + 1:sb_ + 2], 1.0 / D,
                                                      None, ALU.mult),
                  r=[c(1, 2)], w=[c(3, 4)])
            kb.op("vector", lambda e: e.tensor_mul(s_[:, sb_ + 4:sb_ + 5],
                                                   s_[:, sb_ + 2:sb_ + 3],
                                                   s_[:, sb_ + 2:sb_ + 3]),
                  r=[c(2, 3)], w=[c(4, 5)])
            kb.op("vector", lambda e: e.tensor_sub(s_[:, sb_ + 5:sb_ + 6],
                                                   s_[:, sb_ + 3:sb_ + 4],
                                                   s_[:, sb_ + 4:sb_ + 5]),
                  r=[c(3, 5)], w=[c(5, 6)])
            kb.op("scalar", lambda e: e.activation(s_[:, sb_ + 6:sb_ + 7],
                                                   s_[:, sb_ + 5:sb_ + 6], AF.Ln,
                                                   bias=cpool[:, 0:1]),
                  r=[c(5, 6), "cpool"], w=[c(6, 7)])
            kb.op("scalar", lambda e: e.activation(s_[:, sb_ + 7:sb_ + 8],
                                                   s_[:, sb_ + 6:sb_ + 7], AF.Exp,
                                                   scale=-0.5),
                  r=[c(6, 7)], w=[c(7, 8)])
            kb.op("vector", lambda e: e.tensor_scalar(s_[:, sb_ + 8:sb_ + 9],
                                                      s_[:, sb_ + 2:sb_ + 3],
                                                      s_[:, sb_ + 7:sb_ + 8], -1.0,
                                                      ALU.mult, ALU.mult),
                  r=[c(2, 3), c(7, 8)], w=[c(8, 9)])
            def f_dst(e):
                return e.activation(dst_ap_fn(), bigf[:, src_lo:src_lo + 1024],
                                    AF.Identity, bias=s_[:, sb_ + 8:sb_ + 9],
                                    scale=s_[:, sb_ + 7:sb_ + 8])
            kb.op("scalar", f_dst, r=[SRC, c(7, 9)], w=[dst_w])


        def emit_mix(lt):
            mb = T2 + (lt % 2) * 1024
            def f_lm(e, lt=lt, mb=mb):
                return e.dma_start(out=bigf[:, mb:mb + 1024], in_=d_mix[lt])
            kb.op("gpsimd", f_lm, kind="dma", r=[("d_mix", lt, lt + 1)],
                  w=[("bigf", mb, mb + 1024)])

        def emit_mix_late(lt):
            for n2 in range(2):
                ps, pk = (psA, "psA") if n2 == 0 else (psB, "psB")
                for it in range(NT):
                    def f_mm(e, lt=lt, n2=n2, it=it, ps=ps):
                        return e.matmul(
                            ps[:, 0:512],
                            big_a[:, it * L + lt * 128:it * L + lt * 128 + 128],
                            big_b[:, it * 1024 + n2 * 512:it * 1024 + n2 * 512 + 512],
                            start=(it == 0), stop=(it == 15))
                    kb.op("tensor", f_mm, r=[YG_ALL, WO_ALL], w=[pk])
                def f_ev(e, n2=n2, ps=ps):
                    return e.copy(big_b[:, 18432 + n2 * 512:18432 + n2 * 512 + 512],
                                  ps[:, 0:512])
                kb.op("scalar", f_ev, r=[pk],
                      w=[("big_b", 18432 + n2 * 512, 18432 + n2 * 512 + 512)])
            def f_sp(e, lt=lt):
                return e.dma_start(out=d_mix[lt], in_=big_b[:, 18432:19456])
            kb.op("sync", f_sp, kind="dma", r=[("big_b", 18432, 19456)],
                  w=[("d_mix", lt, lt + 1)])

        def emit_tail_a(lt):
            p = lt % 2
            mb = T2 + p * 1024
            xs = p * 1024
            x2s = 2048 + p * 1024
            sqs = 6180 + p * 1024
            l2s = 16384 if p == 0 else 25088
            l2t = 17408 if p == 0 else 26112
            def f_ldx(e, lt=lt, xs=xs):
                return e.dma_start(out=bigf[:, xs:xs + 1024],
                                   in_=dx[lt * 128:(lt + 1) * 128, :])
            kb.op("sync", f_ldx, kind="dma", w=[("bigf", xs, xs + 1024)])
            XT5 = ("bigf", xs, xs + 1024)

            sb0 = (lt % 4) * 32
            ln_ops(mb, lambda sqs=sqs: bigf[:, sqs:sqs + 1024],
                   ("bigf", sqs, sqs + 1024), sb0, sq_lo=sqs)
            kb.op("vector", lambda e, xs=xs, x2s=x2s, sqs=sqs: e.tensor_add(
                                                   bigf[:, x2s:x2s + 1024],
                                                   bigf[:, xs:xs + 1024],
                                                   bigf[:, sqs:sqs + 1024]),
                  r=[("bigf", sqs, sqs + 1024), XT5], w=[("bigf", x2s, x2s + 1024)])
            X2 = ("bigf", x2s, x2s + 1024)
            ln_ops(x2s, lambda l2s=l2s: big_b[:, l2s:l2s + 1024],
                   ("big_b", l2s, l2s + 1024), sb0 + 10, sq_lo=sqs)
            for dt_ in range(8):
                bank, bk = (psT, "psT") if dt_ % 2 == 0 else (psU, "psU")
                def f_tr(e, dt_=dt_, bank=bank, l2s=l2s):
                    return e.transpose(bank[:, 0:128],
                                       big_b[:, l2s + dt_ * 128:l2s + dt_ * 128 + 128],
                                       identb)
                kb.op("tensor", f_tr, r=[("big_b", l2s, l2s + 1024), ("big_b", 27136, 27264)],
                      w=[bk])
                def f_cp(e, dt_=dt_, bank=bank, l2t=l2t):
                    return e.copy(
                        big_b[:, l2t + dt_ * 128:l2t + dt_ * 128 + 128],
                        bank[:, 0:128])
                kb.op("scalar", f_cp, r=[bk],
                      w=[("big_b", l2t + dt_ * 128, l2t + dt_ * 128 + 128)])
            LN2T = ("big_b", l2t, l2t + 1024)
            for n2 in range(2):
                if p == 0:
                    ps, pk = (psC, "psC") if n2 == 0 else (psD, "psD")
                else:
                    ps, pk = (psE, "psE") if n2 == 0 else (psV, "psV")
                for dt_ in range(8):
                    if dt_ < 7:
                        rhs_ap = sm[:, dt_ * 1024 + n2 * 512:dt_ * 1024 + n2 * 512 + 512]
                        rhs_k = WEFF
                    else:
                        rhs_ap = big_b[:, 24064 + n2 * 512:24064 + n2 * 512 + 512]
                        rhs_k = ("big_b", 24064, 25088)
                    def f_mm(e, n2=n2, dt_=dt_, ps=ps, rhs_ap=rhs_ap, l2t=l2t):
                        return e.matmul(
                            ps[:, 0:512],
                            big_b[:, l2t + dt_ * 128:l2t + dt_ * 128 + 128],
                            rhs_ap,
                            start=(dt_ == 0), stop=(dt_ == 7))
                    kb.op("tensor", f_mm, r=[LN2T, rhs_k], w=[pk])

        def emit_tail_b(lt):
            p = lt % 2
            xs = p * 1024
            x2s = 2048 + p * 1024
            sqs = 6180 + p * 1024
            sb0 = (lt % 4) * 32
            X2 = ("bigf", x2s, x2s + 1024)
            for n2 in range(2):
                if p == 0:
                    ps, pk = (psC, "psC") if n2 == 0 else (psD, "psD")
                else:
                    ps, pk = (psE, "psE") if n2 == 0 else (psV, "psV")
                def f_ev(e, n2=n2, ps=ps, xs=xs):
                    return e.tensor_add(
                        bigf[:, xs + n2 * 512:xs + n2 * 512 + 512], ps[:, 0:512],
                        convh[:, n2 * 512:n2 * 512 + 512])
                kb.op("vector", f_ev, r=[pk, BEFF],
                      w=[("bigf", xs + n2 * 512, xs + n2 * 512 + 512)])
            ln_ops(xs, lambda sqs=sqs: bigf[:, sqs:sqs + 1024],
                   ("bigf", sqs, sqs + 1024), sb0 + 20, sq_lo=sqs)
            kb.op("vector", lambda e, sqs=sqs, x2s=x2s: e.tensor_add(
                                                   bigf[:, sqs:sqs + 1024],
                                                   bigf[:, x2s:x2s + 1024],
                                                   bigf[:, sqs:sqs + 1024]),
                  r=[("bigf", sqs, sqs + 1024), X2], w=[("bigf", sqs, sqs + 1024)])
            def f_out(e, lt=lt, sqs=sqs):
                return e.dma_start(out=dy[lt * 128:(lt + 1) * 128, :],
                                   in_=bigf[:, sqs:sqs + 1024])
            kb.op("sync", f_out, kind="dma", r=[("bigf", sqs, sqs + 1024)],
                  w=[("dy", lt, lt + 1)], sem="dout")

        emit_tail_a(0)
        for lt in range(NT):
            emit_tail_b(lt)
            if lt < 8:
                emit_mix_late(lt + 8)
            if lt + 2 < NT:
                emit_mix(lt + 2)
            if lt + 1 < NT:
                emit_tail_a(lt + 1)

        counters = kb.resolve()
        with nc.Block() as block:
            kb.emit(block, sems, final_waits=[("dout", counters.get("dout", 0))])

    return nc


_NC_CACHE = [None]


def kernel(**inputs):
    x = np.ascontiguousarray(inputs["L0_attn_inputs"], dtype=np.float32)
    W_in = np.ascontiguousarray(inputs["L0_attn_W_in"], dtype=np.float32)
    conv_w = np.ascontiguousarray(inputs["L0_attn_conv_w"], dtype=np.float32)
    conv_b = np.ascontiguousarray(inputs["L0_attn_conv_b"], dtype=np.float32)
    W_x = np.ascontiguousarray(inputs["L0_attn_W_x"], dtype=np.float32)
    W_dt = np.ascontiguousarray(inputs["L0_attn_W_dt"], dtype=np.float32)
    b_dt = np.ascontiguousarray(inputs["L0_attn_b_dt"], dtype=np.float32)
    A_log = np.ascontiguousarray(inputs["L0_attn_A_log"], dtype=np.float32)
    W_out = np.ascontiguousarray(inputs["L0_attn_W_out"], dtype=np.float32)
    ffn_W = np.ascontiguousarray(inputs["L0_ffn_W"], dtype=np.float32)
    ffn_b = np.ascontiguousarray(inputs["L0_ffn_b"], dtype=np.float32)
    logits = np.ascontiguousarray(inputs["L0_ffn_logits"], dtype=np.float32)
    u = np.ascontiguousarray(inputs["L0_ffn_u"], dtype=np.float32)

    W_in2 = np.ascontiguousarray(
        W_in.reshape(8, 128, 32, 128).transpose(1, 2, 0, 3))
    W_dt2 = np.ascontiguousarray(
        W_dt.reshape(8, 128, 16, 128).transpose(1, 2, 0, 3))
    conv_wt = np.ascontiguousarray(
        conv_w.reshape(4, 16, 128).transpose(2, 1, 0).reshape(128, 64))
    conv_b2 = np.ascontiguousarray(conv_b.reshape(16, 128).T)
    b_dt2 = np.ascontiguousarray(b_dt.reshape(16, 128).T)
    a_log2 = np.ascontiguousarray(
        A_log.reshape(16, 128, 16).transpose(1, 0, 2).reshape(128, 256))
    logits1 = np.ascontiguousarray(logits.reshape(1, 8))
    u1 = np.ascontiguousarray(u.reshape(1, 8))
    ident = np.eye(128, dtype=np.float32)
    sel16 = np.zeros((16, 2048), dtype=np.float32)
    for s in range(16):
        sel16[s, s * 128:(s + 1) * 128] = 1.0

    if _NC_CACHE[0] is None:
        _NC_CACHE[0] = build_nc()
    nc = _NC_CACHE[0]

    shared = {"w_in": W_in2, "w_x": W_x, "w_dt": W_dt2, "w_out": W_out,
              "ffn_w": ffn_W, "ffn_b": ffn_b, "conv_wt": conv_wt,
              "conv_b2": conv_b2, "b_dt2": b_dt2, "a_log2": a_log2,
              "logits1": logits1, "u1": u1, "ident": ident, "sel16": sel16}
    in_maps = [dict(shared, x=np.ascontiguousarray(x[b])) for b in range(NCORES)]
    res = run_bass_kernel_spmd(nc, in_maps, core_ids=list(range(NCORES)))
    return np.stack([np.asarray(res.results[b]["y"], dtype=np.float32)
                     for b in range(NCORES)], axis=0)


# version 20
# speedup vs baseline: 1.0178x; 1.0093x over previous
import numpy as np
from contextlib import ExitStack
import concourse.bass as bass
import concourse.mybir as mybir
from concourse.bass_utils import run_bass_kernel_spmd

F32 = mybir.dt.float32
BF16 = mybir.dt.bfloat16
AF = mybir.ActivationFunctionType
ALU = mybir.AluOpType
AX = mybir.AxisListType

L = 2048
D = 1024
DI = 2048
NS = 16
NT = 16
NCORES = 8
EPS = 1e-5
DMA_INC = 16


class KB:

    def __init__(self):
        self.plan = []
        self.bufs = {}

    def _norm(self, spec):
        if isinstance(spec, str):
            return (spec, 0, 1 << 30)
        return spec

    PSUM_KEYS = frozenset(["psA", "psB", "psC", "psD", "psE", "psV", "psT", "psU"])

    def op(self, eng, fn, r=(), w=(), kind="compute", sem=None):
        deps = set()
        r = [self._norm(s) for s in r]
        w = [self._norm(s) for s in w]
        w = w + [s for s in r if s[0] in self.PSUM_KEYS]
        r = [s for s in r if s[0] not in self.PSUM_KEYS]
        for (name, lo, hi) in r:
            for ent in self.bufs.get(name, ()):
                if ent[0] < hi and lo < ent[1] and ent[2] is not None:
                    deps.add(ent[2])
        for (name, lo, hi) in w:
            for ent in self.bufs.get(name, ()):
                if ent[0] < hi and lo < ent[1]:
                    if ent[2] is not None:
                        deps.add(ent[2])
                    deps.update(ent[3])
        i = len(self.plan)
        self.plan.append(dict(eng=eng, fn=fn, deps=sorted(deps), kind=kind,
                              sem=sem, marked=False, token=None, waits=None))
        for (name, lo, hi) in r:
            lst = self.bufs.setdefault(name, [])
            for ent in lst:
                if ent[0] < hi and lo < ent[1]:
                    ent[3].add(i)
        for (name, lo, hi) in w:
            lst = self.bufs.setdefault(name, [])
            lst[:] = [ent for ent in lst if not (lo <= ent[0] and ent[1] <= hi)]
            lst.append([lo, hi, i, set()])
        return i

    def _queue(self, o):
        if o["sem"] is not None:
            return o["sem"]
        return "dma_s" if o["eng"] == "sync" else "dma_g"

    def resolve(self):
        POOLS = {"dma_s": ["dsa", "dsb", "dsc", "dsd"],
                 "dma_g": ["dga", "dgb", "dgc"],
                 "dout": ["dout"]}
        semcum = {}
        qcur = {q: None for q in POOLS}
        qopen = {q: [] for q in POOLS}
        qsi = {q: -1 for q in POOLS}
        group_close = {}
        needed = [[] for _ in self.plan]
        prewaits = [[] for _ in self.plan]

        def close_group(q):
            sem = qcur[q]
            if sem is None:
                return
            v = semcum.get(sem, 0)
            for j in qopen[q]:
                group_close[j] = (sem, v)
            qopen[q] = []
            qcur[q] = None

        for i, o in enumerate(self.plan):
            for d in o["deps"]:
                p = self.plan[d]
                if p["kind"] == "compute":
                    p["marked"] = True
                    needed[i].append(d)
                else:
                    q = self._queue(p)
                    if d in qopen[q]:
                        close_group(q)
                    needed[i].append(d)
            if o["kind"] == "dma":
                q = self._queue(o)
                if qcur[q] is None:
                    qsi[q] += 1
                    sem = POOLS[q][qsi[q] % len(POOLS[q])]
                    if semcum.get(sem, 0) > 0:
                        prewaits[i].append((sem, semcum[sem]))
                    qcur[q] = sem
                sem = qcur[q]
                semcum[sem] = semcum.get(sem, 0) + DMA_INC
                o["marked"] = True
                o["token"] = (sem, semcum[sem], DMA_INC)
                qopen[q].append(i)
        for q in POOLS:
            close_group(q)

        counters = dict(semcum)
        for o in self.plan:
            if o["marked"] and o["kind"] == "compute":
                s = o["eng"]
                counters[s] = counters.get(s, 0) + 1
                o["token"] = (s, counters[s], 1)

        marks = {}
        for i, o in enumerate(self.plan):
            wl = {}
            for (s, v) in prewaits[i]:
                if wl.get(s, 0) < v:
                    wl[s] = v
            for d in needed[i]:
                p = self.plan[d]
                if p["kind"] == "dma":
                    s, v = group_close[d]
                else:
                    s, v, _ = p["token"]
                if wl.get(s, 0) < v:
                    wl[s] = v
            final = []
            for s, v in wl.items():
                key = (o["eng"], s)
                if marks.get(key, 0) < v:
                    marks[key] = v
                    final.append((s, v))
            o["waits"] = final
        return counters

    def emit(self, block, sems, final_waits):
        for name in ["sync", "gpsimd", "tensor", "vector", "scalar"]:
            ops = [o for o in self.plan if o["eng"] == name]

            def make(eng_ops, engname):
                def body(eng):
                    for o in eng_ops:
                        for (s, v) in o["waits"]:
                            eng.wait_ge(sems[s], v)
                        inst = o["fn"](eng)
                        if o["marked"]:
                            s, v, inc = o["token"]
                            inst.then_inc(sems[s], inc)
                    if engname == "sync":
                        for (s, v) in final_waits:
                            eng.wait_ge(sems[s], v)
                return body

            getattr(block, name)(make(ops, name))


_PHASES = [7]


def _fin(nc, kb, sems, es, dy, bigf, T0):
    for lt in range(NT):
        def f_out(e, lt=lt):
            return e.dma_start(out=dy[lt * 128:(lt + 1) * 128, :],
                               in_=bigf[:, T0:T0 + 1024])
        kb.op("sync", f_out, kind="dma", r=[("bigf", T0, T0 + 1024)],
              w=[("dy", lt, lt + 1)], sem="dout")
    counters = kb.resolve()
    with nc.Block() as block:
        kb.emit(block, sems, final_waits=[("dout", counters.get("dout", 0))])
    es.close()
    return nc



def build_nc():
    nc = bass.Bass("TRN2", target_bir_lowering=False, debug=False)

    dx = nc.dram_tensor("x", [L, D], F32, kind="ExternalInput").ap()
    dw_in = nc.dram_tensor("w_in", [128, 32, 8, 128], F32, kind="ExternalInput").ap()
    dw_x = nc.dram_tensor("w_x", [DI, D + 2 * NS], F32, kind="ExternalInput").ap()
    dw_dt = nc.dram_tensor("w_dt", [128, 16, 8, 128], F32, kind="ExternalInput").ap()
    dw_out = nc.dram_tensor("w_out", [DI, D], F32, kind="ExternalInput").ap()
    dffn_w = nc.dram_tensor("ffn_w", [8, D, D], F32, kind="ExternalInput").ap()
    dffn_b = nc.dram_tensor("ffn_b", [8, D], F32, kind="ExternalInput").ap()
    dconv_wt = nc.dram_tensor("conv_wt", [128, 64], F32, kind="ExternalInput").ap()
    dconv_b2 = nc.dram_tensor("conv_b2", [128, 16], F32, kind="ExternalInput").ap()
    db_dt2 = nc.dram_tensor("b_dt2", [128, 16], F32, kind="ExternalInput").ap()
    da_log2 = nc.dram_tensor("a_log2", [128, 256], F32, kind="ExternalInput").ap()
    dlogits = nc.dram_tensor("logits1", [1, 8], F32, kind="ExternalInput").ap()
    du1 = nc.dram_tensor("u1", [1, 8], F32, kind="ExternalInput").ap()
    dident = nc.dram_tensor("ident", [128, 128], F32, kind="ExternalInput").ap()
    dsel = nc.dram_tensor("sel16", [NS, 2048], F32, kind="ExternalInput").ap()
    dy = nc.dram_tensor("y", [L, D], F32, kind="ExternalOutput").ap()

    d_xconv = nc.dram_tensor("d_xconv", [NT, 128, L], BF16, kind="Internal").ap()
    d_sz = nc.dram_tensor("d_sz", [NT, 128, L], BF16, kind="Internal").ap()
    d_yg = nc.dram_tensor("d_yg", [NT, 128, L], BF16, kind="Internal").ap()
    d_weff = nc.dram_tensor("d_weff", [128, 8 * D], BF16, kind="Internal").ap()
    d_mix = nc.dram_tensor("d_mix", [16, 128, D], BF16, kind="Internal").ap()
    d_mask = nc.dram_tensor("d_mask", [1, 8], F32, kind="Internal").ap()

    kb = KB()

    with ExitStack() as es:
        sb = lambda n, shp, dt: es.enter_context(nc.sbuf_tensor(n, shp, dt))
        pt = lambda n, shp, dt: es.enter_context(nc.psum_tensor(n, shp, dt))

        arena = sb("arena", [128, 8 * L], BF16)
        big_a = sb("big_a", [128, 16 * L], BF16)
        big_b = sb("big_b", [128, 27392], BF16)
        T0, T1, T2, T3 = 0, 2052, 4132, 6180
        bigf = sb("bigf", [128, 8228], F32)
        bflat = big_b[0:1, 18432:20480]
        identb = big_b[:, 27136:27264]
        ones1 = big_b[0:1, 27264:27392]
        beffrow = big_b[0:1, 20480:21504]
        sm = sb("sm", [128, 7 * 1024], BF16)
        pa = sb("pa", [128, 256], F32)
        wcol = sb("wcol", [128, 64], F32)
        cbcol = sb("cbcol", [128, 16], F32)
        bdtcol = sb("bdtcol", [128, 16], F32)
        ones8 = sb("ones8", [8, 1], BF16)
        lgu = sb("lgu", [1, 32], F32)
        maskr = sb("maskr", [1, 8], F32)
        maskrb = sb("maskrb", [1, 8], BF16)
        maskt = sb("maskt", [8, 1], F32)
        maskrep = sb("maskrep", [128, 8], F32)
        cpool = sb("cpool", [128, 8], F32)
        hcol = sb("hcol", [128, 256], F32)
        convh = sb("convh", [128, 2051], F32)
        tails = sb("tails", [128, 48], F32)

        psA = pt("psA", [128, 512], F32)
        psB = pt("psB", [128, 512], F32)
        psC = pt("psC", [128, 512], F32)
        psD = pt("psD", [128, 512], F32)
        psE = pt("psE", [128, 512], F32)
        psV = pt("psV", [128, 512], F32)
        psT = pt("psT", [128, 1024], BF16)
        psU = pt("psU", [128, 1024], BF16)

        sems = {n: es.enter_context(nc.semaphore(n)) for n in
                ["dsa", "dsb", "dsc", "dsd", "dga", "dgb", "dgc",
                 "tensor", "vector", "scalar", "gpsimd", "dout"]}

        kb.op("vector", lambda e: e.memset(cpool[:, 0:1], EPS), w=["cpool"])
        kb.op("vector", lambda e: e.memset(cpool[:, 1:2], 1.0), w=["cpool"])
        kb.op("sync", lambda e: e.dma_start(out=pa[:, 0:256], in_=da_log2), kind="dma",
              w=[("pa", 0, 256)])
        kb.op("scalar", lambda e: e.activation(pa[:, 0:256], pa[:, 0:256], AF.Exp),
              r=[("pa", 0, 256)], w=[("pa", 0, 256)])
        kb.op("vector", lambda e: e.tensor_scalar(pa[:, 0:256], pa[:, 0:256], -1.0,
                                                  None, ALU.mult),
              r=[("pa", 0, 256)], w=[("pa", 0, 256)])
        kb.op("sync", lambda e: e.dma_start(out=wcol[:], in_=dconv_wt),
              kind="dma", w=["wcol"])
        kb.op("sync", lambda e: e.dma_start(out=cbcol[:], in_=dconv_b2), kind="dma", w=["cbcol"])
        kb.op("sync", lambda e: e.dma_start(out=bdtcol[:], in_=db_dt2), kind="dma", w=["bdtcol"])
        kb.op("gpsimd", lambda e: e.dma_start(out=identb, in_=dident), kind="dma", w=[("big_b", 27136, 27264)])
        kb.op("vector", lambda e: e.memset(ones1, 1.0), w=[("big_b", 27264, 27392)])
        kb.op("vector", lambda e: e.memset(ones8[:], 1.0), w=["ones8"])
        kb.op("vector", lambda e: e.memset(convh[:, 0:3], 0.0), w=[("convh", 0, 3)])

        kb.op("sync", lambda e: e.dma_start(out=lgu[:, 0:8], in_=dlogits), kind="dma",
              w=[("lgu", 0, 8)])
        kb.op("sync", lambda e: e.dma_start(out=lgu[:, 8:16], in_=du1), kind="dma",
              w=[("lgu", 8, 16)])
        kb.op("scalar", lambda e: e.activation(lgu[:, 16:24], lgu[:, 8:16], AF.Ln),
              r=[("lgu", 8, 16)], w=[("lgu", 16, 24)])
        kb.op("vector", lambda e: e.tensor_scalar(lgu[:, 16:24], lgu[:, 16:24], -1.0, None,
                                                  ALU.mult),
              r=[("lgu", 16, 24)], w=[("lgu", 16, 24)])
        kb.op("scalar", lambda e: e.activation(lgu[:, 16:24], lgu[:, 16:24], AF.Ln),
              r=[("lgu", 16, 24)], w=[("lgu", 16, 24)])
        kb.op("vector", lambda e: e.tensor_sub(lgu[:, 24:32], lgu[:, 0:8], lgu[:, 16:24]),
              r=[("lgu", 0, 24)], w=[("lgu", 24, 32)])
        kb.op("vector", lambda e: e.reduce_max(lgu[:, 16:17], lgu[:, 24:32], axis=AX.X),
              r=[("lgu", 24, 32)], w=[("lgu", 16, 17)])
        kb.op("vector", lambda e: e.tensor_scalar(maskr[:], lgu[:, 24:32], lgu[:, 16:17],
                                                  None, ALU.is_ge),
              r=[("lgu", 16, 32)], w=["maskr"])
        kb.op("vector", lambda e: e.tensor_copy(maskrb[:], maskr[:]), r=["maskr"], w=["maskrb"])
        kb.op("sync", lambda e: e.dma_start(out=d_mask, in_=maskr[:]), kind="dma",
              r=["maskr"], w=["d_mask"])
        kb.op("sync", lambda e: e.dma_start(out=maskt[:], in_=d_mask.rearrange("1 k -> k 1")),
              kind="dma", r=["d_mask"], w=["maskt"])
        kb.op("tensor", lambda e: e.matmul(psE[:, 0:8], ones1, maskrb[:], start=True,
                                           stop=True),
              r=[("big_b", 27264, 27392), "maskrb"], w=["psE"])
        kb.op("vector", lambda e: e.tensor_copy(maskrep[:], psE[:, 0:8]), r=["psE"],
              w=["maskrep"])

        MD0 = 24064
        for k in range(8):
            def f_md(e, k=k):
                return e.tensor_scalar(big_b[:, MD0 + k * 128:MD0 + (k + 1) * 128],
                                       identb, maskrep[:, k:k + 1], None, ALU.mult)
            kb.op("vector", f_md,
                  r=[("big_b", 27136, 27264), "maskrep"],
                  w=[("big_b", MD0 + k * 128, MD0 + (k + 1) * 128)])
        weff_steps = []

        def _weff_step(dt_, k):
            def go():
                lane = 4 + k % 2
                def f_load(e, k=k, dt_=dt_, lane=lane):
                    return e.dma_start(out=sm[:, lane * 1024:(lane + 1) * 1024],
                                       in_=dffn_w[k, dt_ * 128:(dt_ + 1) * 128, :])
                kb.op("gpsimd", f_load, kind="dma",
                      w=[("sm", lane * 1024, (lane + 1) * 1024)])
                for n2 in range(2):
                    ps, pk = (psC, "psC") if n2 == 0 else (psD, "psD")
                    def f_mm(e, k=k, n2=n2, ps=ps, lane=lane):
                        return e.matmul(ps[:, 0:512],
                                        big_b[:, MD0 + k * 128:MD0 + (k + 1) * 128],
                                        sm[:, lane * 1024 + n2 * 512:lane * 1024 + n2 * 512 + 512],
                                        start=(k == 0), stop=(k == 7))
                    kb.op("tensor", f_mm,
                          r=[("big_b", MD0 + k * 128, MD0 + (k + 1) * 128),
                             ("sm", lane * 1024, (lane + 1) * 1024)], w=[pk])
                if k == 7:
                    for n2 in range(2):
                        ps, pk = (psC, "psC") if n2 == 0 else (psD, "psD")
                        def f_ev(e, dt_=dt_, n2=n2, ps=ps):
                            return e.copy(
                                sm[:, 6144 + n2 * 512:6144 + n2 * 512 + 512],
                                ps[:, 0:512])
                        kb.op("scalar", f_ev, r=[pk],
                              w=[("sm", 6144 + n2 * 512, 6144 + n2 * 512 + 512)])
                    def f_sp(e, dt_=dt_):
                        return e.dma_start(
                            out=d_weff[:, dt_ * 1024:(dt_ + 1) * 1024],
                            in_=sm[:, 6144:6144 + 1024])
                    kb.op("sync", f_sp, kind="dma", r=[("sm", 6144, 7168)],
                          w=[("d_weff", dt_ * 1024, (dt_ + 1) * 1024)])
            return go

        for dt_ in range(8):
            for k in range(8):
                weff_steps.append(_weff_step(dt_, k))

        pscnt = [0]

        def next_ps():
            pscnt[0] += 1
            return [(psA, "psA"), (psB, "psB"), (psE, "psE")][pscnt[0] % 3]

        LN1T = ("arena", 0, 8192)

        def _p1_bufs(h, j):
            if h == 0:
                p = j % 2
                xt_ap = bigf[:, p * 1024:(p + 1) * 1024]
                XT = ("bigf", p * 1024, (p + 1) * 1024)
                sq_ap = bigf[:, 2048 + p * 1024:2048 + (p + 1) * 1024]
                SQ = ("bigf", 2048 + p * 1024, 2048 + (p + 1) * 1024)
                stg_ap = sm[:, 2048 + p * 1024:2048 + (p + 1) * 1024]
                STG = ("sm", 2048 + p * 1024, 2048 + (p + 1) * 1024)
                sb0 = 104 + p * 10
            else:
                xt_ap = convh[:, 0:1024]
                XT = ("convh", 0, 1024)
                sq_ap = convh[:, 1024:2048]
                SQ = ("convh", 1024, 2048)
                stg_ap = big_b[:, 24064:25088]
                STG = ("big_b", 24064, 25088)
                sb0 = 104 + (j % 2) * 10
            return xt_ap, XT, sq_ap, SQ, stg_ap, STG, sb0

        def emit_p1_ldx(h, j):
            lt = h * 8 + j
            xt_ap, XT = _p1_bufs(h, j)[0:2]
            def f_ldx(e, lt=lt, xt_ap=xt_ap):
                return e.dma_start(out=xt_ap, in_=dx[lt * 128:(lt + 1) * 128, :])
            kb.op("sync", f_ldx, kind="dma", w=[XT])

        def emit_p1_lt(h, j, do_load=True):
            lt = h * 8 + j
            xt_ap, XT, sq_ap, SQ, stg_ap, STG, sb0 = _p1_bufs(h, j)
            if do_load:
                emit_p1_ldx(h, j)
            kb.op("vector", lambda e: e.reduce_sum(statn[:, sb0:sb0 + 1], xt_ap,
                                                   axis=AX.X), r=[XT], w=[("statn", sb0, sb0 + 1)])
            kb.op("scalar", lambda e: e.activation(sq_ap, xt_ap,
                                                   AF.Square,
                                                   accum_out=statn[:, sb0 + 1:sb0 + 2]),
                  r=[XT], w=[SQ, ("statn", sb0 + 1, sb0 + 2)])
            kb.op("vector", lambda e: e.tensor_scalar(statn[:, sb0 + 2:sb0 + 3], statn[:, sb0:sb0 + 1],
                                                      1.0 / D, None, ALU.mult),
                  r=[("statn", sb0, sb0 + 1)], w=[("statn", sb0 + 2, sb0 + 3)])
            kb.op("vector", lambda e: e.tensor_scalar(statn[:, sb0 + 3:sb0 + 4], statn[:, sb0 + 1:sb0 + 2],
                                                      1.0 / D, None, ALU.mult),
                  r=[("statn", sb0 + 1, sb0 + 2)], w=[("statn", sb0 + 3, sb0 + 4)])
            kb.op("vector", lambda e: e.tensor_mul(statn[:, sb0 + 4:sb0 + 5], statn[:, sb0 + 2:sb0 + 3],
                                                   statn[:, sb0 + 2:sb0 + 3]),
                  r=[("statn", sb0 + 2, sb0 + 3)], w=[("statn", sb0 + 4, sb0 + 5)])
            kb.op("vector", lambda e: e.tensor_sub(statn[:, sb0 + 5:sb0 + 6], statn[:, sb0 + 3:sb0 + 4],
                                                   statn[:, sb0 + 4:sb0 + 5]),
                  r=[("statn", sb0 + 3, sb0 + 5)], w=[("statn", sb0 + 5, sb0 + 6)])
            kb.op("scalar", lambda e: e.activation(statn[:, sb0 + 6:sb0 + 7], statn[:, sb0 + 5:sb0 + 6],
                                                   AF.Ln, bias=cpool[:, 0:1]),
                  r=[("statn", sb0 + 5, sb0 + 6), "cpool"], w=[("statn", sb0 + 6, sb0 + 7)])
            kb.op("scalar", lambda e: e.activation(statn[:, sb0 + 7:sb0 + 8], statn[:, sb0 + 6:sb0 + 7],
                                                   AF.Exp, scale=-0.5),
                  r=[("statn", sb0 + 6, sb0 + 7)], w=[("statn", sb0 + 7, sb0 + 8)])
            kb.op("vector", lambda e: e.tensor_scalar(statn[:, sb0 + 8:sb0 + 9],
                                                      statn[:, sb0 + 2:sb0 + 3],
                                                      statn[:, sb0 + 7:sb0 + 8], -1.0,
                                                      ALU.mult, ALU.mult),
                  r=[("statn", sb0 + 2, sb0 + 3), ("statn", sb0 + 7, sb0 + 8)],
                  w=[("statn", sb0 + 8, sb0 + 9)])
            def f_norm(e, stg_ap=stg_ap, xt_ap=xt_ap):
                return e.activation(stg_ap, xt_ap, AF.Identity,
                                    bias=statn[:, sb0 + 8:sb0 + 9],
                                    scale=statn[:, sb0 + 7:sb0 + 8])
            kb.op("scalar", f_norm, r=[XT, ("statn", sb0 + 7, sb0 + 9)], w=[STG])
            for dt_ in range(8):
                bank, bk = (psT, "psT") if dt_ % 2 == 0 else (psU, "psU")
                def f_tr(e, dt_=dt_, bank=bank, stg_ap=stg_ap):
                    return e.transpose(bank[:, 0:128],
                                       stg_ap[:, dt_ * 128:dt_ * 128 + 128], identb)
                kb.op("tensor", f_tr, r=[STG, ("big_b", 27136, 27264)], w=[bk])
                def f_cp(e, j=j, dt_=dt_, bank=bank):
                    return e.copy(
                        arena[:, dt_ * 1024 + j * 128:dt_ * 1024 + j * 128 + 128],
                        bank[:, 0:128])
                kb.op("scalar", f_cp, r=[bk],
                      w=[("arena", dt_ * 1024 + j * 128, dt_ * 1024 + j * 128 + 128)])

        WINB = 16896

        def emit_p2_ct(h, cti, ct_order):
            ct = ct_order[cti]
            if h == 0:
                nring, rbase = 4, WINB
            else:
                nring, rbase = 2, 25088
            lane = cti % nring
            WI = ("big_b", rbase + lane * 1024, rbase + (lane + 1) * 1024)
            if cti < nring:
                def f_ldw0(e, ct=ct, lane=lane, rbase=rbase):
                    return e.dma_start(
                        out=big_b[:, rbase + lane * 1024:rbase + lane * 1024 + 1024],
                        in_=dw_in[:, ct].rearrange("p dt c -> p (dt c)"))
                kb.op("gpsimd", f_ldw0, kind="dma", w=[WI])
            for lc2 in range(2):
                ps, pk = next_ps()
                for dt_ in range(8):
                    def f_mm(e, lc2=lc2, dt_=dt_, ps=ps, lane=lane, rbase=rbase):
                        return e.matmul(
                            ps[:, 0:512],
                            big_b[:, rbase + lane * 1024 + dt_ * 128:rbase + lane * 1024 + dt_ * 128 + 128],
                            arena[:, dt_ * 1024 + lc2 * 512:dt_ * 1024 + lc2 * 512 + 512],
                            start=(dt_ == 0), stop=(dt_ == 7))
                    kb.op("tensor", f_mm,
                          r=[WI, ("arena", dt_ * 1024 + lc2 * 512,
                                  dt_ * 1024 + lc2 * 512 + 512)], w=[pk])
                if ct < 16:
                    def f_ev(e, lc2=lc2, ps=ps):
                        return e.copy(convh[:, 3 + lc2 * 512:3 + lc2 * 512 + 512],
                                      ps[:, 0:512])
                    kb.op("scalar", f_ev, r=[pk],
                          w=[("convh", 3 + lc2 * 512, 3 + lc2 * 512 + 512)])
                else:
                    if h == 0:
                        zstg, ZS = sm[:, 2048 + lc2 * 512:2048 + lc2 * 512 + 512], \
                            ("sm", 2048 + lc2 * 512, 2048 + lc2 * 512 + 512)
                    else:
                        zstg, ZS = big_b[:, 24064 + lc2 * 512:24064 + lc2 * 512 + 512], \
                            ("big_b", 24064 + lc2 * 512, 24064 + lc2 * 512 + 512)
                    def f_silu(e, ps=ps, zstg=zstg):
                        return e.activation(zstg, ps[:, 0:512], AF.Silu)
                    kb.op("scalar", f_silu, r=[pk], w=[ZS])
            if ct < 16:
                if h == 1:
                    def f_pad(e, ct=ct):
                        return e.tensor_copy(convh[:, 0:3], tails[:, ct * 3:ct * 3 + 3])
                    kb.op("vector", f_pad, r=[("tails", ct * 3, ct * 3 + 3)],
                          w=[("convh", 0, 3)])
                else:
                    def f_tail(e, ct=ct):
                        return e.tensor_copy(tails[:, ct * 3:ct * 3 + 3], convh[:, 1024:1027])
                    kb.op("vector", f_tail, r=[("convh", 1024, 1027)],
                          w=[("tails", ct * 3, ct * 3 + 3)])
                XP = ("convh", 0, 1027)
                def f_c0(e, ct=ct):
                    return e.tensor_scalar(convh[:, 1027:1027 + 1024], convh[:, 0:1024],
                                           wcol[:, ct * 4:ct * 4 + 1], None, ALU.mult)
                kb.op("vector", f_c0, r=[XP, "wcol"], w=[("convh", 1027, 2051)])
                for k in range(1, 4):
                    def f_ck(e, ct=ct, k=k):
                        return e.scalar_tensor_tensor(
                            convh[:, 1027:1027 + 1024], convh[:, k:k + 1024],
                            wcol[:, ct * 4 + k:ct * 4 + k + 1],
                            convh[:, 1027:1027 + 1024], ALU.mult, ALU.add)
                    kb.op("vector", f_ck, r=[XP, "wcol", ("convh", 1027, 2051)],
                          w=[("convh", 1027, 2051)])
                if h == 0:
                    def f_si(e, ct=ct):
                        return e.activation(big_a[:, ct * 1024:(ct + 1) * 1024],
                                            convh[:, 1027:1027 + 1024],
                                            AF.Silu, bias=cbcol[:, ct:ct + 1])
                    kb.op("scalar", f_si, r=[("convh", 1027, 2051), "cbcol"],
                          w=[("big_a", ct * 1024, (ct + 1) * 1024)])
                    def f_sp(e, ct=ct):
                        return e.dma_start(out=d_xconv[ct][:, 0:1024],
                                           in_=big_a[:, ct * 1024:(ct + 1) * 1024])
                    kb.op("sync", f_sp, kind="dma", r=[("big_a", ct * 1024, (ct + 1) * 1024)],
                          w=[("d_xconv", ct * 2, ct * 2 + 1)])
                else:
                    def f_si(e, ct=ct):
                        return e.activation(big_b[:, 24064:25088],
                                            convh[:, 1027:1027 + 1024],
                                            AF.Silu, bias=cbcol[:, ct:ct + 1])
                    kb.op("scalar", f_si, r=[("convh", 1027, 2051), "cbcol"],
                          w=[("big_b", 24064, 25088)])
                    def f_sp(e, ct=ct):
                        return e.dma_start(out=d_xconv[ct][:, 1024:2048],
                                           in_=big_b[:, 24064:25088])
                    kb.op("sync", f_sp, kind="dma", r=[("big_b", 24064, 25088)],
                          w=[("d_xconv", ct * 2 + 1, ct * 2 + 2)])
            else:
                if h == 0:
                    def f_spz(e, ct=ct):
                        return e.dma_start(out=d_sz[ct - 16][:, 0:1024],
                                           in_=sm[:, 2048:3072])
                    kb.op("sync", f_spz, kind="dma", r=[("sm", 2048, 3072)],
                          w=[("d_sz", (ct - 16) * 2, (ct - 16) * 2 + 1)])
                else:
                    def f_spz(e, ct=ct):
                        return e.dma_start(out=d_sz[ct - 16][:, 1024:2048],
                                           in_=big_b[:, 24064:25088])
                    kb.op("sync", f_spz, kind="dma", r=[("big_b", 24064, 25088)],
                          w=[("d_sz", (ct - 16) * 2 + 1, (ct - 16) * 2 + 2)])
            if cti + nring - 1 < 32 and cti + nring - 1 >= nring:
                nct = ct_order[cti + nring - 1]
                sl = (cti + nring - 1) % nring
                def f_ldw(e, nct=nct, sl=sl, rbase=rbase):
                    return e.dma_start(
                        out=big_b[:, rbase + sl * 1024:rbase + sl * 1024 + 1024],
                        in_=dw_in[:, nct].rearrange("p dt c -> p (dt c)"))
                kb.op("gpsimd", f_ldw, kind="dma",
                      w=[("big_b", rbase + sl * 1024, rbase + sl * 1024 + 1024)])

        WX_ALL = ("big_b", 0, 16 * 1056)

        REP_STG = {0: 24064, 1: 23040}
        SEL0 = 18432

        repc = [0]

        def rep_unit(which, s):
            bb = 0 if which == 0 else 16384
            for lc2 in range(2):
                repc[0] += 1
                ps, pk = (psC, "psC") if repc[0] % 2 == 0 else (psD, "psD")
                def f_mm(e, which=which, s=s, lc2=lc2, ps=ps):
                    return e.matmul(
                        ps[:, 0:512],
                        big_b[0:16, SEL0 + s * 128:SEL0 + (s + 1) * 128],
                        big_b[0:16, REP_STG[which] + lc2 * 512:REP_STG[which] + lc2 * 512 + 512],
                        start=True, stop=True)
                kb.op("tensor", f_mm,
                      r=[("big_b", SEL0 + s * 128, SEL0 + (s + 1) * 128),
                         ("big_b", REP_STG[which] + lc2 * 512,
                          REP_STG[which] + lc2 * 512 + 512)],
                      w=[pk])
                def f_ev(e, bb=bb, s=s, lc2=lc2, ps=ps):
                    return e.copy(
                        big_a[:, bb + s * HF + lc2 * 512:bb + s * HF + lc2 * 512 + 512],
                        ps[:, 0:512])
                kb.op("scalar", f_ev, r=[pk],
                      w=[("big_a", bb + s * HF + lc2 * 512,
                          bb + s * HF + lc2 * 512 + 512)])

        def emit_p3a_half(h):
            if h == 1:
                for ct in range(16):
                    def f_rx(e, ct=ct):
                        return e.dma_start(out=big_a[:, ct * 1024:(ct + 1) * 1024],
                                           in_=d_xconv[ct][:, 1024:2048])
                    kb.op("sync" if ct % 2 == 0 else "gpsimd", f_rx, kind="dma",
                          r=[("d_xconv", ct * 2 + 1, ct * 2 + 2)],
                          w=[("big_a", ct * 1024, (ct + 1) * 1024)])
            kb.op("gpsimd", lambda e: e.dma_start(out=big_b[0:16, SEL0:SEL0 + 2048],
                                                  in_=dsel), kind="dma",
                  w=[("big_b", SEL0, SEL0 + 2048)])
            for which in (1, 0):
                col0 = 1024 if which == 0 else 1040
                ps, pk = (psD, "psD") if which == 0 else (psC, "psC")
                for lc2 in range(2):
                    for ct in range(16):
                        def f_mm(e, col0=col0, lc2=lc2, ct=ct, ps=ps):
                            return e.matmul(
                                ps[0:16, 0:512],
                                big_b[:, ct * 1056 + col0:ct * 1056 + col0 + 16],
                                big_a[:, ct * 1024 + lc2 * 512:ct * 1024 + lc2 * 512 + 512],
                                start=(ct == 0), stop=(ct == 15))
                        kb.op("tensor", f_mm,
                              r=[WX_ALL, ("big_a", ct * 1024 + lc2 * 512,
                                          ct * 1024 + lc2 * 512 + 512)], w=[pk])
                    def f_st(e, ps=ps, which=which, lc2=lc2):
                        return e.copy(
                            big_b[0:16, REP_STG[which] + lc2 * 512:REP_STG[which] + lc2 * 512 + 512],
                            ps[0:16, 0:512])
                    kb.op("scalar", f_st, r=[pk],
                          w=[("big_b", REP_STG[which] + lc2 * 512,
                              REP_STG[which] + lc2 * 512 + 512)])
            emit_p3b_loads(h, 0)
            base = 8192 if h == 0 else 0
            for dp in range(8):
                for lc2 in range(2):
                    ps, pk = next_ps()
                    for ct in range(16):
                        def f_mm(e, dp=dp, lc2=lc2, ct=ct, ps=ps):
                            return e.matmul(
                                ps[:, 0:512],
                                big_b[:, ct * 1056 + dp * 128:ct * 1056 + dp * 128 + 128],
                                big_a[:, ct * 1024 + lc2 * 512:ct * 1024 + lc2 * 512 + 512],
                                start=(ct == 0), stop=(ct == 15))
                        kb.op("tensor", f_mm,
                              r=[WX_ALL, ("big_a", ct * 1024 + lc2 * 512,
                                          ct * 1024 + lc2 * 512 + 512)], w=[pk])
                    def f_ev(e, dp=dp, lc2=lc2, ps=ps, base=base):
                        return e.copy(
                            arena[:, base + dp * 1024 + lc2 * 512:base + dp * 1024 + lc2 * 512 + 512],
                            ps[:, 0:512])
                    kb.op("scalar", f_ev, r=[pk],
                          w=[("arena", base + dp * 1024 + lc2 * 512,
                              base + dp * 1024 + lc2 * 512 + 512)])
            emit_p3b(h, 0, do_loads=False)
            for s in (0, 1):
                rep_unit(0, s)
                rep_unit(1, s)

        HLn, L0n, L1n, L2n = 16896, 17920, 18944, 19968
        WDTn, YGn = 20992, 23040
        HK = ("big_b", HLn, HLn + 1024)
        LK = [("big_b", c, c + 1024) for c in (L0n, L1n, L2n)]
        HF = 1024


        def emit_p3b_loads(half, it):
            par = it % 2
            WDL = ("big_b", WDTn + par * 1024, WDTn + (par + 1) * 1024)
            def f_lw(e, it=it, par=par):
                return e.dma_start(
                    out=big_b[:, WDTn + par * 1024:WDTn + (par + 1) * 1024],
                    in_=dw_dt[:, it].rearrange("p dt i -> p (dt i)"))
            kb.op("gpsimd", f_lw, kind="dma", w=[WDL])
            def f_lxc(e, it=it, half=half, par=par):
                return e.dma_start(out=sm[:, (2 + par) * 1024:(3 + par) * 1024],
                                   in_=d_xconv[it][:, half * HF:(half + 1) * HF])
            kb.op("sync", f_lxc, kind="dma", r=[("d_xconv", it * 2 + half, it * 2 + half + 1)],
                  w=[("sm", (2 + par) * 1024, (3 + par) * 1024)])
            def f_lsz(e, it=it, half=half, par=par):
                return e.dma_start(out=sm[:, (4 + par) * 1024:(5 + par) * 1024],
                                   in_=d_sz[it][:, half * HF:(half + 1) * HF])
            kb.op("sync", f_lsz, kind="dma", r=[("d_sz", it * 2 + half, it * 2 + half + 1)],
                  w=[("sm", (4 + par) * 1024, (5 + par) * 1024)])

        def emit_p3b(half, it, do_loads=True):
            par = it % 2
            WDL = ("big_b", WDTn + par * 1024, WDTn + (par + 1) * 1024)
            if do_loads:
                emit_p3b_loads(half, it)
            XCL = ("sm", (2 + par) * 1024, (3 + par) * 1024)
            SZL = ("sm", (4 + par) * 1024, (5 + par) * 1024)
            DTB = T3 + par * 1024
            DTK = ("bigf", DTB, DTB + 1024)
            for lc2 in range(2):
                ps, pk = next_ps()
                base_h = 8192 if half == 0 else 0
                for dp in range(8):
                    def f_mm(e, lc2=lc2, dp=dp, ps=ps, par=par, base_h=base_h):
                        return e.matmul(
                            ps[:, 0:512],
                            big_b[:, WDTn + par * 1024 + dp * 128:WDTn + par * 1024 + dp * 128 + 128],
                            arena[:, base_h + dp * 1024 + lc2 * 512:base_h + dp * 1024 + lc2 * 512 + 512],
                            start=(dp == 0), stop=(dp == 7))
                    kb.op("tensor", f_mm, r=[WDL, ("arena", base_h, base_h + 8192)], w=[pk])
                R_ = ("bigf", T2 + 1024 + lc2 * 512, T2 + 1024 + lc2 * 512 + 512)
                A_ = ("bigf", T0 + 1024 + lc2 * 512, T0 + 1024 + lc2 * 512 + 512)
                def f_r(e, it=it, lc2=lc2, ps=ps):
                    return e.activation(
                        bigf[:, T2 + 1024 + lc2 * 512:T2 + 1024 + lc2 * 512 + 512],
                        ps[:, 0:512], AF.Relu, bias=bdtcol[:, it:it + 1])
                kb.op("scalar", f_r, r=[pk, "bdtcol"], w=[R_])
                def f_a(e, it=it, lc2=lc2, ps=ps):
                    return e.activation(
                        bigf[:, T0 + 1024 + lc2 * 512:T0 + 1024 + lc2 * 512 + 512],
                        ps[:, 0:512], AF.Abs, bias=bdtcol[:, it:it + 1])
                kb.op("scalar", f_a, r=[pk, "bdtcol"], w=[A_])
                def f_e(e, lc2=lc2):
                    return e.activation(
                        bigf[:, T0 + 1024 + lc2 * 512:T0 + 1024 + lc2 * 512 + 512],
                        bigf[:, T0 + 1024 + lc2 * 512:T0 + 1024 + lc2 * 512 + 512],
                        AF.Exp, scale=-1.0)
                kb.op("scalar", f_e, r=[A_], w=[A_])
                def f_l(e, lc2=lc2):
                    return e.activation(
                        bigf[:, T0 + 1024 + lc2 * 512:T0 + 1024 + lc2 * 512 + 512],
                        bigf[:, T0 + 1024 + lc2 * 512:T0 + 1024 + lc2 * 512 + 512],
                        AF.Ln, bias=cpool[:, 1:2])
                kb.op("scalar", f_l, r=[A_, "cpool"], w=[A_])
                def f_d(e, lc2=lc2, DTB=DTB):
                    return e.tensor_add(bigf[:, DTB + lc2 * 512:DTB + lc2 * 512 + 512],
                                        bigf[:, T2 + 1024 + lc2 * 512:T2 + 1024 + lc2 * 512 + 512],
                                        bigf[:, T0 + 1024 + lc2 * 512:T0 + 1024 + lc2 * 512 + 512])
                kb.op("vector", f_d, r=[R_, A_],
                      w=[("bigf", DTB + lc2 * 512, DTB + lc2 * 512 + 512)])
            def f_dtx(e, DTB=DTB, par=par):
                return e.tensor_mul(sm[:, par * 1024:(par + 1) * 1024],
                                    bigf[:, DTB:DTB + 1024],
                                    sm[:, (2 + par) * 1024:(3 + par) * 1024])
            kb.op("vector", f_dtx, r=[DTK, XCL], w=[("sm", par * 1024, (par + 1) * 1024)])

        def emit_p4(half, it, lead_reps=False):
            par = it % 2
            DTB = T3 + par * 1024
            DTK = ("bigf", DTB, DTB + 1024)
            DTXL = ("sm", par * 1024, (par + 1) * 1024)
            XCL = ("sm", (2 + par) * 1024, (3 + par) * 1024)
            SZL = ("sm", (4 + par) * 1024, (5 + par) * 1024)
            YA = ("bigf", T2, T2 + 1024)
            for s in range(NS):
                sp = s % 2
                DA = ("bigf", T1 + sp * 1024, T1 + (sp + 1) * 1024)
                def f_da(e, it=it, s=s, sp=sp, DTB=DTB):
                    return e.activation(bigf[:, T1 + sp * 1024:T1 + (sp + 1) * 1024],
                                        bigf[:, DTB:DTB + 1024], AF.Exp,
                                        scale=pa[:, it * 16 + s:it * 16 + s + 1])
                kb.op("scalar", f_da, r=[DTK, ("pa", 0, 256)], w=[DA])
                DBU = ("sm", 6144, 7168)
                def f_dbu(e, s=s, par=par):
                    return e.tensor_mul(sm[:, 6144:7168],
                                        sm[:, par * 1024:(par + 1) * 1024],
                                        big_a[:, s * HF:(s + 1) * HF])
                kb.op("vector", f_dbu, r=[DTXL, ("big_a", s * HF, (s + 1) * HF)],
                      w=[DBU])
                if half == 0:
                    def f_scan(e, sp=sp):
                        return e.tensor_tensor_scan(
                            big_b[:, HLn:HLn + 1024],
                            bigf[:, T1 + sp * 1024:T1 + (sp + 1) * 1024],
                            sm[:, 6144:7168],
                            0.0, ALU.mult, ALU.add)
                    kb.op("vector", f_scan, r=[DA, DBU], w=[HK])
                    ic = it * 16 + s
                    def f_hc(e, ic=ic):
                        return e.tensor_copy(hcol[:, ic:ic + 1],
                                             big_b[:, HLn + 1023:HLn + 1024])
                    kb.op("vector", f_hc, r=[HK], w=[("hcol", ic, ic + 1)])
                else:
                    ic = it * 16 + s
                    def f_scan(e, sp=sp, ic=ic):
                        return e.tensor_tensor_scan(
                            big_b[:, HLn:HLn + 1024],
                            bigf[:, T1 + sp * 1024:T1 + (sp + 1) * 1024],
                            sm[:, 6144:7168],
                            hcol[:, ic:ic + 1], ALU.mult, ALU.add)
                    kb.op("vector", f_scan, r=[DA, DBU, ("hcol", ic, ic + 1)], w=[HK])
                leaf = L0n if s % 2 == 0 else L1n
                def f_ch(e, s=s, leaf=leaf):
                    return e.tensor_mul(big_b[:, leaf:leaf + 1024],
                                        big_b[:, HLn:HLn + 1024],
                                        big_a[:, 16384 + s * HF:16384 + (s + 1) * HF])
                kb.op("vector", f_ch,
                      r=[HK, ("big_a", 16384 + s * HF, 16384 + (s + 1) * HF)],
                      w=[("big_b", leaf, leaf + 1024)])
                if s % 4 == 1:
                    def f_pair(e):
                        return e.tensor_add(big_b[:, L2n:L2n + 1024],
                                            big_b[:, L0n:L0n + 1024],
                                            big_b[:, L1n:L1n + 1024])
                    kb.op("vector", f_pair, r=[LK[0], LK[1]], w=[LK[2]])
                if s % 4 == 3:
                    def f_pair1(e):
                        return e.tensor_add(big_b[:, L0n:L0n + 1024],
                                            big_b[:, L0n:L0n + 1024],
                                            big_b[:, L1n:L1n + 1024])
                    kb.op("vector", f_pair1, r=[LK[0], LK[1]], w=[LK[0]])
                    def f_quad(e):
                        return e.tensor_add(big_b[:, L0n:L0n + 1024],
                                            big_b[:, L2n:L2n + 1024],
                                            big_b[:, L0n:L0n + 1024])
                    kb.op("vector", f_quad, r=[LK[2], LK[0]], w=[LK[0]])
                    if s == 3:
                        kb.op("vector", lambda e: e.tensor_copy(bigf[:, T2:T2 + 1024],
                                                                big_b[:, L0n:L0n + 1024]),
                              r=[LK[0]], w=[YA])
                    else:
                        kb.op("vector", lambda e: e.tensor_add(bigf[:, T2:T2 + 1024],
                                                               bigf[:, T2:T2 + 1024],
                                                               big_b[:, L0n:L0n + 1024]),
                              r=[LK[0], YA], w=[YA])
                if lead_reps and s + 2 < NS:
                    rep_unit(0, s + 2)
                    rep_unit(1, s + 2)
            YF = ("bigf", T0, T0 + 1024)
            def f_yf(e, par=par):
                return e.tensor_add(bigf[:, T0:T0 + 1024], bigf[:, T2:T2 + 1024],
                                    sm[:, (2 + par) * 1024:(3 + par) * 1024])
            kb.op("vector", f_yf, r=[YA, XCL], w=[YF])
            def f_yg(e, par=par):
                return e.tensor_mul(big_b[:, YGn:YGn + 1024], bigf[:, T0:T0 + 1024],
                                    sm[:, (4 + par) * 1024:(5 + par) * 1024])
            kb.op("vector", f_yg, r=[YF, SZL], w=[("big_b", YGn, YGn + 1024)])
            def f_spyg(e, it=it, half=half):
                return e.dma_start(out=d_yg[it][:, half * HF:(half + 1) * HF],
                                   in_=big_b[:, YGn:YGn + 1024])
            kb.op("sync", f_spyg, kind="dma", r=[("big_b", YGn, YGn + 1024)],
                  w=[("d_yg", it * 2 + half, it * 2 + half + 1)])

        ct_order = []
        for k in range(16):
            ct_order += [k, 16 + k]
        emit_p1_ldx(0, 0)
        for j in range(8):
            if j + 1 < 8:
                emit_p1_ldx(0, j + 1)
            emit_p1_lt(0, j, do_load=False)
        kb.op("vector", lambda e: e.memset(convh[:, 0:3], 0.0), w=[("convh", 0, 3)])
        for cti in range(32):
            emit_p2_ct(0, cti, ct_order)
            for _ in range(2):
                if weff_steps:
                    weff_steps.pop(0)()
        while weff_steps:
            weff_steps.pop(0)()
        for ct in range(16):
            def f_lwx(e, ct=ct):
                return e.dma_start(out=big_b[:, ct * 1056:(ct + 1) * 1056],
                                   in_=dw_x[ct * 128:(ct + 1) * 128, :])
            kb.op("gpsimd", f_lwx, kind="dma",
                  w=[("big_b", ct * 1056, (ct + 1) * 1056)])
        emit_p3a_half(0)
        for it in range(NT):
            if it + 1 < NT:
                emit_p3b_loads(0, it + 1)
            if it < 8:
                emit_p1_ldx(1, it)
            emit_p4(0, it, lead_reps=(it == 0))
            if it + 1 < NT:
                emit_p3b(0, it + 1, do_loads=False)
            if it < 8:
                emit_p1_lt(1, it, do_load=False)
            else:
                for cti in range((it - 8) * 4, (it - 8) * 4 + 4):
                    emit_p2_ct(1, cti, ct_order)
        def emit_mix_early_loads(lt):
            ab = 8192 + (lt % 4) * 2048
            for it in range(NT):
                def f_ly(e, lt=lt, it=it, ab=ab):
                    return e.dma_start(
                        out=arena[:, ab + it * 128:ab + (it + 1) * 128],
                        in_=d_yg[it][:, lt * 128:(lt + 1) * 128])
                kb.op("gpsimd", f_ly, kind="dma", r=[("d_yg", it * 2, it * 2 + 1)],
                      w=[("arena", ab + it * 128, ab + (it + 1) * 128)])

        def emit_mix_early(lt):
            ab = 8192 + (lt % 4) * 2048
            for n2 in range(2):
                ps, pk = (psC, "psC") if n2 == 0 else (psD, "psD")
                for it in range(NT):
                    def f_mm(e, n2=n2, it=it, ps=ps, ab=ab):
                        return e.matmul(
                            ps[:, 0:512],
                            arena[:, ab + it * 128:ab + (it + 1) * 128],
                            big_b[:, it * 1024 + n2 * 512:it * 1024 + n2 * 512 + 512],
                            start=(it == 0), stop=(it == 15))
                    kb.op("tensor", f_mm,
                          r=[("arena", ab, ab + 2048), ("big_b", 0, 16384)], w=[pk])
                def f_ev(e, n2=n2, ps=ps):
                    return e.copy(big_b[:, 16384 + n2 * 512:16384 + n2 * 512 + 512],
                                  ps[:, 0:512])
                kb.op("scalar", f_ev, r=[pk],
                      w=[("big_b", 16384 + n2 * 512, 16384 + n2 * 512 + 512)])
            def f_sp(e, lt=lt):
                return e.dma_start(out=d_mix[lt], in_=big_b[:, 16384:17408])
            kb.op("sync", f_sp, kind="dma", r=[("big_b", 16384, 17408)],
                  w=[("d_mix", lt, lt + 1)])

        emit_p3a_half(1)
        for it in range(NT):
            if it + 1 < NT:
                emit_p3b_loads(1, it + 1)
            emit_p4(1, it, lead_reps=(it == 0))
            if it + 1 < NT:
                emit_p3b(1, it + 1, do_loads=False)
            if it < 8:
                for k in (it * 2, it * 2 + 1):
                    def f_lwo(e, k=k):
                        return e.dma_start(out=big_b[:, k * 1024:(k + 1) * 1024],
                                           in_=dw_out[k * 128:(k + 1) * 128, :])
                    kb.op("gpsimd", f_lwo, kind="dma",
                          w=[("big_b", k * 1024, (k + 1) * 1024)])
            if it >= 7 and it < 15:
                emit_mix_early_loads(it - 7)
            if it >= 8:
                emit_mix_early(it - 8)

        for lt in (0, 1):
            mb0 = T2 + (lt % 2) * 1024
            def f_lm0(e, lt=lt, mb0=mb0):
                return e.dma_start(out=bigf[:, mb0:mb0 + 1024], in_=d_mix[lt])
            kb.op("gpsimd", f_lm0, kind="dma", r=[("d_mix", lt, lt + 1)],
                  w=[("bigf", mb0, mb0 + 1024)])
        for it in range(NT):
            def f_lyg(e, it=it):
                return e.dma_start(out=big_a[:, it * L + 1024:(it + 1) * L],
                                   in_=d_yg[it][:, 1024:2048])
            kb.op("sync", f_lyg, kind="dma", r=[("d_yg", it * 2 + 1, it * 2 + 2)],
                  w=[("big_a", it * L + 1024, (it + 1) * L)])
        kb.op("gpsimd", lambda e: e.dma_start(out=sm[:, 0:7168], in_=d_weff[:, 0:7168]),
              kind="dma", r=[("d_weff", 0, 7168)], w=["sm"])
        kb.op("gpsimd", lambda e: e.dma_start(out=big_b[:, 24064:25088],
                                              in_=d_weff[:, 7168:8192]),
              kind="dma", r=[("d_weff", 7168, 8192)], w=[("big_b", 24064, 25088)])
        YG_ALL = ("big_a", 0, 16 * L)
        WO_ALL = ("big_b", 0, 16 * 1024)
        WEFF = ("sm", 0, 7168)

        kb.op("sync", lambda e: e.dma_start(out=bigf[0:8, T1:T1 + 1024], in_=dffn_b),
              kind="dma", w=[("bigf", T1, T1 + 1024)])
        kb.op("vector", lambda e: e.tensor_scalar(bigf[0:8, T1:T1 + 1024],
                                                  bigf[0:8, T1:T1 + 1024],
                                                  maskt[:], None, ALU.mult),
              r=[("bigf", T1, T1 + 1024), "maskt"], w=[("bigf", T1, T1 + 1024)])
        kb.op("vector", lambda e: e.tensor_copy(big_b[0:8, 16384:16384 + 1024],
                                                bigf[0:8, T1:T1 + 1024]),
              r=[("bigf", T1, T1 + 1024)], w=[("big_b", 16384, 17408)])
        for n2 in range(2):
            def f_mm(e, n2=n2):
                return e.matmul(psE[0:1, 0:512], ones8[:],
                                big_b[0:8, 16384 + n2 * 512:16384 + n2 * 512 + 512],
                                start=True, stop=True)
            kb.op("tensor", f_mm, r=[("big_b", 16384, 17408), "ones8"], w=["psE"])
            def f_ev(e, n2=n2):
                return e.tensor_copy(beffrow[:, n2 * 512:n2 * 512 + 512], psE[0:1, 0:512])
            kb.op("vector", f_ev, r=["psE"], w=[("big_b", 20480 + n2 * 512, 20480 + n2 * 512 + 512)])
        for n2 in range(2):
            def f_mm(e, n2=n2):
                return e.matmul(psE[:, 0:512], ones1,
                                beffrow[:, n2 * 512:n2 * 512 + 512], start=True,
                                stop=True)
            kb.op("tensor", f_mm, r=[("big_b", 20480, 21504), ("big_b", 27264, 27392)], w=["psE"])
            def f_ev(e, n2=n2):
                return e.tensor_copy(
                    convh[:, n2 * 512:n2 * 512 + 512], psE[:, 0:512])
            kb.op("vector", f_ev, r=["psE"],
                  w=[("convh", n2 * 512, n2 * 512 + 512)])
        BEFF = ("convh", 0, 1024)

        statn = sb("statn", [128, 128], F32)

        def ln_ops(src_lo, dst_ap_fn, dst_w, sb_, sq_lo):
            SRC = ("bigf", src_lo, src_lo + 1024)
            SQ5 = ("bigf", sq_lo, sq_lo + 1024)
            c = lambda a, b: ("statn", sb_ + a, sb_ + b)
            s_ = statn
            kb.op("vector", lambda e: e.reduce_sum(s_[:, sb_:sb_ + 1],
                                                   bigf[:, src_lo:src_lo + 1024],
                                                   axis=AX.X), r=[SRC], w=[c(0, 1)])
            kb.op("scalar", lambda e: e.activation(bigf[:, sq_lo:sq_lo + 1024],
                                                   bigf[:, src_lo:src_lo + 1024],
                                                   AF.Square,
                                                   accum_out=s_[:, sb_ + 1:sb_ + 2]),
                  r=[SRC], w=[SQ5, c(1, 2)])
            kb.op("vector", lambda e: e.tensor_scalar(s_[:, sb_ + 2:sb_ + 3],
                                                      s_[:, sb_:sb_ + 1], 1.0 / D,
                                                      None, ALU.mult),
                  r=[c(0, 1)], w=[c(2, 3)])
            kb.op("vector", lambda e: e.tensor_scalar(s_[:, sb_ + 3:sb_ + 4],
                                                      s_[:, sb_ + 1:sb_ + 2], 1.0 / D,
                                                      None, ALU.mult),
                  r=[c(1, 2)], w=[c(3, 4)])
            kb.op("vector", lambda e: e.tensor_mul(s_[:, sb_ + 4:sb_ + 5],
                                                   s_[:, sb_ + 2:sb_ + 3],
                                                   s_[:, sb_ + 2:sb_ + 3]),
                  r=[c(2, 3)], w=[c(4, 5)])
            kb.op("vector", lambda e: e.tensor_sub(s_[:, sb_ + 5:sb_ + 6],
                                                   s_[:, sb_ + 3:sb_ + 4],
                                                   s_[:, sb_ + 4:sb_ + 5]),
                  r=[c(3, 5)], w=[c(5, 6)])
            kb.op("scalar", lambda e: e.activation(s_[:, sb_ + 6:sb_ + 7],
                                                   s_[:, sb_ + 5:sb_ + 6], AF.Ln,
                                                   bias=cpool[:, 0:1]),
                  r=[c(5, 6), "cpool"], w=[c(6, 7)])
            kb.op("scalar", lambda e: e.activation(s_[:, sb_ + 7:sb_ + 8],
                                                   s_[:, sb_ + 6:sb_ + 7], AF.Exp,
                                                   scale=-0.5),
                  r=[c(6, 7)], w=[c(7, 8)])
            kb.op("vector", lambda e: e.tensor_scalar(s_[:, sb_ + 8:sb_ + 9],
                                                      s_[:, sb_ + 2:sb_ + 3],
                                                      s_[:, sb_ + 7:sb_ + 8], -1.0,
                                                      ALU.mult, ALU.mult),
                  r=[c(2, 3), c(7, 8)], w=[c(8, 9)])
            def f_dst(e):
                return e.activation(dst_ap_fn(), bigf[:, src_lo:src_lo + 1024],
                                    AF.Identity, bias=s_[:, sb_ + 8:sb_ + 9],
                                    scale=s_[:, sb_ + 7:sb_ + 8])
            kb.op("scalar", f_dst, r=[SRC, c(7, 9)], w=[dst_w])


        def emit_mix(lt):
            mb = T2 + (lt % 2) * 1024
            def f_lm(e, lt=lt, mb=mb):
                return e.dma_start(out=bigf[:, mb:mb + 1024], in_=d_mix[lt])
            kb.op("gpsimd", f_lm, kind="dma", r=[("d_mix", lt, lt + 1)],
                  w=[("bigf", mb, mb + 1024)])

        def emit_mix_late(lt):
            for n2 in range(2):
                ps, pk = (psA, "psA") if n2 == 0 else (psB, "psB")
                for it in range(NT):
                    def f_mm(e, lt=lt, n2=n2, it=it, ps=ps):
                        return e.matmul(
                            ps[:, 0:512],
                            big_a[:, it * L + lt * 128:it * L + lt * 128 + 128],
                            big_b[:, it * 1024 + n2 * 512:it * 1024 + n2 * 512 + 512],
                            start=(it == 0), stop=(it == 15))
                    kb.op("tensor", f_mm, r=[YG_ALL, WO_ALL], w=[pk])
                def f_ev(e, n2=n2, ps=ps):
                    return e.copy(big_b[:, 18432 + n2 * 512:18432 + n2 * 512 + 512],
                                  ps[:, 0:512])
                kb.op("scalar", f_ev, r=[pk],
                      w=[("big_b", 18432 + n2 * 512, 18432 + n2 * 512 + 512)])
            def f_sp(e, lt=lt):
                return e.dma_start(out=d_mix[lt], in_=big_b[:, 18432:19456])
            kb.op("sync", f_sp, kind="dma", r=[("big_b", 18432, 19456)],
                  w=[("d_mix", lt, lt + 1)])

        def emit_tail_a(lt):
            p = lt % 2
            mb = T2 + p * 1024
            xs = p * 1024
            x2s = 2048 + p * 1024
            sqs = 6180 + p * 1024
            l2s = 16384 if p == 0 else 25088
            l2t = 17408 if p == 0 else 26112
            def f_ldx(e, lt=lt, xs=xs):
                return e.dma_start(out=bigf[:, xs:xs + 1024],
                                   in_=dx[lt * 128:(lt + 1) * 128, :])
            kb.op("sync", f_ldx, kind="dma", w=[("bigf", xs, xs + 1024)])
            XT5 = ("bigf", xs, xs + 1024)

            sb0 = (lt % 4) * 32
            ln_ops(mb, lambda sqs=sqs: bigf[:, sqs:sqs + 1024],
                   ("bigf", sqs, sqs + 1024), sb0, sq_lo=sqs)
            kb.op("vector", lambda e, xs=xs, x2s=x2s, sqs=sqs: e.tensor_add(
                                                   bigf[:, x2s:x2s + 1024],
                                                   bigf[:, xs:xs + 1024],
                                                   bigf[:, sqs:sqs + 1024]),
                  r=[("bigf", sqs, sqs + 1024), XT5], w=[("bigf", x2s, x2s + 1024)])
            X2 = ("bigf", x2s, x2s + 1024)
            ln_ops(x2s, lambda l2s=l2s: big_b[:, l2s:l2s + 1024],
                   ("big_b", l2s, l2s + 1024), sb0 + 10, sq_lo=sqs)
            for dt_ in range(8):
                bank, bk = (psT, "psT") if dt_ % 2 == 0 else (psU, "psU")
                def f_tr(e, dt_=dt_, bank=bank, l2s=l2s):
                    return e.transpose(bank[:, 0:128],
                                       big_b[:, l2s + dt_ * 128:l2s + dt_ * 128 + 128],
                                       identb)
                kb.op("tensor", f_tr, r=[("big_b", l2s, l2s + 1024), ("big_b", 27136, 27264)],
                      w=[bk])
                def f_cp(e, dt_=dt_, bank=bank, l2t=l2t):
                    return e.copy(
                        big_b[:, l2t + dt_ * 128:l2t + dt_ * 128 + 128],
                        bank[:, 0:128])
                kb.op("scalar", f_cp, r=[bk],
                      w=[("big_b", l2t + dt_ * 128, l2t + dt_ * 128 + 128)])
            LN2T = ("big_b", l2t, l2t + 1024)
            for n2 in range(2):
                if p == 0:
                    ps, pk = (psC, "psC") if n2 == 0 else (psD, "psD")
                else:
                    ps, pk = (psE, "psE") if n2 == 0 else (psV, "psV")
                for dt_ in range(8):
                    if dt_ < 7:
                        rhs_ap = sm[:, dt_ * 1024 + n2 * 512:dt_ * 1024 + n2 * 512 + 512]
                        rhs_k = WEFF
                    else:
                        rhs_ap = big_b[:, 24064 + n2 * 512:24064 + n2 * 512 + 512]
                        rhs_k = ("big_b", 24064, 25088)
                    def f_mm(e, n2=n2, dt_=dt_, ps=ps, rhs_ap=rhs_ap, l2t=l2t):
                        return e.matmul(
                            ps[:, 0:512],
                            big_b[:, l2t + dt_ * 128:l2t + dt_ * 128 + 128],
                            rhs_ap,
                            start=(dt_ == 0), stop=(dt_ == 7))
                    kb.op("tensor", f_mm, r=[LN2T, rhs_k], w=[pk])

        def emit_tail_b(lt):
            p = lt % 2
            xs = p * 1024
            x2s = 2048 + p * 1024
            sqs = 6180 + p * 1024
            sb0 = (lt % 4) * 32
            X2 = ("bigf", x2s, x2s + 1024)
            for n2 in range(2):
                if p == 0:
                    ps, pk = (psC, "psC") if n2 == 0 else (psD, "psD")
                else:
                    ps, pk = (psE, "psE") if n2 == 0 else (psV, "psV")
                def f_ev(e, n2=n2, ps=ps, xs=xs):
                    return e.tensor_add(
                        bigf[:, xs + n2 * 512:xs + n2 * 512 + 512], ps[:, 0:512],
                        convh[:, n2 * 512:n2 * 512 + 512])
                kb.op("vector", f_ev, r=[pk, BEFF],
                      w=[("bigf", xs + n2 * 512, xs + n2 * 512 + 512)])
            ln_ops(xs, lambda sqs=sqs: bigf[:, sqs:sqs + 1024],
                   ("bigf", sqs, sqs + 1024), sb0 + 20, sq_lo=sqs)
            kb.op("vector", lambda e, sqs=sqs, x2s=x2s: e.tensor_add(
                                                   bigf[:, sqs:sqs + 1024],
                                                   bigf[:, x2s:x2s + 1024],
                                                   bigf[:, sqs:sqs + 1024]),
                  r=[("bigf", sqs, sqs + 1024), X2], w=[("bigf", sqs, sqs + 1024)])
            def f_out(e, lt=lt, sqs=sqs):
                return e.dma_start(out=dy[lt * 128:(lt + 1) * 128, :],
                                   in_=bigf[:, sqs:sqs + 1024])
            kb.op("sync", f_out, kind="dma", r=[("bigf", sqs, sqs + 1024)],
                  w=[("dy", lt, lt + 1)], sem="dout")

        emit_tail_a(0)
        for lt in range(NT):
            emit_tail_b(lt)
            if lt < 8:
                emit_mix_late(lt + 8)
            if lt + 2 < NT:
                emit_mix(lt + 2)
            if lt + 1 < NT:
                emit_tail_a(lt + 1)

        counters = kb.resolve()
        with nc.Block() as block:
            kb.emit(block, sems, final_waits=[("dout", counters.get("dout", 0))])

    return nc


_NC_CACHE = [None]


def kernel(**inputs):
    x = np.ascontiguousarray(inputs["L0_attn_inputs"], dtype=np.float32)
    W_in = np.ascontiguousarray(inputs["L0_attn_W_in"], dtype=np.float32)
    conv_w = np.ascontiguousarray(inputs["L0_attn_conv_w"], dtype=np.float32)
    conv_b = np.ascontiguousarray(inputs["L0_attn_conv_b"], dtype=np.float32)
    W_x = np.ascontiguousarray(inputs["L0_attn_W_x"], dtype=np.float32)
    W_dt = np.ascontiguousarray(inputs["L0_attn_W_dt"], dtype=np.float32)
    b_dt = np.ascontiguousarray(inputs["L0_attn_b_dt"], dtype=np.float32)
    A_log = np.ascontiguousarray(inputs["L0_attn_A_log"], dtype=np.float32)
    W_out = np.ascontiguousarray(inputs["L0_attn_W_out"], dtype=np.float32)
    ffn_W = np.ascontiguousarray(inputs["L0_ffn_W"], dtype=np.float32)
    ffn_b = np.ascontiguousarray(inputs["L0_ffn_b"], dtype=np.float32)
    logits = np.ascontiguousarray(inputs["L0_ffn_logits"], dtype=np.float32)
    u = np.ascontiguousarray(inputs["L0_ffn_u"], dtype=np.float32)

    W_in2 = np.ascontiguousarray(
        W_in.reshape(8, 128, 32, 128).transpose(1, 2, 0, 3))
    W_dt2 = np.ascontiguousarray(
        W_dt.reshape(8, 128, 16, 128).transpose(1, 2, 0, 3))
    conv_wt = np.ascontiguousarray(
        conv_w.reshape(4, 16, 128).transpose(2, 1, 0).reshape(128, 64))
    conv_b2 = np.ascontiguousarray(conv_b.reshape(16, 128).T)
    b_dt2 = np.ascontiguousarray(b_dt.reshape(16, 128).T)
    a_log2 = np.ascontiguousarray(
        A_log.reshape(16, 128, 16).transpose(1, 0, 2).reshape(128, 256))
    logits1 = np.ascontiguousarray(logits.reshape(1, 8))
    u1 = np.ascontiguousarray(u.reshape(1, 8))
    ident = np.eye(128, dtype=np.float32)
    sel16 = np.zeros((16, 2048), dtype=np.float32)
    for s in range(16):
        sel16[s, s * 128:(s + 1) * 128] = 1.0

    if _NC_CACHE[0] is None:
        _NC_CACHE[0] = build_nc()
    nc = _NC_CACHE[0]

    shared = {"w_in": W_in2, "w_x": W_x, "w_dt": W_dt2, "w_out": W_out,
              "ffn_w": ffn_W, "ffn_b": ffn_b, "conv_wt": conv_wt,
              "conv_b2": conv_b2, "b_dt2": b_dt2, "a_log2": a_log2,
              "logits1": logits1, "u1": u1, "ident": ident, "sel16": sel16}
    in_maps = [dict(shared, x=np.ascontiguousarray(x[b])) for b in range(NCORES)]
    res = run_bass_kernel_spmd(nc, in_maps, core_ids=list(range(NCORES)))
    return np.stack([np.asarray(res.results[b]["y"], dtype=np.float32)
                     for b in range(NCORES)], axis=0)


# version 21
# speedup vs baseline: 1.0438x; 1.0255x over previous
import numpy as np
from contextlib import ExitStack
import concourse.bass as bass
import concourse.mybir as mybir
from concourse.bass_utils import run_bass_kernel_spmd

F32 = mybir.dt.float32
BF16 = mybir.dt.bfloat16
AF = mybir.ActivationFunctionType
ALU = mybir.AluOpType
AX = mybir.AxisListType

L = 2048
D = 1024
DI = 2048
NS = 16
NT = 16
NCORES = 8
EPS = 1e-5
DMA_INC = 16


class KB:

    def __init__(self):
        self.plan = []
        self.bufs = {}

    def _norm(self, spec):
        if isinstance(spec, str):
            return (spec, 0, 1 << 30)
        return spec

    PSUM_KEYS = frozenset(["psA", "psB", "psC", "psD", "psE", "psV", "psT", "psU"])

    def op(self, eng, fn, r=(), w=(), kind="compute", sem=None):
        deps = set()
        r = [self._norm(s) for s in r]
        w = [self._norm(s) for s in w]
        w = w + [s for s in r if s[0] in self.PSUM_KEYS]
        r = [s for s in r if s[0] not in self.PSUM_KEYS]
        for (name, lo, hi) in r:
            for ent in self.bufs.get(name, ()):
                if ent[0] < hi and lo < ent[1] and ent[2] is not None:
                    deps.add(ent[2])
        for (name, lo, hi) in w:
            for ent in self.bufs.get(name, ()):
                if ent[0] < hi and lo < ent[1]:
                    if ent[2] is not None:
                        deps.add(ent[2])
                    deps.update(ent[3])
        i = len(self.plan)
        self.plan.append(dict(eng=eng, fn=fn, deps=sorted(deps), kind=kind,
                              sem=sem, marked=False, token=None, waits=None))
        for (name, lo, hi) in r:
            lst = self.bufs.setdefault(name, [])
            for ent in lst:
                if ent[0] < hi and lo < ent[1]:
                    ent[3].add(i)
        for (name, lo, hi) in w:
            lst = self.bufs.setdefault(name, [])
            lst[:] = [ent for ent in lst if not (lo <= ent[0] and ent[1] <= hi)]
            lst.append([lo, hi, i, set()])
        return i

    def _queue(self, o):
        if o["sem"] is not None:
            return o["sem"]
        return "dma_s" if o["eng"] == "sync" else "dma_g"

    def resolve(self):
        POOLS = {"dma_s": ["dsa", "dsb", "dsc", "dsd"],
                 "dma_g": ["dga", "dgb", "dgc"],
                 "dout": ["dout"]}
        semcum = {}
        qcur = {q: None for q in POOLS}
        qopen = {q: [] for q in POOLS}
        qsi = {q: -1 for q in POOLS}
        group_close = {}
        needed = [[] for _ in self.plan]
        prewaits = [[] for _ in self.plan]

        def close_group(q):
            sem = qcur[q]
            if sem is None:
                return
            v = semcum.get(sem, 0)
            for j in qopen[q]:
                group_close[j] = (sem, v)
            qopen[q] = []
            qcur[q] = None

        for i, o in enumerate(self.plan):
            for d in o["deps"]:
                p = self.plan[d]
                if p["kind"] == "compute":
                    p["marked"] = True
                    needed[i].append(d)
                else:
                    q = self._queue(p)
                    if d in qopen[q]:
                        close_group(q)
                    needed[i].append(d)
            if o["kind"] == "dma":
                q = self._queue(o)
                if qcur[q] is None:
                    qsi[q] += 1
                    sem = POOLS[q][qsi[q] % len(POOLS[q])]
                    if semcum.get(sem, 0) > 0:
                        prewaits[i].append((sem, semcum[sem]))
                    qcur[q] = sem
                sem = qcur[q]
                semcum[sem] = semcum.get(sem, 0) + DMA_INC
                o["marked"] = True
                o["token"] = (sem, semcum[sem], DMA_INC)
                qopen[q].append(i)
        for q in POOLS:
            close_group(q)

        counters = dict(semcum)
        for o in self.plan:
            if o["marked"] and o["kind"] == "compute":
                s = o["eng"]
                counters[s] = counters.get(s, 0) + 1
                o["token"] = (s, counters[s], 1)

        marks = {}
        for i, o in enumerate(self.plan):
            wl = {}
            for (s, v) in prewaits[i]:
                if wl.get(s, 0) < v:
                    wl[s] = v
            for d in needed[i]:
                p = self.plan[d]
                if p["kind"] == "dma":
                    s, v = group_close[d]
                else:
                    s, v, _ = p["token"]
                if wl.get(s, 0) < v:
                    wl[s] = v
            final = []
            for s, v in wl.items():
                key = (o["eng"], s)
                if marks.get(key, 0) < v:
                    marks[key] = v
                    final.append((s, v))
            o["waits"] = final
        return counters

    def emit(self, block, sems, final_waits):
        for name in ["sync", "gpsimd", "tensor", "vector", "scalar"]:
            ops = [o for o in self.plan if o["eng"] == name]

            def make(eng_ops, engname):
                def body(eng):
                    for o in eng_ops:
                        for (s, v) in o["waits"]:
                            eng.wait_ge(sems[s], v)
                        inst = o["fn"](eng)
                        if o["marked"]:
                            s, v, inc = o["token"]
                            inst.then_inc(sems[s], inc)
                    if engname == "sync":
                        for (s, v) in final_waits:
                            eng.wait_ge(sems[s], v)
                return body

            getattr(block, name)(make(ops, name))


_PHASES = [7]


def _fin(nc, kb, sems, es, dy, bigf, T0):
    for lt in range(NT):
        def f_out(e, lt=lt):
            return e.dma_start(out=dy[lt * 128:(lt + 1) * 128, :],
                               in_=bigf[:, T0:T0 + 1024])
        kb.op("sync", f_out, kind="dma", r=[("bigf", T0, T0 + 1024)],
              w=[("dy", lt, lt + 1)], sem="dout")
    counters = kb.resolve()
    with nc.Block() as block:
        kb.emit(block, sems, final_waits=[("dout", counters.get("dout", 0))])
    es.close()
    return nc



def build_nc():
    nc = bass.Bass("TRN2", target_bir_lowering=False, debug=False)

    dx = nc.dram_tensor("x", [L, D], F32, kind="ExternalInput").ap()
    dw_in = nc.dram_tensor("w_in", [128, 32, 8, 128], F32, kind="ExternalInput").ap()
    dw_x = nc.dram_tensor("w_x", [DI, D + 2 * NS], F32, kind="ExternalInput").ap()
    dw_dt = nc.dram_tensor("w_dt", [128, 16, 8, 128], F32, kind="ExternalInput").ap()
    dw_out = nc.dram_tensor("w_out", [DI, D], F32, kind="ExternalInput").ap()
    dffn_w = nc.dram_tensor("ffn_w", [8, D, D], F32, kind="ExternalInput").ap()
    dffn_b = nc.dram_tensor("ffn_b", [8, D], F32, kind="ExternalInput").ap()
    dconv_wt = nc.dram_tensor("conv_wt", [128, 64], F32, kind="ExternalInput").ap()
    dconv_b2 = nc.dram_tensor("conv_b2", [128, 16], F32, kind="ExternalInput").ap()
    db_dt2 = nc.dram_tensor("b_dt2", [128, 16], F32, kind="ExternalInput").ap()
    da_log2 = nc.dram_tensor("a_log2", [128, 256], F32, kind="ExternalInput").ap()
    dlogits = nc.dram_tensor("logits1", [1, 8], F32, kind="ExternalInput").ap()
    du1 = nc.dram_tensor("u1", [1, 8], F32, kind="ExternalInput").ap()
    dident = nc.dram_tensor("ident", [128, 128], F32, kind="ExternalInput").ap()
    dsel = nc.dram_tensor("sel16", [NS, 2048], F32, kind="ExternalInput").ap()
    dy = nc.dram_tensor("y", [L, D], F32, kind="ExternalOutput").ap()

    d_xconv = nc.dram_tensor("d_xconv", [NT, 128, L], BF16, kind="Internal").ap()
    d_sz = nc.dram_tensor("d_sz", [NT, 128, L], BF16, kind="Internal").ap()
    d_yg = nc.dram_tensor("d_yg", [NT, 128, L], BF16, kind="Internal").ap()
    d_weff = nc.dram_tensor("d_weff", [128, 8 * D], BF16, kind="Internal").ap()
    d_mix = nc.dram_tensor("d_mix", [16, 128, D], BF16, kind="Internal").ap()
    d_mask = nc.dram_tensor("d_mask", [1, 8], F32, kind="Internal").ap()

    kb = KB()

    with ExitStack() as es:
        sb = lambda n, shp, dt: es.enter_context(nc.sbuf_tensor(n, shp, dt))
        pt = lambda n, shp, dt: es.enter_context(nc.psum_tensor(n, shp, dt))

        arena = sb("arena", [128, 8 * L], BF16)
        big_a = sb("big_a", [128, 16 * L], BF16)
        big_b = sb("big_b", [128, 27392], BF16)
        T0, T1, T2, T3 = 0, 2052, 4132, 6180
        bigf = sb("bigf", [128, 8228], F32)
        bflat = big_b[0:1, 18432:20480]
        identb = big_b[:, 27136:27264]
        ones1 = big_b[0:1, 27264:27392]
        beffrow = big_b[0:1, 20480:21504]
        sm = sb("sm", [128, 7 * 1024], BF16)
        pa = sb("pa", [128, 256], F32)
        wcol = sb("wcol", [128, 64], F32)
        cbcol = sb("cbcol", [128, 16], F32)
        bdtcol = sb("bdtcol", [128, 16], F32)
        ones8 = sb("ones8", [8, 1], BF16)
        lgu = sb("lgu", [1, 32], F32)
        maskr = sb("maskr", [1, 8], F32)
        maskrb = sb("maskrb", [1, 8], BF16)
        maskt = sb("maskt", [8, 1], F32)
        maskrep = sb("maskrep", [128, 8], F32)
        cpool = sb("cpool", [128, 8], F32)
        hcol = sb("hcol", [128, 256], F32)
        convh = sb("convh", [128, 2051], F32)
        tails = sb("tails", [128, 48], F32)

        psA = pt("psA", [128, 512], F32)
        psB = pt("psB", [128, 512], F32)
        psC = pt("psC", [128, 512], F32)
        psD = pt("psD", [128, 512], F32)
        psE = pt("psE", [128, 512], F32)
        psV = pt("psV", [128, 512], F32)
        psT = pt("psT", [128, 1024], BF16)
        psU = pt("psU", [128, 1024], BF16)

        sems = {n: es.enter_context(nc.semaphore(n)) for n in
                ["dsa", "dsb", "dsc", "dsd", "dga", "dgb", "dgc",
                 "tensor", "vector", "scalar", "gpsimd", "dout"]}

        kb.op("vector", lambda e: e.memset(cpool[:, 0:1], EPS), w=["cpool"])
        kb.op("vector", lambda e: e.memset(cpool[:, 1:2], 1.0), w=["cpool"])
        kb.op("sync", lambda e: e.dma_start(out=pa[:, 0:256], in_=da_log2), kind="dma",
              w=[("pa", 0, 256)])
        kb.op("scalar", lambda e: e.activation(pa[:, 0:256], pa[:, 0:256], AF.Exp),
              r=[("pa", 0, 256)], w=[("pa", 0, 256)])
        kb.op("vector", lambda e: e.tensor_scalar(pa[:, 0:256], pa[:, 0:256], -1.0,
                                                  None, ALU.mult),
              r=[("pa", 0, 256)], w=[("pa", 0, 256)])
        kb.op("sync", lambda e: e.dma_start(out=wcol[:], in_=dconv_wt),
              kind="dma", w=["wcol"])
        kb.op("sync", lambda e: e.dma_start(out=cbcol[:], in_=dconv_b2), kind="dma", w=["cbcol"])
        kb.op("sync", lambda e: e.dma_start(out=bdtcol[:], in_=db_dt2), kind="dma", w=["bdtcol"])
        kb.op("gpsimd", lambda e: e.dma_start(out=identb, in_=dident), kind="dma", w=[("big_b", 27136, 27264)])
        kb.op("vector", lambda e: e.memset(ones1, 1.0), w=[("big_b", 27264, 27392)])
        kb.op("vector", lambda e: e.memset(ones8[:], 1.0), w=["ones8"])
        kb.op("vector", lambda e: e.memset(convh[:, 0:3], 0.0), w=[("convh", 0, 3)])

        kb.op("sync", lambda e: e.dma_start(out=lgu[:, 0:8], in_=dlogits), kind="dma",
              w=[("lgu", 0, 8)])
        kb.op("sync", lambda e: e.dma_start(out=lgu[:, 8:16], in_=du1), kind="dma",
              w=[("lgu", 8, 16)])
        kb.op("scalar", lambda e: e.activation(lgu[:, 16:24], lgu[:, 8:16], AF.Ln),
              r=[("lgu", 8, 16)], w=[("lgu", 16, 24)])
        kb.op("vector", lambda e: e.tensor_scalar(lgu[:, 16:24], lgu[:, 16:24], -1.0, None,
                                                  ALU.mult),
              r=[("lgu", 16, 24)], w=[("lgu", 16, 24)])
        kb.op("scalar", lambda e: e.activation(lgu[:, 16:24], lgu[:, 16:24], AF.Ln),
              r=[("lgu", 16, 24)], w=[("lgu", 16, 24)])
        kb.op("vector", lambda e: e.tensor_sub(lgu[:, 24:32], lgu[:, 0:8], lgu[:, 16:24]),
              r=[("lgu", 0, 24)], w=[("lgu", 24, 32)])
        kb.op("vector", lambda e: e.reduce_max(lgu[:, 16:17], lgu[:, 24:32], axis=AX.X),
              r=[("lgu", 24, 32)], w=[("lgu", 16, 17)])
        kb.op("vector", lambda e: e.tensor_scalar(maskr[:], lgu[:, 24:32], lgu[:, 16:17],
                                                  None, ALU.is_ge),
              r=[("lgu", 16, 32)], w=["maskr"])
        kb.op("vector", lambda e: e.tensor_copy(maskrb[:], maskr[:]), r=["maskr"], w=["maskrb"])
        kb.op("sync", lambda e: e.dma_start(out=d_mask, in_=maskr[:]), kind="dma",
              r=["maskr"], w=["d_mask"])
        kb.op("sync", lambda e: e.dma_start(out=maskt[:], in_=d_mask.rearrange("1 k -> k 1")),
              kind="dma", r=["d_mask"], w=["maskt"])
        kb.op("tensor", lambda e: e.matmul(psE[:, 0:8], ones1, maskrb[:], start=True,
                                           stop=True),
              r=[("big_b", 27264, 27392), "maskrb"], w=["psE"])
        kb.op("vector", lambda e: e.tensor_copy(maskrep[:], psE[:, 0:8]), r=["psE"],
              w=["maskrep"])

        MD0 = 24064
        for k in range(8):
            def f_md(e, k=k):
                return e.tensor_scalar(big_b[:, MD0 + k * 128:MD0 + (k + 1) * 128],
                                       identb, maskrep[:, k:k + 1], None, ALU.mult)
            kb.op("vector", f_md,
                  r=[("big_b", 27136, 27264), "maskrep"],
                  w=[("big_b", MD0 + k * 128, MD0 + (k + 1) * 128)])
        weff_steps = []

        def _weff_step(dt_, k):
            def go():
                lane = 4 + k % 2
                def f_load(e, k=k, dt_=dt_, lane=lane):
                    return e.dma_start(out=sm[:, lane * 1024:(lane + 1) * 1024],
                                       in_=dffn_w[k, dt_ * 128:(dt_ + 1) * 128, :])
                kb.op("gpsimd", f_load, kind="dma",
                      w=[("sm", lane * 1024, (lane + 1) * 1024)])
                for n2 in range(2):
                    ps, pk = (psC, "psC") if n2 == 0 else (psD, "psD")
                    def f_mm(e, k=k, n2=n2, ps=ps, lane=lane):
                        return e.matmul(ps[:, 0:512],
                                        big_b[:, MD0 + k * 128:MD0 + (k + 1) * 128],
                                        sm[:, lane * 1024 + n2 * 512:lane * 1024 + n2 * 512 + 512],
                                        start=(k == 0), stop=(k == 7))
                    kb.op("tensor", f_mm,
                          r=[("big_b", MD0 + k * 128, MD0 + (k + 1) * 128),
                             ("sm", lane * 1024, (lane + 1) * 1024)], w=[pk])
                if k == 7:
                    for n2 in range(2):
                        ps, pk = (psC, "psC") if n2 == 0 else (psD, "psD")
                        def f_ev(e, dt_=dt_, n2=n2, ps=ps):
                            return e.copy(
                                sm[:, 6144 + n2 * 512:6144 + n2 * 512 + 512],
                                ps[:, 0:512])
                        kb.op("scalar", f_ev, r=[pk],
                              w=[("sm", 6144 + n2 * 512, 6144 + n2 * 512 + 512)])
                    def f_sp(e, dt_=dt_):
                        return e.dma_start(
                            out=d_weff[:, dt_ * 1024:(dt_ + 1) * 1024],
                            in_=sm[:, 6144:6144 + 1024])
                    kb.op("sync", f_sp, kind="dma", r=[("sm", 6144, 7168)],
                          w=[("d_weff", dt_ * 1024, (dt_ + 1) * 1024)])
            return go

        for dt_ in range(8):
            for k in range(8):
                weff_steps.append(_weff_step(dt_, k))

        pscnt = [0]

        def next_ps():
            pscnt[0] += 1
            return [(psA, "psA"), (psB, "psB"), (psE, "psE")][pscnt[0] % 3]

        LN1T = ("arena", 0, 8192)

        def _p1_bufs(h, j):
            if h == 0:
                p = j % 2
                xt_ap = bigf[:, p * 1024:(p + 1) * 1024]
                XT = ("bigf", p * 1024, (p + 1) * 1024)
                sq_ap = bigf[:, 2048 + p * 1024:2048 + (p + 1) * 1024]
                SQ = ("bigf", 2048 + p * 1024, 2048 + (p + 1) * 1024)
                stg_ap = sm[:, 2048 + p * 1024:2048 + (p + 1) * 1024]
                STG = ("sm", 2048 + p * 1024, 2048 + (p + 1) * 1024)
                sb0 = 104 + p * 10
            else:
                xt_ap = convh[:, 0:1024]
                XT = ("convh", 0, 1024)
                sq_ap = convh[:, 1024:2048]
                SQ = ("convh", 1024, 2048)
                stg_ap = big_b[:, 24064:25088]
                STG = ("big_b", 24064, 25088)
                sb0 = 104 + (j % 2) * 10
            return xt_ap, XT, sq_ap, SQ, stg_ap, STG, sb0

        def emit_p1_ldx(h, j):
            lt = h * 8 + j
            xt_ap, XT = _p1_bufs(h, j)[0:2]
            def f_ldx(e, lt=lt, xt_ap=xt_ap):
                return e.dma_start(out=xt_ap, in_=dx[lt * 128:(lt + 1) * 128, :])
            kb.op("sync", f_ldx, kind="dma", w=[XT])

        def emit_p1_lt(h, j, do_load=True):
            lt = h * 8 + j
            xt_ap, XT, sq_ap, SQ, stg_ap, STG, sb0 = _p1_bufs(h, j)
            if do_load:
                emit_p1_ldx(h, j)
            kb.op("vector", lambda e: e.reduce_sum(statn[:, sb0:sb0 + 1], xt_ap,
                                                   axis=AX.X), r=[XT], w=[("statn", sb0, sb0 + 1)])
            kb.op("scalar", lambda e: e.activation(sq_ap, xt_ap,
                                                   AF.Square,
                                                   accum_out=statn[:, sb0 + 1:sb0 + 2]),
                  r=[XT], w=[SQ, ("statn", sb0 + 1, sb0 + 2)])
            kb.op("vector", lambda e: e.tensor_scalar(statn[:, sb0 + 2:sb0 + 3], statn[:, sb0:sb0 + 1],
                                                      1.0 / D, None, ALU.mult),
                  r=[("statn", sb0, sb0 + 1)], w=[("statn", sb0 + 2, sb0 + 3)])
            kb.op("vector", lambda e: e.tensor_scalar(statn[:, sb0 + 3:sb0 + 4], statn[:, sb0 + 1:sb0 + 2],
                                                      1.0 / D, None, ALU.mult),
                  r=[("statn", sb0 + 1, sb0 + 2)], w=[("statn", sb0 + 3, sb0 + 4)])
            kb.op("vector", lambda e: e.tensor_mul(statn[:, sb0 + 4:sb0 + 5], statn[:, sb0 + 2:sb0 + 3],
                                                   statn[:, sb0 + 2:sb0 + 3]),
                  r=[("statn", sb0 + 2, sb0 + 3)], w=[("statn", sb0 + 4, sb0 + 5)])
            kb.op("vector", lambda e: e.tensor_sub(statn[:, sb0 + 5:sb0 + 6], statn[:, sb0 + 3:sb0 + 4],
                                                   statn[:, sb0 + 4:sb0 + 5]),
                  r=[("statn", sb0 + 3, sb0 + 5)], w=[("statn", sb0 + 5, sb0 + 6)])
            kb.op("scalar", lambda e: e.activation(statn[:, sb0 + 6:sb0 + 7], statn[:, sb0 + 5:sb0 + 6],
                                                   AF.Ln, bias=cpool[:, 0:1]),
                  r=[("statn", sb0 + 5, sb0 + 6), "cpool"], w=[("statn", sb0 + 6, sb0 + 7)])
            kb.op("scalar", lambda e: e.activation(statn[:, sb0 + 7:sb0 + 8], statn[:, sb0 + 6:sb0 + 7],
                                                   AF.Exp, scale=-0.5),
                  r=[("statn", sb0 + 6, sb0 + 7)], w=[("statn", sb0 + 7, sb0 + 8)])
            kb.op("vector", lambda e: e.tensor_scalar(statn[:, sb0 + 8:sb0 + 9],
                                                      statn[:, sb0 + 2:sb0 + 3],
                                                      statn[:, sb0 + 7:sb0 + 8], -1.0,
                                                      ALU.mult, ALU.mult),
                  r=[("statn", sb0 + 2, sb0 + 3), ("statn", sb0 + 7, sb0 + 8)],
                  w=[("statn", sb0 + 8, sb0 + 9)])
            def f_norm(e, stg_ap=stg_ap, xt_ap=xt_ap):
                return e.activation(stg_ap, xt_ap, AF.Identity,
                                    bias=statn[:, sb0 + 8:sb0 + 9],
                                    scale=statn[:, sb0 + 7:sb0 + 8])
            kb.op("scalar", f_norm, r=[XT, ("statn", sb0 + 7, sb0 + 9)], w=[STG])
            for dt_ in range(8):
                bank, bk = (psT, "psT") if dt_ % 2 == 0 else (psU, "psU")
                def f_tr(e, dt_=dt_, bank=bank, stg_ap=stg_ap):
                    return e.transpose(bank[:, 0:128],
                                       stg_ap[:, dt_ * 128:dt_ * 128 + 128], identb)
                kb.op("tensor", f_tr, r=[STG, ("big_b", 27136, 27264)], w=[bk])
                def f_cp(e, j=j, dt_=dt_, bank=bank):
                    return e.copy(
                        arena[:, dt_ * 1024 + j * 128:dt_ * 1024 + j * 128 + 128],
                        bank[:, 0:128])
                kb.op("scalar", f_cp, r=[bk],
                      w=[("arena", dt_ * 1024 + j * 128, dt_ * 1024 + j * 128 + 128)])

        WINB = 16896

        def emit_p2_ct(h, cti, ct_order):
            ct = ct_order[cti]
            if h == 0:
                nring, rbase = 4, WINB
            else:
                nring, rbase = 2, 25088
            lane = cti % nring
            WI = ("big_b", rbase + lane * 1024, rbase + (lane + 1) * 1024)
            if cti < nring:
                def f_ldw0(e, ct=ct, lane=lane, rbase=rbase):
                    return e.dma_start(
                        out=big_b[:, rbase + lane * 1024:rbase + lane * 1024 + 1024],
                        in_=dw_in[:, ct].rearrange("p dt c -> p (dt c)"))
                kb.op("gpsimd", f_ldw0, kind="dma", w=[WI])
            for lc2 in range(2):
                ps, pk = next_ps()
                for dt_ in range(8):
                    def f_mm(e, lc2=lc2, dt_=dt_, ps=ps, lane=lane, rbase=rbase):
                        return e.matmul(
                            ps[:, 0:512],
                            big_b[:, rbase + lane * 1024 + dt_ * 128:rbase + lane * 1024 + dt_ * 128 + 128],
                            arena[:, dt_ * 1024 + lc2 * 512:dt_ * 1024 + lc2 * 512 + 512],
                            start=(dt_ == 0), stop=(dt_ == 7))
                    kb.op("tensor", f_mm,
                          r=[WI, ("arena", dt_ * 1024 + lc2 * 512,
                                  dt_ * 1024 + lc2 * 512 + 512)], w=[pk])
                if ct < 16:
                    def f_ev(e, lc2=lc2, ps=ps):
                        return e.copy(convh[:, 3 + lc2 * 512:3 + lc2 * 512 + 512],
                                      ps[:, 0:512])
                    kb.op("scalar", f_ev, r=[pk],
                          w=[("convh", 3 + lc2 * 512, 3 + lc2 * 512 + 512)])
                else:
                    if h == 0:
                        zstg, ZS = sm[:, 2048 + lc2 * 512:2048 + lc2 * 512 + 512], \
                            ("sm", 2048 + lc2 * 512, 2048 + lc2 * 512 + 512)
                    else:
                        zstg, ZS = big_b[:, 24064 + lc2 * 512:24064 + lc2 * 512 + 512], \
                            ("big_b", 24064 + lc2 * 512, 24064 + lc2 * 512 + 512)
                    def f_silu(e, ps=ps, zstg=zstg):
                        return e.activation(zstg, ps[:, 0:512], AF.Silu)
                    kb.op("scalar", f_silu, r=[pk], w=[ZS])
            if ct < 16:
                if h == 1:
                    def f_pad(e, ct=ct):
                        return e.tensor_copy(convh[:, 0:3], tails[:, ct * 3:ct * 3 + 3])
                    kb.op("vector", f_pad, r=[("tails", ct * 3, ct * 3 + 3)],
                          w=[("convh", 0, 3)])
                else:
                    def f_tail(e, ct=ct):
                        return e.tensor_copy(tails[:, ct * 3:ct * 3 + 3], convh[:, 1024:1027])
                    kb.op("vector", f_tail, r=[("convh", 1024, 1027)],
                          w=[("tails", ct * 3, ct * 3 + 3)])
                XP = ("convh", 0, 1027)
                def f_c0(e, ct=ct):
                    return e.tensor_scalar(convh[:, 1027:1027 + 1024], convh[:, 0:1024],
                                           wcol[:, ct * 4:ct * 4 + 1], None, ALU.mult)
                kb.op("vector", f_c0, r=[XP, "wcol"], w=[("convh", 1027, 2051)])
                for k in range(1, 4):
                    def f_ck(e, ct=ct, k=k):
                        return e.scalar_tensor_tensor(
                            convh[:, 1027:1027 + 1024], convh[:, k:k + 1024],
                            wcol[:, ct * 4 + k:ct * 4 + k + 1],
                            convh[:, 1027:1027 + 1024], ALU.mult, ALU.add)
                    kb.op("vector", f_ck, r=[XP, "wcol", ("convh", 1027, 2051)],
                          w=[("convh", 1027, 2051)])
                if h == 0:
                    def f_si(e, ct=ct):
                        return e.activation(big_a[:, ct * 1024:(ct + 1) * 1024],
                                            convh[:, 1027:1027 + 1024],
                                            AF.Silu, bias=cbcol[:, ct:ct + 1])
                    kb.op("scalar", f_si, r=[("convh", 1027, 2051), "cbcol"],
                          w=[("big_a", ct * 1024, (ct + 1) * 1024)])
                    def f_sp(e, ct=ct):
                        return e.dma_start(out=d_xconv[ct][:, 0:1024],
                                           in_=big_a[:, ct * 1024:(ct + 1) * 1024])
                    kb.op("sync", f_sp, kind="dma", r=[("big_a", ct * 1024, (ct + 1) * 1024)],
                          w=[("d_xconv", ct * 2, ct * 2 + 1)])
                else:
                    def f_si(e, ct=ct):
                        return e.activation(big_b[:, 24064:25088],
                                            convh[:, 1027:1027 + 1024],
                                            AF.Silu, bias=cbcol[:, ct:ct + 1])
                    kb.op("scalar", f_si, r=[("convh", 1027, 2051), "cbcol"],
                          w=[("big_b", 24064, 25088)])
                    def f_sp(e, ct=ct):
                        return e.dma_start(out=d_xconv[ct][:, 1024:2048],
                                           in_=big_b[:, 24064:25088])
                    kb.op("sync", f_sp, kind="dma", r=[("big_b", 24064, 25088)],
                          w=[("d_xconv", ct * 2 + 1, ct * 2 + 2)])
            else:
                if h == 0:
                    def f_spz(e, ct=ct):
                        return e.dma_start(out=d_sz[ct - 16][:, 0:1024],
                                           in_=sm[:, 2048:3072])
                    kb.op("sync", f_spz, kind="dma", r=[("sm", 2048, 3072)],
                          w=[("d_sz", (ct - 16) * 2, (ct - 16) * 2 + 1)])
                else:
                    def f_spz(e, ct=ct):
                        return e.dma_start(out=d_sz[ct - 16][:, 1024:2048],
                                           in_=big_b[:, 24064:25088])
                    kb.op("sync", f_spz, kind="dma", r=[("big_b", 24064, 25088)],
                          w=[("d_sz", (ct - 16) * 2 + 1, (ct - 16) * 2 + 2)])
            if cti + nring - 1 < 32 and cti + nring - 1 >= nring:
                nct = ct_order[cti + nring - 1]
                sl = (cti + nring - 1) % nring
                def f_ldw(e, nct=nct, sl=sl, rbase=rbase):
                    return e.dma_start(
                        out=big_b[:, rbase + sl * 1024:rbase + sl * 1024 + 1024],
                        in_=dw_in[:, nct].rearrange("p dt c -> p (dt c)"))
                kb.op("gpsimd", f_ldw, kind="dma",
                      w=[("big_b", rbase + sl * 1024, rbase + sl * 1024 + 1024)])

        WX_ALL = ("big_b", 0, 16 * 1056)

        REP_STG = {0: 24064, 1: 23040}
        SEL0 = 18432

        repc = [0]

        def rep_unit(which, s):
            bb = 0 if which == 0 else 16384
            for lc2 in range(2):
                repc[0] += 1
                ps, pk = (psC, "psC") if repc[0] % 2 == 0 else (psD, "psD")
                def f_mm(e, which=which, s=s, lc2=lc2, ps=ps):
                    return e.matmul(
                        ps[:, 0:512],
                        big_b[0:16, SEL0 + s * 128:SEL0 + (s + 1) * 128],
                        big_b[0:16, REP_STG[which] + lc2 * 512:REP_STG[which] + lc2 * 512 + 512],
                        start=True, stop=True)
                kb.op("tensor", f_mm,
                      r=[("big_b", SEL0 + s * 128, SEL0 + (s + 1) * 128),
                         ("big_b", REP_STG[which] + lc2 * 512,
                          REP_STG[which] + lc2 * 512 + 512)],
                      w=[pk])
                def f_ev(e, bb=bb, s=s, lc2=lc2, ps=ps):
                    return e.copy(
                        big_a[:, bb + s * HF + lc2 * 512:bb + s * HF + lc2 * 512 + 512],
                        ps[:, 0:512])
                kb.op("scalar", f_ev, r=[pk],
                      w=[("big_a", bb + s * HF + lc2 * 512,
                          bb + s * HF + lc2 * 512 + 512)])

        def emit_p3a_half(h):
            if h == 1:
                for ct in range(16):
                    def f_rx(e, ct=ct):
                        return e.dma_start(out=big_a[:, ct * 1024:(ct + 1) * 1024],
                                           in_=d_xconv[ct][:, 1024:2048])
                    kb.op("sync" if ct % 2 == 0 else "gpsimd", f_rx, kind="dma",
                          r=[("d_xconv", ct * 2 + 1, ct * 2 + 2)],
                          w=[("big_a", ct * 1024, (ct + 1) * 1024)])
            kb.op("gpsimd", lambda e: e.dma_start(out=big_b[0:16, SEL0:SEL0 + 2048],
                                                  in_=dsel), kind="dma",
                  w=[("big_b", SEL0, SEL0 + 2048)])
            for which in (1, 0):
                col0 = 1024 if which == 0 else 1040
                ps, pk = (psD, "psD") if which == 0 else (psC, "psC")
                for lc2 in range(2):
                    for ct in range(16):
                        def f_mm(e, col0=col0, lc2=lc2, ct=ct, ps=ps):
                            return e.matmul(
                                ps[0:16, 0:512],
                                big_b[:, ct * 1056 + col0:ct * 1056 + col0 + 16],
                                big_a[:, ct * 1024 + lc2 * 512:ct * 1024 + lc2 * 512 + 512],
                                start=(ct == 0), stop=(ct == 15))
                        kb.op("tensor", f_mm,
                              r=[WX_ALL, ("big_a", ct * 1024 + lc2 * 512,
                                          ct * 1024 + lc2 * 512 + 512)], w=[pk])
                    def f_st(e, ps=ps, which=which, lc2=lc2):
                        return e.copy(
                            big_b[0:16, REP_STG[which] + lc2 * 512:REP_STG[which] + lc2 * 512 + 512],
                            ps[0:16, 0:512])
                    kb.op("scalar", f_st, r=[pk],
                          w=[("big_b", REP_STG[which] + lc2 * 512,
                              REP_STG[which] + lc2 * 512 + 512)])
            emit_p3b_loads(h, 0)
            base = 8192 if h == 0 else 0
            for dp in range(8):
                for lc2 in range(2):
                    ps, pk = next_ps()
                    for ct in range(16):
                        def f_mm(e, dp=dp, lc2=lc2, ct=ct, ps=ps):
                            return e.matmul(
                                ps[:, 0:512],
                                big_b[:, ct * 1056 + dp * 128:ct * 1056 + dp * 128 + 128],
                                big_a[:, ct * 1024 + lc2 * 512:ct * 1024 + lc2 * 512 + 512],
                                start=(ct == 0), stop=(ct == 15))
                        kb.op("tensor", f_mm,
                              r=[WX_ALL, ("big_a", ct * 1024 + lc2 * 512,
                                          ct * 1024 + lc2 * 512 + 512)], w=[pk])
                    def f_ev(e, dp=dp, lc2=lc2, ps=ps, base=base):
                        return e.copy(
                            arena[:, base + dp * 1024 + lc2 * 512:base + dp * 1024 + lc2 * 512 + 512],
                            ps[:, 0:512])
                    kb.op("scalar", f_ev, r=[pk],
                          w=[("arena", base + dp * 1024 + lc2 * 512,
                              base + dp * 1024 + lc2 * 512 + 512)])
            emit_p3b(h, 0, do_loads=False)
            for s in (0, 1):
                rep_unit(0, s)
                rep_unit(1, s)

        HLn, L0n, L1n, L2n = 16896, 17920, 18944, 19968
        WDTn, YGn = 20992, 23040
        HK = ("big_b", HLn, HLn + 1024)
        LK = [("big_b", c, c + 1024) for c in (L0n, L1n, L2n)]
        HF = 1024


        def emit_p3b_loads(half, it):
            par = it % 2
            WDL = ("big_b", WDTn + par * 1024, WDTn + (par + 1) * 1024)
            def f_lw(e, it=it, par=par):
                return e.dma_start(
                    out=big_b[:, WDTn + par * 1024:WDTn + (par + 1) * 1024],
                    in_=dw_dt[:, it].rearrange("p dt i -> p (dt i)"))
            kb.op("gpsimd", f_lw, kind="dma", w=[WDL])
            def f_lxc(e, it=it, half=half, par=par):
                return e.dma_start(out=sm[:, (2 + par) * 1024:(3 + par) * 1024],
                                   in_=d_xconv[it][:, half * HF:(half + 1) * HF])
            kb.op("sync", f_lxc, kind="dma", r=[("d_xconv", it * 2 + half, it * 2 + half + 1)],
                  w=[("sm", (2 + par) * 1024, (3 + par) * 1024)])
            def f_lsz(e, it=it, half=half, par=par):
                return e.dma_start(out=sm[:, (4 + par) * 1024:(5 + par) * 1024],
                                   in_=d_sz[it][:, half * HF:(half + 1) * HF])
            kb.op("sync", f_lsz, kind="dma", r=[("d_sz", it * 2 + half, it * 2 + half + 1)],
                  w=[("sm", (4 + par) * 1024, (5 + par) * 1024)])

        def emit_p3b(half, it, do_loads=True):
            par = it % 2
            WDL = ("big_b", WDTn + par * 1024, WDTn + (par + 1) * 1024)
            if do_loads:
                emit_p3b_loads(half, it)
            XCL = ("sm", (2 + par) * 1024, (3 + par) * 1024)
            SZL = ("sm", (4 + par) * 1024, (5 + par) * 1024)
            DTB = T3 + par * 1024
            DTK = ("bigf", DTB, DTB + 1024)
            for lc2 in range(2):
                ps, pk = next_ps()
                base_h = 8192 if half == 0 else 0
                for dp in range(8):
                    def f_mm(e, lc2=lc2, dp=dp, ps=ps, par=par, base_h=base_h):
                        return e.matmul(
                            ps[:, 0:512],
                            big_b[:, WDTn + par * 1024 + dp * 128:WDTn + par * 1024 + dp * 128 + 128],
                            arena[:, base_h + dp * 1024 + lc2 * 512:base_h + dp * 1024 + lc2 * 512 + 512],
                            start=(dp == 0), stop=(dp == 7))
                    kb.op("tensor", f_mm, r=[WDL, ("arena", base_h, base_h + 8192)], w=[pk])
                R_ = ("bigf", T2 + 1024 + lc2 * 512, T2 + 1024 + lc2 * 512 + 512)
                A_ = ("bigf", T0 + 1024 + lc2 * 512, T0 + 1024 + lc2 * 512 + 512)
                def f_r(e, it=it, lc2=lc2, ps=ps):
                    return e.activation(
                        bigf[:, T2 + 1024 + lc2 * 512:T2 + 1024 + lc2 * 512 + 512],
                        ps[:, 0:512], AF.Relu, bias=bdtcol[:, it:it + 1])
                kb.op("scalar", f_r, r=[pk, "bdtcol"], w=[R_])
                def f_a(e, it=it, lc2=lc2, ps=ps):
                    return e.activation(
                        bigf[:, T0 + 1024 + lc2 * 512:T0 + 1024 + lc2 * 512 + 512],
                        ps[:, 0:512], AF.Abs, bias=bdtcol[:, it:it + 1])
                kb.op("scalar", f_a, r=[pk, "bdtcol"], w=[A_])
                def f_e(e, lc2=lc2):
                    return e.activation(
                        bigf[:, T0 + 1024 + lc2 * 512:T0 + 1024 + lc2 * 512 + 512],
                        bigf[:, T0 + 1024 + lc2 * 512:T0 + 1024 + lc2 * 512 + 512],
                        AF.Exp, scale=-1.0)
                kb.op("scalar", f_e, r=[A_], w=[A_])
                def f_l(e, lc2=lc2):
                    return e.activation(
                        bigf[:, T0 + 1024 + lc2 * 512:T0 + 1024 + lc2 * 512 + 512],
                        bigf[:, T0 + 1024 + lc2 * 512:T0 + 1024 + lc2 * 512 + 512],
                        AF.Ln, bias=cpool[:, 1:2])
                kb.op("scalar", f_l, r=[A_, "cpool"], w=[A_])
                def f_d(e, lc2=lc2, DTB=DTB):
                    return e.tensor_add(bigf[:, DTB + lc2 * 512:DTB + lc2 * 512 + 512],
                                        bigf[:, T2 + 1024 + lc2 * 512:T2 + 1024 + lc2 * 512 + 512],
                                        bigf[:, T0 + 1024 + lc2 * 512:T0 + 1024 + lc2 * 512 + 512])
                kb.op("vector", f_d, r=[R_, A_],
                      w=[("bigf", DTB + lc2 * 512, DTB + lc2 * 512 + 512)])
            def f_dtx(e, DTB=DTB, par=par):
                return e.tensor_mul(sm[:, par * 1024:(par + 1) * 1024],
                                    bigf[:, DTB:DTB + 1024],
                                    sm[:, (2 + par) * 1024:(3 + par) * 1024])
            kb.op("vector", f_dtx, r=[DTK, XCL], w=[("sm", par * 1024, (par + 1) * 1024)])

        def emit_p4(half, it, lead_reps=False, hooks=None):
            par = it % 2
            DTB = T3 + par * 1024
            DTK = ("bigf", DTB, DTB + 1024)
            DTXL = ("sm", par * 1024, (par + 1) * 1024)
            XCL = ("sm", (2 + par) * 1024, (3 + par) * 1024)
            SZL = ("sm", (4 + par) * 1024, (5 + par) * 1024)
            YA = ("bigf", T2, T2 + 1024)
            for s in range(NS):
                sp = s % 2
                DA = ("bigf", T1 + sp * 1024, T1 + (sp + 1) * 1024)
                def f_da(e, it=it, s=s, sp=sp, DTB=DTB):
                    return e.activation(bigf[:, T1 + sp * 1024:T1 + (sp + 1) * 1024],
                                        bigf[:, DTB:DTB + 1024], AF.Exp,
                                        scale=pa[:, it * 16 + s:it * 16 + s + 1])
                kb.op("scalar", f_da, r=[DTK, ("pa", 0, 256)], w=[DA])
                DBU = ("sm", 6144, 7168)
                def f_dbu(e, s=s, par=par):
                    return e.tensor_mul(sm[:, 6144:7168],
                                        sm[:, par * 1024:(par + 1) * 1024],
                                        big_a[:, s * HF:(s + 1) * HF])
                kb.op("vector", f_dbu, r=[DTXL, ("big_a", s * HF, (s + 1) * HF)],
                      w=[DBU])
                if half == 0:
                    def f_scan(e, sp=sp):
                        return e.tensor_tensor_scan(
                            big_b[:, HLn:HLn + 1024],
                            bigf[:, T1 + sp * 1024:T1 + (sp + 1) * 1024],
                            sm[:, 6144:7168],
                            0.0, ALU.mult, ALU.add)
                    kb.op("vector", f_scan, r=[DA, DBU], w=[HK])
                    ic = it * 16 + s
                    def f_hc(e, ic=ic):
                        return e.tensor_copy(hcol[:, ic:ic + 1],
                                             big_b[:, HLn + 1023:HLn + 1024])
                    kb.op("vector", f_hc, r=[HK], w=[("hcol", ic, ic + 1)])
                else:
                    ic = it * 16 + s
                    def f_scan(e, sp=sp, ic=ic):
                        return e.tensor_tensor_scan(
                            big_b[:, HLn:HLn + 1024],
                            bigf[:, T1 + sp * 1024:T1 + (sp + 1) * 1024],
                            sm[:, 6144:7168],
                            hcol[:, ic:ic + 1], ALU.mult, ALU.add)
                    kb.op("vector", f_scan, r=[DA, DBU, ("hcol", ic, ic + 1)], w=[HK])
                leaf = L0n if s % 2 == 0 else L1n
                def f_ch(e, s=s, leaf=leaf):
                    return e.tensor_mul(big_b[:, leaf:leaf + 1024],
                                        big_b[:, HLn:HLn + 1024],
                                        big_a[:, 16384 + s * HF:16384 + (s + 1) * HF])
                kb.op("vector", f_ch,
                      r=[HK, ("big_a", 16384 + s * HF, 16384 + (s + 1) * HF)],
                      w=[("big_b", leaf, leaf + 1024)])
                if s % 4 == 1:
                    def f_pair(e):
                        return e.tensor_add(big_b[:, L2n:L2n + 1024],
                                            big_b[:, L0n:L0n + 1024],
                                            big_b[:, L1n:L1n + 1024])
                    kb.op("vector", f_pair, r=[LK[0], LK[1]], w=[LK[2]])
                if s % 4 == 3:
                    def f_pair1(e):
                        return e.tensor_add(big_b[:, L0n:L0n + 1024],
                                            big_b[:, L0n:L0n + 1024],
                                            big_b[:, L1n:L1n + 1024])
                    kb.op("vector", f_pair1, r=[LK[0], LK[1]], w=[LK[0]])
                    def f_quad(e):
                        return e.tensor_add(big_b[:, L0n:L0n + 1024],
                                            big_b[:, L2n:L2n + 1024],
                                            big_b[:, L0n:L0n + 1024])
                    kb.op("vector", f_quad, r=[LK[2], LK[0]], w=[LK[0]])
                    if s == 3:
                        kb.op("vector", lambda e: e.tensor_copy(bigf[:, T2:T2 + 1024],
                                                                big_b[:, L0n:L0n + 1024]),
                              r=[LK[0]], w=[YA])
                    else:
                        kb.op("vector", lambda e: e.tensor_add(bigf[:, T2:T2 + 1024],
                                                               bigf[:, T2:T2 + 1024],
                                                               big_b[:, L0n:L0n + 1024]),
                              r=[LK[0], YA], w=[YA])
                if lead_reps and s + 2 < NS:
                    rep_unit(0, s + 2)
                    rep_unit(1, s + 2)
                if hooks and s in hooks:
                    for fn in hooks[s]:
                        fn()
            YF = ("bigf", T0, T0 + 1024)
            def f_yf(e, par=par):
                return e.tensor_add(bigf[:, T0:T0 + 1024], bigf[:, T2:T2 + 1024],
                                    sm[:, (2 + par) * 1024:(3 + par) * 1024])
            kb.op("vector", f_yf, r=[YA, XCL], w=[YF])
            def f_yg(e, par=par):
                return e.tensor_mul(big_b[:, YGn:YGn + 1024], bigf[:, T0:T0 + 1024],
                                    sm[:, (4 + par) * 1024:(5 + par) * 1024])
            kb.op("vector", f_yg, r=[YF, SZL], w=[("big_b", YGn, YGn + 1024)])
            def f_spyg(e, it=it, half=half):
                return e.dma_start(out=d_yg[it][:, half * HF:(half + 1) * HF],
                                   in_=big_b[:, YGn:YGn + 1024])
            kb.op("sync", f_spyg, kind="dma", r=[("big_b", YGn, YGn + 1024)],
                  w=[("d_yg", it * 2 + half, it * 2 + half + 1)])

        ct_order = []
        for k in range(16):
            ct_order += [k, 16 + k]
        emit_p1_ldx(0, 0)
        for j in range(8):
            if j + 1 < 8:
                emit_p1_ldx(0, j + 1)
            emit_p1_lt(0, j, do_load=False)
        kb.op("vector", lambda e: e.memset(convh[:, 0:3], 0.0), w=[("convh", 0, 3)])
        for cti in range(32):
            emit_p2_ct(0, cti, ct_order)
            for _ in range(2):
                if weff_steps:
                    weff_steps.pop(0)()
        while weff_steps:
            weff_steps.pop(0)()
        for ct in range(16):
            def f_lwx(e, ct=ct):
                return e.dma_start(out=big_b[:, ct * 1056:(ct + 1) * 1056],
                                   in_=dw_x[ct * 128:(ct + 1) * 128, :])
            kb.op("gpsimd", f_lwx, kind="dma",
                  w=[("big_b", ct * 1056, (ct + 1) * 1056)])
        emit_p3a_half(0)
        for it in range(NT):
            if it + 1 < NT:
                emit_p3b_loads(0, it + 1)
            if it < 8:
                emit_p1_ldx(1, it)
            hooks = {}
            if it + 1 < NT:
                hooks.setdefault(7, []).append(
                    lambda it=it: emit_p3b(0, it + 1, do_loads=False))
            if it < 8:
                hooks.setdefault(11, []).append(
                    lambda it=it: emit_p1_lt(1, it, do_load=False))
            else:
                for k, cti in enumerate(range((it - 8) * 4, (it - 8) * 4 + 4)):
                    hooks.setdefault(1 + k * 4, []).append(
                        lambda cti=cti: emit_p2_ct(1, cti, ct_order))
            emit_p4(0, it, lead_reps=(it == 0), hooks=hooks)
        def emit_mix_early_loads(lt):
            ab = 8192 + (lt % 4) * 2048
            for it in range(NT):
                def f_ly(e, lt=lt, it=it, ab=ab):
                    return e.dma_start(
                        out=arena[:, ab + it * 128:ab + (it + 1) * 128],
                        in_=d_yg[it][:, lt * 128:(lt + 1) * 128])
                kb.op("gpsimd", f_ly, kind="dma", r=[("d_yg", it * 2, it * 2 + 1)],
                      w=[("arena", ab + it * 128, ab + (it + 1) * 128)])

        def emit_mix_early(lt):
            ab = 8192 + (lt % 4) * 2048
            for n2 in range(2):
                ps, pk = (psC, "psC") if n2 == 0 else (psD, "psD")
                for it in range(NT):
                    def f_mm(e, n2=n2, it=it, ps=ps, ab=ab):
                        return e.matmul(
                            ps[:, 0:512],
                            arena[:, ab + it * 128:ab + (it + 1) * 128],
                            big_b[:, it * 1024 + n2 * 512:it * 1024 + n2 * 512 + 512],
                            start=(it == 0), stop=(it == 15))
                    kb.op("tensor", f_mm,
                          r=[("arena", ab, ab + 2048), ("big_b", 0, 16384)], w=[pk])
                def f_ev(e, n2=n2, ps=ps):
                    return e.copy(big_b[:, 16384 + n2 * 512:16384 + n2 * 512 + 512],
                                  ps[:, 0:512])
                kb.op("scalar", f_ev, r=[pk],
                      w=[("big_b", 16384 + n2 * 512, 16384 + n2 * 512 + 512)])
            def f_sp(e, lt=lt):
                return e.dma_start(out=d_mix[lt], in_=big_b[:, 16384:17408])
            kb.op("sync", f_sp, kind="dma", r=[("big_b", 16384, 17408)],
                  w=[("d_mix", lt, lt + 1)])

        emit_p3a_half(1)
        for it in range(NT):
            if it + 1 < NT:
                emit_p3b_loads(1, it + 1)
            if it < 8:
                for k in (it * 2, it * 2 + 1):
                    def f_lwo(e, k=k):
                        return e.dma_start(out=big_b[:, k * 1024:(k + 1) * 1024],
                                           in_=dw_out[k * 128:(k + 1) * 128, :])
                    kb.op("gpsimd", f_lwo, kind="dma",
                          w=[("big_b", k * 1024, (k + 1) * 1024)])
            if it >= 7 and it < 15:
                emit_mix_early_loads(it - 7)
            hooks = {}
            if it + 1 < NT:
                hooks.setdefault(7, []).append(
                    lambda it=it: emit_p3b(1, it + 1, do_loads=False))
            if it >= 8:
                hooks.setdefault(2, []).append(
                    lambda it=it: emit_mix_early(it - 8))
            emit_p4(1, it, lead_reps=(it == 0), hooks=hooks)

        for lt in (0, 1):
            mb0 = T2 + (lt % 2) * 1024
            def f_lm0(e, lt=lt, mb0=mb0):
                return e.dma_start(out=bigf[:, mb0:mb0 + 1024], in_=d_mix[lt])
            kb.op("gpsimd", f_lm0, kind="dma", r=[("d_mix", lt, lt + 1)],
                  w=[("bigf", mb0, mb0 + 1024)])
        for it in range(NT):
            def f_lyg(e, it=it):
                return e.dma_start(out=big_a[:, it * L + 1024:(it + 1) * L],
                                   in_=d_yg[it][:, 1024:2048])
            kb.op("sync", f_lyg, kind="dma", r=[("d_yg", it * 2 + 1, it * 2 + 2)],
                  w=[("big_a", it * L + 1024, (it + 1) * L)])
        kb.op("gpsimd", lambda e: e.dma_start(out=sm[:, 0:7168], in_=d_weff[:, 0:7168]),
              kind="dma", r=[("d_weff", 0, 7168)], w=["sm"])
        kb.op("gpsimd", lambda e: e.dma_start(out=big_b[:, 24064:25088],
                                              in_=d_weff[:, 7168:8192]),
              kind="dma", r=[("d_weff", 7168, 8192)], w=[("big_b", 24064, 25088)])
        YG_ALL = ("big_a", 0, 16 * L)
        WO_ALL = ("big_b", 0, 16 * 1024)
        WEFF = ("sm", 0, 7168)

        kb.op("sync", lambda e: e.dma_start(out=bigf[0:8, T1:T1 + 1024], in_=dffn_b),
              kind="dma", w=[("bigf", T1, T1 + 1024)])
        kb.op("vector", lambda e: e.tensor_scalar(bigf[0:8, T1:T1 + 1024],
                                                  bigf[0:8, T1:T1 + 1024],
                                                  maskt[:], None, ALU.mult),
              r=[("bigf", T1, T1 + 1024), "maskt"], w=[("bigf", T1, T1 + 1024)])
        kb.op("vector", lambda e: e.tensor_copy(big_b[0:8, 16384:16384 + 1024],
                                                bigf[0:8, T1:T1 + 1024]),
              r=[("bigf", T1, T1 + 1024)], w=[("big_b", 16384, 17408)])
        for n2 in range(2):
            def f_mm(e, n2=n2):
                return e.matmul(psE[0:1, 0:512], ones8[:],
                                big_b[0:8, 16384 + n2 * 512:16384 + n2 * 512 + 512],
                                start=True, stop=True)
            kb.op("tensor", f_mm, r=[("big_b", 16384, 17408), "ones8"], w=["psE"])
            def f_ev(e, n2=n2):
                return e.tensor_copy(beffrow[:, n2 * 512:n2 * 512 + 512], psE[0:1, 0:512])
            kb.op("vector", f_ev, r=["psE"], w=[("big_b", 20480 + n2 * 512, 20480 + n2 * 512 + 512)])
        for n2 in range(2):
            def f_mm(e, n2=n2):
                return e.matmul(psE[:, 0:512], ones1,
                                beffrow[:, n2 * 512:n2 * 512 + 512], start=True,
                                stop=True)
            kb.op("tensor", f_mm, r=[("big_b", 20480, 21504), ("big_b", 27264, 27392)], w=["psE"])
            def f_ev(e, n2=n2):
                return e.tensor_copy(
                    convh[:, n2 * 512:n2 * 512 + 512], psE[:, 0:512])
            kb.op("vector", f_ev, r=["psE"],
                  w=[("convh", n2 * 512, n2 * 512 + 512)])
        BEFF = ("convh", 0, 1024)

        statn = sb("statn", [128, 128], F32)

        def ln_ops(src_lo, dst_ap_fn, dst_w, sb_, sq_lo):
            SRC = ("bigf", src_lo, src_lo + 1024)
            SQ5 = ("bigf", sq_lo, sq_lo + 1024)
            c = lambda a, b: ("statn", sb_ + a, sb_ + b)
            s_ = statn
            kb.op("vector", lambda e: e.reduce_sum(s_[:, sb_:sb_ + 1],
                                                   bigf[:, src_lo:src_lo + 1024],
                                                   axis=AX.X), r=[SRC], w=[c(0, 1)])
            kb.op("scalar", lambda e: e.activation(bigf[:, sq_lo:sq_lo + 1024],
                                                   bigf[:, src_lo:src_lo + 1024],
                                                   AF.Square,
                                                   accum_out=s_[:, sb_ + 1:sb_ + 2]),
                  r=[SRC], w=[SQ5, c(1, 2)])
            kb.op("vector", lambda e: e.tensor_scalar(s_[:, sb_ + 2:sb_ + 3],
                                                      s_[:, sb_:sb_ + 1], 1.0 / D,
                                                      None, ALU.mult),
                  r=[c(0, 1)], w=[c(2, 3)])
            kb.op("vector", lambda e: e.tensor_scalar(s_[:, sb_ + 3:sb_ + 4],
                                                      s_[:, sb_ + 1:sb_ + 2], 1.0 / D,
                                                      None, ALU.mult),
                  r=[c(1, 2)], w=[c(3, 4)])
            kb.op("vector", lambda e: e.tensor_mul(s_[:, sb_ + 4:sb_ + 5],
                                                   s_[:, sb_ + 2:sb_ + 3],
                                                   s_[:, sb_ + 2:sb_ + 3]),
                  r=[c(2, 3)], w=[c(4, 5)])
            kb.op("vector", lambda e: e.tensor_sub(s_[:, sb_ + 5:sb_ + 6],
                                                   s_[:, sb_ + 3:sb_ + 4],
                                                   s_[:, sb_ + 4:sb_ + 5]),
                  r=[c(3, 5)], w=[c(5, 6)])
            kb.op("scalar", lambda e: e.activation(s_[:, sb_ + 6:sb_ + 7],
                                                   s_[:, sb_ + 5:sb_ + 6], AF.Ln,
                                                   bias=cpool[:, 0:1]),
                  r=[c(5, 6), "cpool"], w=[c(6, 7)])
            kb.op("scalar", lambda e: e.activation(s_[:, sb_ + 7:sb_ + 8],
                                                   s_[:, sb_ + 6:sb_ + 7], AF.Exp,
                                                   scale=-0.5),
                  r=[c(6, 7)], w=[c(7, 8)])
            kb.op("vector", lambda e: e.tensor_scalar(s_[:, sb_ + 8:sb_ + 9],
                                                      s_[:, sb_ + 2:sb_ + 3],
                                                      s_[:, sb_ + 7:sb_ + 8], -1.0,
                                                      ALU.mult, ALU.mult),
                  r=[c(2, 3), c(7, 8)], w=[c(8, 9)])
            def f_dst(e):
                return e.activation(dst_ap_fn(), bigf[:, src_lo:src_lo + 1024],
                                    AF.Identity, bias=s_[:, sb_ + 8:sb_ + 9],
                                    scale=s_[:, sb_ + 7:sb_ + 8])
            kb.op("scalar", f_dst, r=[SRC, c(7, 9)], w=[dst_w])


        def emit_mix(lt):
            mb = T2 + (lt % 2) * 1024
            def f_lm(e, lt=lt, mb=mb):
                return e.dma_start(out=bigf[:, mb:mb + 1024], in_=d_mix[lt])
            kb.op("gpsimd", f_lm, kind="dma", r=[("d_mix", lt, lt + 1)],
                  w=[("bigf", mb, mb + 1024)])

        def emit_mix_late(lt):
            for n2 in range(2):
                ps, pk = (psA, "psA") if n2 == 0 else (psB, "psB")
                for it in range(NT):
                    def f_mm(e, lt=lt, n2=n2, it=it, ps=ps):
                        return e.matmul(
                            ps[:, 0:512],
                            big_a[:, it * L + lt * 128:it * L + lt * 128 + 128],
                            big_b[:, it * 1024 + n2 * 512:it * 1024 + n2 * 512 + 512],
                            start=(it == 0), stop=(it == 15))
                    kb.op("tensor", f_mm, r=[YG_ALL, WO_ALL], w=[pk])
                def f_ev(e, n2=n2, ps=ps):
                    return e.copy(big_b[:, 18432 + n2 * 512:18432 + n2 * 512 + 512],
                                  ps[:, 0:512])
                kb.op("scalar", f_ev, r=[pk],
                      w=[("big_b", 18432 + n2 * 512, 18432 + n2 * 512 + 512)])
            def f_sp(e, lt=lt):
                return e.dma_start(out=d_mix[lt], in_=big_b[:, 18432:19456])
            kb.op("sync", f_sp, kind="dma", r=[("big_b", 18432, 19456)],
                  w=[("d_mix", lt, lt + 1)])

        def emit_tail_a(lt):
            p = lt % 2
            mb = T2 + p * 1024
            xs = p * 1024
            x2s = 2048 + p * 1024
            sqs = 6180 + p * 1024
            l2s = 16384 if p == 0 else 25088
            l2t = 17408 if p == 0 else 26112
            def f_ldx(e, lt=lt, xs=xs):
                return e.dma_start(out=bigf[:, xs:xs + 1024],
                                   in_=dx[lt * 128:(lt + 1) * 128, :])
            kb.op("sync", f_ldx, kind="dma", w=[("bigf", xs, xs + 1024)])
            XT5 = ("bigf", xs, xs + 1024)

            sb0 = (lt % 4) * 32
            ln_ops(mb, lambda sqs=sqs: bigf[:, sqs:sqs + 1024],
                   ("bigf", sqs, sqs + 1024), sb0, sq_lo=sqs)
            kb.op("vector", lambda e, xs=xs, x2s=x2s, sqs=sqs: e.tensor_add(
                                                   bigf[:, x2s:x2s + 1024],
                                                   bigf[:, xs:xs + 1024],
                                                   bigf[:, sqs:sqs + 1024]),
                  r=[("bigf", sqs, sqs + 1024), XT5], w=[("bigf", x2s, x2s + 1024)])
            X2 = ("bigf", x2s, x2s + 1024)
            ln_ops(x2s, lambda l2s=l2s: big_b[:, l2s:l2s + 1024],
                   ("big_b", l2s, l2s + 1024), sb0 + 10, sq_lo=sqs)
            for dt_ in range(8):
                bank, bk = (psT, "psT") if dt_ % 2 == 0 else (psU, "psU")
                def f_tr(e, dt_=dt_, bank=bank, l2s=l2s):
                    return e.transpose(bank[:, 0:128],
                                       big_b[:, l2s + dt_ * 128:l2s + dt_ * 128 + 128],
                                       identb)
                kb.op("tensor", f_tr, r=[("big_b", l2s, l2s + 1024), ("big_b", 27136, 27264)],
                      w=[bk])
                def f_cp(e, dt_=dt_, bank=bank, l2t=l2t):
                    return e.copy(
                        big_b[:, l2t + dt_ * 128:l2t + dt_ * 128 + 128],
                        bank[:, 0:128])
                kb.op("scalar", f_cp, r=[bk],
                      w=[("big_b", l2t + dt_ * 128, l2t + dt_ * 128 + 128)])
            LN2T = ("big_b", l2t, l2t + 1024)
            for n2 in range(2):
                if p == 0:
                    ps, pk = (psC, "psC") if n2 == 0 else (psD, "psD")
                else:
                    ps, pk = (psE, "psE") if n2 == 0 else (psV, "psV")
                for dt_ in range(8):
                    if dt_ < 7:
                        rhs_ap = sm[:, dt_ * 1024 + n2 * 512:dt_ * 1024 + n2 * 512 + 512]
                        rhs_k = WEFF
                    else:
                        rhs_ap = big_b[:, 24064 + n2 * 512:24064 + n2 * 512 + 512]
                        rhs_k = ("big_b", 24064, 25088)
                    def f_mm(e, n2=n2, dt_=dt_, ps=ps, rhs_ap=rhs_ap, l2t=l2t):
                        return e.matmul(
                            ps[:, 0:512],
                            big_b[:, l2t + dt_ * 128:l2t + dt_ * 128 + 128],
                            rhs_ap,
                            start=(dt_ == 0), stop=(dt_ == 7))
                    kb.op("tensor", f_mm, r=[LN2T, rhs_k], w=[pk])

        def emit_tail_b(lt):
            p = lt % 2
            xs = p * 1024
            x2s = 2048 + p * 1024
            sqs = 6180 + p * 1024
            sb0 = (lt % 4) * 32
            X2 = ("bigf", x2s, x2s + 1024)
            for n2 in range(2):
                if p == 0:
                    ps, pk = (psC, "psC") if n2 == 0 else (psD, "psD")
                else:
                    ps, pk = (psE, "psE") if n2 == 0 else (psV, "psV")
                def f_ev(e, n2=n2, ps=ps, xs=xs):
                    return e.tensor_add(
                        bigf[:, xs + n2 * 512:xs + n2 * 512 + 512], ps[:, 0:512],
                        convh[:, n2 * 512:n2 * 512 + 512])
                kb.op("vector", f_ev, r=[pk, BEFF],
                      w=[("bigf", xs + n2 * 512, xs + n2 * 512 + 512)])
            ln_ops(xs, lambda sqs=sqs: bigf[:, sqs:sqs + 1024],
                   ("bigf", sqs, sqs + 1024), sb0 + 20, sq_lo=sqs)
            kb.op("vector", lambda e, sqs=sqs, x2s=x2s: e.tensor_add(
                                                   bigf[:, sqs:sqs + 1024],
                                                   bigf[:, x2s:x2s + 1024],
                                                   bigf[:, sqs:sqs + 1024]),
                  r=[("bigf", sqs, sqs + 1024), X2], w=[("bigf", sqs, sqs + 1024)])
            def f_out(e, lt=lt, sqs=sqs):
                return e.dma_start(out=dy[lt * 128:(lt + 1) * 128, :],
                                   in_=bigf[:, sqs:sqs + 1024])
            kb.op("sync", f_out, kind="dma", r=[("bigf", sqs, sqs + 1024)],
                  w=[("dy", lt, lt + 1)], sem="dout")

        emit_tail_a(0)
        for lt in range(NT):
            emit_tail_b(lt)
            if lt < 8:
                emit_mix_late(lt + 8)
            if lt + 2 < NT:
                emit_mix(lt + 2)
            if lt + 1 < NT:
                emit_tail_a(lt + 1)

        counters = kb.resolve()
        with nc.Block() as block:
            kb.emit(block, sems, final_waits=[("dout", counters.get("dout", 0))])

    return nc


_NC_CACHE = [None]


def kernel(**inputs):
    x = np.ascontiguousarray(inputs["L0_attn_inputs"], dtype=np.float32)
    W_in = np.ascontiguousarray(inputs["L0_attn_W_in"], dtype=np.float32)
    conv_w = np.ascontiguousarray(inputs["L0_attn_conv_w"], dtype=np.float32)
    conv_b = np.ascontiguousarray(inputs["L0_attn_conv_b"], dtype=np.float32)
    W_x = np.ascontiguousarray(inputs["L0_attn_W_x"], dtype=np.float32)
    W_dt = np.ascontiguousarray(inputs["L0_attn_W_dt"], dtype=np.float32)
    b_dt = np.ascontiguousarray(inputs["L0_attn_b_dt"], dtype=np.float32)
    A_log = np.ascontiguousarray(inputs["L0_attn_A_log"], dtype=np.float32)
    W_out = np.ascontiguousarray(inputs["L0_attn_W_out"], dtype=np.float32)
    ffn_W = np.ascontiguousarray(inputs["L0_ffn_W"], dtype=np.float32)
    ffn_b = np.ascontiguousarray(inputs["L0_ffn_b"], dtype=np.float32)
    logits = np.ascontiguousarray(inputs["L0_ffn_logits"], dtype=np.float32)
    u = np.ascontiguousarray(inputs["L0_ffn_u"], dtype=np.float32)

    W_in2 = np.ascontiguousarray(
        W_in.reshape(8, 128, 32, 128).transpose(1, 2, 0, 3))
    W_dt2 = np.ascontiguousarray(
        W_dt.reshape(8, 128, 16, 128).transpose(1, 2, 0, 3))
    conv_wt = np.ascontiguousarray(
        conv_w.reshape(4, 16, 128).transpose(2, 1, 0).reshape(128, 64))
    conv_b2 = np.ascontiguousarray(conv_b.reshape(16, 128).T)
    b_dt2 = np.ascontiguousarray(b_dt.reshape(16, 128).T)
    a_log2 = np.ascontiguousarray(
        A_log.reshape(16, 128, 16).transpose(1, 0, 2).reshape(128, 256))
    logits1 = np.ascontiguousarray(logits.reshape(1, 8))
    u1 = np.ascontiguousarray(u.reshape(1, 8))
    ident = np.eye(128, dtype=np.float32)
    sel16 = np.zeros((16, 2048), dtype=np.float32)
    for s in range(16):
        sel16[s, s * 128:(s + 1) * 128] = 1.0

    if _NC_CACHE[0] is None:
        _NC_CACHE[0] = build_nc()
    nc = _NC_CACHE[0]

    shared = {"w_in": W_in2, "w_x": W_x, "w_dt": W_dt2, "w_out": W_out,
              "ffn_w": ffn_W, "ffn_b": ffn_b, "conv_wt": conv_wt,
              "conv_b2": conv_b2, "b_dt2": b_dt2, "a_log2": a_log2,
              "logits1": logits1, "u1": u1, "ident": ident, "sel16": sel16}
    in_maps = [dict(shared, x=np.ascontiguousarray(x[b])) for b in range(NCORES)]
    res = run_bass_kernel_spmd(nc, in_maps, core_ids=list(range(NCORES)))
    return np.stack([np.asarray(res.results[b]["y"], dtype=np.float32)
                     for b in range(NCORES)], axis=0)


# version 22 (final)
# speedup vs baseline: 1.0522x; 1.0081x over previous
import numpy as np
from contextlib import ExitStack
import concourse.bass as bass
import concourse.mybir as mybir
from concourse.bass_utils import run_bass_kernel_spmd

F32 = mybir.dt.float32
BF16 = mybir.dt.bfloat16
AF = mybir.ActivationFunctionType
ALU = mybir.AluOpType
AX = mybir.AxisListType

L = 2048
D = 1024
DI = 2048
NS = 16
NT = 16
NCORES = 8
EPS = 1e-5
DMA_INC = 16


class KB:

    def __init__(self):
        self.plan = []
        self.bufs = {}

    def _norm(self, spec):
        if isinstance(spec, str):
            return (spec, 0, 1 << 30)
        return spec

    PSUM_KEYS = frozenset(["psA", "psB", "psC", "psD", "psE", "psV", "psT", "psU"])

    def op(self, eng, fn, r=(), w=(), kind="compute", sem=None):
        deps = set()
        r = [self._norm(s) for s in r]
        w = [self._norm(s) for s in w]
        w = w + [s for s in r if s[0] in self.PSUM_KEYS]
        r = [s for s in r if s[0] not in self.PSUM_KEYS]
        for (name, lo, hi) in r:
            for ent in self.bufs.get(name, ()):
                if ent[0] < hi and lo < ent[1] and ent[2] is not None:
                    deps.add(ent[2])
        for (name, lo, hi) in w:
            for ent in self.bufs.get(name, ()):
                if ent[0] < hi and lo < ent[1]:
                    if ent[2] is not None:
                        deps.add(ent[2])
                    deps.update(ent[3])
        i = len(self.plan)
        self.plan.append(dict(eng=eng, fn=fn, deps=sorted(deps), kind=kind,
                              sem=sem, marked=False, token=None, waits=None))
        for (name, lo, hi) in r:
            lst = self.bufs.setdefault(name, [])
            for ent in lst:
                if ent[0] < hi and lo < ent[1]:
                    ent[3].add(i)
        for (name, lo, hi) in w:
            lst = self.bufs.setdefault(name, [])
            lst[:] = [ent for ent in lst if not (lo <= ent[0] and ent[1] <= hi)]
            lst.append([lo, hi, i, set()])
        return i

    def _queue(self, o):
        if o["sem"] is not None:
            return o["sem"]
        return "dma_s" if o["eng"] == "sync" else "dma_g"

    def resolve(self):
        POOLS = {"dma_s": ["dsa", "dsb", "dsc", "dsd"],
                 "dma_g": ["dga", "dgb", "dgc"],
                 "dout": ["dout"]}
        semcum = {}
        qcur = {q: None for q in POOLS}
        qopen = {q: [] for q in POOLS}
        qsi = {q: -1 for q in POOLS}
        group_close = {}
        needed = [[] for _ in self.plan]
        prewaits = [[] for _ in self.plan]

        def close_group(q):
            sem = qcur[q]
            if sem is None:
                return
            v = semcum.get(sem, 0)
            for j in qopen[q]:
                group_close[j] = (sem, v)
            qopen[q] = []
            qcur[q] = None

        for i, o in enumerate(self.plan):
            for d in o["deps"]:
                p = self.plan[d]
                if p["kind"] == "compute":
                    p["marked"] = True
                    needed[i].append(d)
                else:
                    q = self._queue(p)
                    if d in qopen[q]:
                        close_group(q)
                    needed[i].append(d)
            if o["kind"] == "dma":
                q = self._queue(o)
                if qcur[q] is None:
                    qsi[q] += 1
                    sem = POOLS[q][qsi[q] % len(POOLS[q])]
                    if semcum.get(sem, 0) > 0:
                        prewaits[i].append((sem, semcum[sem]))
                    qcur[q] = sem
                sem = qcur[q]
                semcum[sem] = semcum.get(sem, 0) + DMA_INC
                o["marked"] = True
                o["token"] = (sem, semcum[sem], DMA_INC)
                qopen[q].append(i)
        for q in POOLS:
            close_group(q)

        counters = dict(semcum)
        for o in self.plan:
            if o["marked"] and o["kind"] == "compute":
                s = o["eng"]
                counters[s] = counters.get(s, 0) + 1
                o["token"] = (s, counters[s], 1)

        marks = {}
        for i, o in enumerate(self.plan):
            wl = {}
            for (s, v) in prewaits[i]:
                if wl.get(s, 0) < v:
                    wl[s] = v
            for d in needed[i]:
                p = self.plan[d]
                if p["kind"] == "dma":
                    s, v = group_close[d]
                else:
                    s, v, _ = p["token"]
                if wl.get(s, 0) < v:
                    wl[s] = v
            final = []
            for s, v in wl.items():
                key = (o["eng"], s)
                if marks.get(key, 0) < v:
                    marks[key] = v
                    final.append((s, v))
            o["waits"] = final
        return counters

    def emit(self, block, sems, final_waits):
        for name in ["sync", "gpsimd", "tensor", "vector", "scalar"]:
            ops = [o for o in self.plan if o["eng"] == name]

            def make(eng_ops, engname):
                def body(eng):
                    for o in eng_ops:
                        for (s, v) in o["waits"]:
                            eng.wait_ge(sems[s], v)
                        inst = o["fn"](eng)
                        if o["marked"]:
                            s, v, inc = o["token"]
                            inst.then_inc(sems[s], inc)
                    if engname == "sync":
                        for (s, v) in final_waits:
                            eng.wait_ge(sems[s], v)
                return body

            getattr(block, name)(make(ops, name))


_PHASES = [7]


def _fin(nc, kb, sems, es, dy, bigf, T0):
    for lt in range(NT):
        def f_out(e, lt=lt):
            return e.dma_start(out=dy[lt * 128:(lt + 1) * 128, :],
                               in_=bigf[:, T0:T0 + 1024])
        kb.op("sync", f_out, kind="dma", r=[("bigf", T0, T0 + 1024)],
              w=[("dy", lt, lt + 1)], sem="dout")
    counters = kb.resolve()
    with nc.Block() as block:
        kb.emit(block, sems, final_waits=[("dout", counters.get("dout", 0))])
    es.close()
    return nc



def build_nc():
    nc = bass.Bass("TRN2", target_bir_lowering=False, debug=False)

    dx = nc.dram_tensor("x", [L, D], F32, kind="ExternalInput").ap()
    dw_in = nc.dram_tensor("w_in", [128, 32, 8, 128], F32, kind="ExternalInput").ap()
    dw_x = nc.dram_tensor("w_x", [DI, D + 2 * NS], F32, kind="ExternalInput").ap()
    dw_dt = nc.dram_tensor("w_dt", [128, 16, 8, 128], F32, kind="ExternalInput").ap()
    dw_out = nc.dram_tensor("w_out", [DI, D], F32, kind="ExternalInput").ap()
    dffn_w = nc.dram_tensor("ffn_w", [8, D, D], F32, kind="ExternalInput").ap()
    dffn_b = nc.dram_tensor("ffn_b", [8, D], F32, kind="ExternalInput").ap()
    dconv_wt = nc.dram_tensor("conv_wt", [128, 64], F32, kind="ExternalInput").ap()
    dconv_b2 = nc.dram_tensor("conv_b2", [128, 16], F32, kind="ExternalInput").ap()
    db_dt2 = nc.dram_tensor("b_dt2", [128, 16], F32, kind="ExternalInput").ap()
    da_log2 = nc.dram_tensor("a_log2", [128, 256], F32, kind="ExternalInput").ap()
    dlogits = nc.dram_tensor("logits1", [1, 8], F32, kind="ExternalInput").ap()
    du1 = nc.dram_tensor("u1", [1, 8], F32, kind="ExternalInput").ap()
    dident = nc.dram_tensor("ident", [128, 128], F32, kind="ExternalInput").ap()
    dsel = nc.dram_tensor("sel16", [NS, 2048], F32, kind="ExternalInput").ap()
    dy = nc.dram_tensor("y", [L, D], F32, kind="ExternalOutput").ap()

    d_xconv = nc.dram_tensor("d_xconv", [NT, 128, L], BF16, kind="Internal").ap()
    d_sz = nc.dram_tensor("d_sz", [NT, 128, L], BF16, kind="Internal").ap()
    d_yg = nc.dram_tensor("d_yg", [NT, 128, L], BF16, kind="Internal").ap()
    d_weff = nc.dram_tensor("d_weff", [128, 8 * D], BF16, kind="Internal").ap()
    d_mix = nc.dram_tensor("d_mix", [16, 128, D], BF16, kind="Internal").ap()
    d_mask = nc.dram_tensor("d_mask", [1, 8], F32, kind="Internal").ap()

    kb = KB()

    with ExitStack() as es:
        sb = lambda n, shp, dt: es.enter_context(nc.sbuf_tensor(n, shp, dt))
        pt = lambda n, shp, dt: es.enter_context(nc.psum_tensor(n, shp, dt))

        arena = sb("arena", [128, 8 * L], BF16)
        big_a = sb("big_a", [128, 16 * L], BF16)
        big_b = sb("big_b", [128, 27392], BF16)
        T0, T1, T2, T3 = 0, 2052, 4132, 6180
        bigf = sb("bigf", [128, 8228], F32)
        bflat = big_b[0:1, 18432:20480]
        identb = big_b[:, 27136:27264]
        ones1 = big_b[0:1, 27264:27392]
        beffrow = big_b[0:1, 20480:21504]
        sm = sb("sm", [128, 7 * 1024], BF16)
        pa = sb("pa", [128, 256], F32)
        wcol = sb("wcol", [128, 64], F32)
        cbcol = sb("cbcol", [128, 16], F32)
        bdtcol = sb("bdtcol", [128, 16], F32)
        ones8 = sb("ones8", [8, 1], BF16)
        lgu = sb("lgu", [1, 32], F32)
        maskr = sb("maskr", [1, 8], F32)
        maskrb = sb("maskrb", [1, 8], BF16)
        maskt = sb("maskt", [8, 1], F32)
        maskrep = sb("maskrep", [128, 8], F32)
        cpool = sb("cpool", [128, 8], F32)
        hcol = sb("hcol", [128, 256], F32)
        convh = sb("convh", [128, 2051], F32)
        tails = sb("tails", [128, 48], F32)

        psA = pt("psA", [128, 512], F32)
        psB = pt("psB", [128, 512], F32)
        psC = pt("psC", [128, 512], F32)
        psD = pt("psD", [128, 512], F32)
        psE = pt("psE", [128, 512], F32)
        psV = pt("psV", [128, 512], F32)
        psT = pt("psT", [128, 1024], BF16)
        psU = pt("psU", [128, 1024], BF16)

        sems = {n: es.enter_context(nc.semaphore(n)) for n in
                ["dsa", "dsb", "dsc", "dsd", "dga", "dgb", "dgc",
                 "tensor", "vector", "scalar", "gpsimd", "dout"]}

        kb.op("vector", lambda e: e.memset(cpool[:, 0:1], EPS), w=["cpool"])
        kb.op("vector", lambda e: e.memset(cpool[:, 1:2], 1.0), w=["cpool"])
        kb.op("sync", lambda e: e.dma_start(out=pa[:, 0:256], in_=da_log2), kind="dma",
              w=[("pa", 0, 256)])
        kb.op("scalar", lambda e: e.activation(pa[:, 0:256], pa[:, 0:256], AF.Exp),
              r=[("pa", 0, 256)], w=[("pa", 0, 256)])
        kb.op("vector", lambda e: e.tensor_scalar(pa[:, 0:256], pa[:, 0:256], -1.0,
                                                  None, ALU.mult),
              r=[("pa", 0, 256)], w=[("pa", 0, 256)])
        kb.op("sync", lambda e: e.dma_start(out=wcol[:], in_=dconv_wt),
              kind="dma", w=["wcol"])
        kb.op("sync", lambda e: e.dma_start(out=cbcol[:], in_=dconv_b2), kind="dma", w=["cbcol"])
        kb.op("sync", lambda e: e.dma_start(out=bdtcol[:], in_=db_dt2), kind="dma", w=["bdtcol"])
        kb.op("gpsimd", lambda e: e.dma_start(out=identb, in_=dident), kind="dma", w=[("big_b", 27136, 27264)])
        kb.op("vector", lambda e: e.memset(ones1, 1.0), w=[("big_b", 27264, 27392)])
        kb.op("vector", lambda e: e.memset(ones8[:], 1.0), w=["ones8"])
        kb.op("vector", lambda e: e.memset(convh[:, 0:3], 0.0), w=[("convh", 0, 3)])

        kb.op("sync", lambda e: e.dma_start(out=lgu[:, 0:8], in_=dlogits), kind="dma",
              w=[("lgu", 0, 8)])
        kb.op("sync", lambda e: e.dma_start(out=lgu[:, 8:16], in_=du1), kind="dma",
              w=[("lgu", 8, 16)])
        kb.op("scalar", lambda e: e.activation(lgu[:, 16:24], lgu[:, 8:16], AF.Ln),
              r=[("lgu", 8, 16)], w=[("lgu", 16, 24)])
        kb.op("vector", lambda e: e.tensor_scalar(lgu[:, 16:24], lgu[:, 16:24], -1.0, None,
                                                  ALU.mult),
              r=[("lgu", 16, 24)], w=[("lgu", 16, 24)])
        kb.op("scalar", lambda e: e.activation(lgu[:, 16:24], lgu[:, 16:24], AF.Ln),
              r=[("lgu", 16, 24)], w=[("lgu", 16, 24)])
        kb.op("vector", lambda e: e.tensor_sub(lgu[:, 24:32], lgu[:, 0:8], lgu[:, 16:24]),
              r=[("lgu", 0, 24)], w=[("lgu", 24, 32)])
        kb.op("vector", lambda e: e.reduce_max(lgu[:, 16:17], lgu[:, 24:32], axis=AX.X),
              r=[("lgu", 24, 32)], w=[("lgu", 16, 17)])
        kb.op("vector", lambda e: e.tensor_scalar(maskr[:], lgu[:, 24:32], lgu[:, 16:17],
                                                  None, ALU.is_ge),
              r=[("lgu", 16, 32)], w=["maskr"])
        kb.op("vector", lambda e: e.tensor_copy(maskrb[:], maskr[:]), r=["maskr"], w=["maskrb"])
        kb.op("sync", lambda e: e.dma_start(out=d_mask, in_=maskr[:]), kind="dma",
              r=["maskr"], w=["d_mask"])
        kb.op("sync", lambda e: e.dma_start(out=maskt[:], in_=d_mask.rearrange("1 k -> k 1")),
              kind="dma", r=["d_mask"], w=["maskt"])
        kb.op("tensor", lambda e: e.matmul(psE[:, 0:8], ones1, maskrb[:], start=True,
                                           stop=True),
              r=[("big_b", 27264, 27392), "maskrb"], w=["psE"])
        kb.op("vector", lambda e: e.tensor_copy(maskrep[:], psE[:, 0:8]), r=["psE"],
              w=["maskrep"])

        MD0 = 24064
        for k in range(8):
            def f_md(e, k=k):
                return e.tensor_scalar(big_b[:, MD0 + k * 128:MD0 + (k + 1) * 128],
                                       identb, maskrep[:, k:k + 1], None, ALU.mult)
            kb.op("vector", f_md,
                  r=[("big_b", 27136, 27264), "maskrep"],
                  w=[("big_b", MD0 + k * 128, MD0 + (k + 1) * 128)])
        weff_steps = []

        def _weff_step(dt_, k):
            def go():
                lane = 4 + k % 2
                def f_load(e, k=k, dt_=dt_, lane=lane):
                    return e.dma_start(out=sm[:, lane * 1024:(lane + 1) * 1024],
                                       in_=dffn_w[k, dt_ * 128:(dt_ + 1) * 128, :])
                kb.op("gpsimd", f_load, kind="dma",
                      w=[("sm", lane * 1024, (lane + 1) * 1024)])
                for n2 in range(2):
                    ps, pk = (psC, "psC") if n2 == 0 else (psD, "psD")
                    def f_mm(e, k=k, n2=n2, ps=ps, lane=lane):
                        return e.matmul(ps[:, 0:512],
                                        big_b[:, MD0 + k * 128:MD0 + (k + 1) * 128],
                                        sm[:, lane * 1024 + n2 * 512:lane * 1024 + n2 * 512 + 512],
                                        start=(k == 0), stop=(k == 7))
                    kb.op("tensor", f_mm,
                          r=[("big_b", MD0 + k * 128, MD0 + (k + 1) * 128),
                             ("sm", lane * 1024, (lane + 1) * 1024)], w=[pk])
                if k == 7:
                    for n2 in range(2):
                        ps, pk = (psC, "psC") if n2 == 0 else (psD, "psD")
                        def f_ev(e, dt_=dt_, n2=n2, ps=ps):
                            return e.copy(
                                sm[:, 6144 + n2 * 512:6144 + n2 * 512 + 512],
                                ps[:, 0:512])
                        kb.op("scalar", f_ev, r=[pk],
                              w=[("sm", 6144 + n2 * 512, 6144 + n2 * 512 + 512)])
                    def f_sp(e, dt_=dt_):
                        return e.dma_start(
                            out=d_weff[:, dt_ * 1024:(dt_ + 1) * 1024],
                            in_=sm[:, 6144:6144 + 1024])
                    kb.op("sync", f_sp, kind="dma", r=[("sm", 6144, 7168)],
                          w=[("d_weff", dt_ * 1024, (dt_ + 1) * 1024)])
            return go

        for dt_ in range(8):
            for k in range(8):
                weff_steps.append(_weff_step(dt_, k))

        pscnt = [0]

        def next_ps():
            pscnt[0] += 1
            return [(psA, "psA"), (psB, "psB"), (psE, "psE")][pscnt[0] % 3]

        LN1T = ("arena", 0, 8192)

        def _p1_bufs(h, j):
            if h == 0:
                p = j % 2
                xt_ap = bigf[:, p * 1024:(p + 1) * 1024]
                XT = ("bigf", p * 1024, (p + 1) * 1024)
                sq_ap = bigf[:, 2048 + p * 1024:2048 + (p + 1) * 1024]
                SQ = ("bigf", 2048 + p * 1024, 2048 + (p + 1) * 1024)
                stg_ap = sm[:, 2048 + p * 1024:2048 + (p + 1) * 1024]
                STG = ("sm", 2048 + p * 1024, 2048 + (p + 1) * 1024)
                sb0 = 104 + p * 10
            else:
                xt_ap = convh[:, 0:1024]
                XT = ("convh", 0, 1024)
                sq_ap = convh[:, 1024:2048]
                SQ = ("convh", 1024, 2048)
                stg_ap = big_b[:, 24064:25088]
                STG = ("big_b", 24064, 25088)
                sb0 = 104 + (j % 2) * 10
            return xt_ap, XT, sq_ap, SQ, stg_ap, STG, sb0

        def emit_p1_ldx(h, j):
            lt = h * 8 + j
            xt_ap, XT = _p1_bufs(h, j)[0:2]
            def f_ldx(e, lt=lt, xt_ap=xt_ap):
                return e.dma_start(out=xt_ap, in_=dx[lt * 128:(lt + 1) * 128, :])
            kb.op("sync", f_ldx, kind="dma", w=[XT])

        def emit_p1_lt(h, j, do_load=True, part=None):
            lt = h * 8 + j
            xt_ap, XT, sq_ap, SQ, stg_ap, STG, sb0 = _p1_bufs(h, j)
            if part == "apply":
                emit_p1_apply(h, j)
                return
            if do_load:
                emit_p1_ldx(h, j)
            kb.op("vector", lambda e: e.reduce_sum(statn[:, sb0:sb0 + 1], xt_ap,
                                                   axis=AX.X), r=[XT], w=[("statn", sb0, sb0 + 1)])
            kb.op("scalar", lambda e: e.activation(sq_ap, xt_ap,
                                                   AF.Square,
                                                   accum_out=statn[:, sb0 + 1:sb0 + 2]),
                  r=[XT], w=[SQ, ("statn", sb0 + 1, sb0 + 2)])
            kb.op("vector", lambda e: e.tensor_scalar(statn[:, sb0 + 2:sb0 + 3], statn[:, sb0:sb0 + 1],
                                                      1.0 / D, None, ALU.mult),
                  r=[("statn", sb0, sb0 + 1)], w=[("statn", sb0 + 2, sb0 + 3)])
            kb.op("vector", lambda e: e.tensor_scalar(statn[:, sb0 + 3:sb0 + 4], statn[:, sb0 + 1:sb0 + 2],
                                                      1.0 / D, None, ALU.mult),
                  r=[("statn", sb0 + 1, sb0 + 2)], w=[("statn", sb0 + 3, sb0 + 4)])
            kb.op("vector", lambda e: e.tensor_mul(statn[:, sb0 + 4:sb0 + 5], statn[:, sb0 + 2:sb0 + 3],
                                                   statn[:, sb0 + 2:sb0 + 3]),
                  r=[("statn", sb0 + 2, sb0 + 3)], w=[("statn", sb0 + 4, sb0 + 5)])
            kb.op("vector", lambda e: e.tensor_sub(statn[:, sb0 + 5:sb0 + 6], statn[:, sb0 + 3:sb0 + 4],
                                                   statn[:, sb0 + 4:sb0 + 5]),
                  r=[("statn", sb0 + 3, sb0 + 5)], w=[("statn", sb0 + 5, sb0 + 6)])
            kb.op("scalar", lambda e: e.activation(statn[:, sb0 + 6:sb0 + 7], statn[:, sb0 + 5:sb0 + 6],
                                                   AF.Ln, bias=cpool[:, 0:1]),
                  r=[("statn", sb0 + 5, sb0 + 6), "cpool"], w=[("statn", sb0 + 6, sb0 + 7)])
            kb.op("scalar", lambda e: e.activation(statn[:, sb0 + 7:sb0 + 8], statn[:, sb0 + 6:sb0 + 7],
                                                   AF.Exp, scale=-0.5),
                  r=[("statn", sb0 + 6, sb0 + 7)], w=[("statn", sb0 + 7, sb0 + 8)])
            kb.op("vector", lambda e: e.tensor_scalar(statn[:, sb0 + 8:sb0 + 9],
                                                      statn[:, sb0 + 2:sb0 + 3],
                                                      statn[:, sb0 + 7:sb0 + 8], -1.0,
                                                      ALU.mult, ALU.mult),
                  r=[("statn", sb0 + 2, sb0 + 3), ("statn", sb0 + 7, sb0 + 8)],
                  w=[("statn", sb0 + 8, sb0 + 9)])
            if part != "stats":
                emit_p1_apply(h, j)

        def emit_p1_apply(h, j):
            xt_ap, XT, sq_ap, SQ, stg_ap, STG, sb0 = _p1_bufs(h, j)
            def f_norm(e, stg_ap=stg_ap, xt_ap=xt_ap):
                return e.activation(stg_ap, xt_ap, AF.Identity,
                                    bias=statn[:, sb0 + 8:sb0 + 9],
                                    scale=statn[:, sb0 + 7:sb0 + 8])
            kb.op("scalar", f_norm, r=[XT, ("statn", sb0 + 7, sb0 + 9)], w=[STG])
            for dt_ in range(8):
                bank, bk = (psT, "psT") if dt_ % 2 == 0 else (psU, "psU")
                def f_tr(e, dt_=dt_, bank=bank, stg_ap=stg_ap):
                    return e.transpose(bank[:, 0:128],
                                       stg_ap[:, dt_ * 128:dt_ * 128 + 128], identb)
                kb.op("tensor", f_tr, r=[STG, ("big_b", 27136, 27264)], w=[bk])
                def f_cp(e, j=j, dt_=dt_, bank=bank):
                    return e.copy(
                        arena[:, dt_ * 1024 + j * 128:dt_ * 1024 + j * 128 + 128],
                        bank[:, 0:128])
                kb.op("scalar", f_cp, r=[bk],
                      w=[("arena", dt_ * 1024 + j * 128, dt_ * 1024 + j * 128 + 128)])

        WINB = 16896

        def emit_p2_ct(h, cti, ct_order):
            ct = ct_order[cti]
            if h == 0:
                nring, rbase = 4, WINB
            else:
                nring, rbase = 2, 25088
            lane = cti % nring
            WI = ("big_b", rbase + lane * 1024, rbase + (lane + 1) * 1024)
            if cti < nring:
                def f_ldw0(e, ct=ct, lane=lane, rbase=rbase):
                    return e.dma_start(
                        out=big_b[:, rbase + lane * 1024:rbase + lane * 1024 + 1024],
                        in_=dw_in[:, ct].rearrange("p dt c -> p (dt c)"))
                kb.op("gpsimd", f_ldw0, kind="dma", w=[WI])
            for lc2 in range(2):
                ps, pk = next_ps()
                for dt_ in range(8):
                    def f_mm(e, lc2=lc2, dt_=dt_, ps=ps, lane=lane, rbase=rbase):
                        return e.matmul(
                            ps[:, 0:512],
                            big_b[:, rbase + lane * 1024 + dt_ * 128:rbase + lane * 1024 + dt_ * 128 + 128],
                            arena[:, dt_ * 1024 + lc2 * 512:dt_ * 1024 + lc2 * 512 + 512],
                            start=(dt_ == 0), stop=(dt_ == 7))
                    kb.op("tensor", f_mm,
                          r=[WI, ("arena", dt_ * 1024 + lc2 * 512,
                                  dt_ * 1024 + lc2 * 512 + 512)], w=[pk])
                if ct < 16:
                    def f_ev(e, lc2=lc2, ps=ps):
                        return e.copy(convh[:, 3 + lc2 * 512:3 + lc2 * 512 + 512],
                                      ps[:, 0:512])
                    kb.op("scalar", f_ev, r=[pk],
                          w=[("convh", 3 + lc2 * 512, 3 + lc2 * 512 + 512)])
                else:
                    if h == 0:
                        zstg, ZS = sm[:, 2048 + lc2 * 512:2048 + lc2 * 512 + 512], \
                            ("sm", 2048 + lc2 * 512, 2048 + lc2 * 512 + 512)
                    else:
                        zstg, ZS = big_b[:, 24064 + lc2 * 512:24064 + lc2 * 512 + 512], \
                            ("big_b", 24064 + lc2 * 512, 24064 + lc2 * 512 + 512)
                    def f_silu(e, ps=ps, zstg=zstg):
                        return e.activation(zstg, ps[:, 0:512], AF.Silu)
                    kb.op("scalar", f_silu, r=[pk], w=[ZS])
            if ct < 16:
                if h == 1:
                    def f_pad(e, ct=ct):
                        return e.tensor_copy(convh[:, 0:3], tails[:, ct * 3:ct * 3 + 3])
                    kb.op("vector", f_pad, r=[("tails", ct * 3, ct * 3 + 3)],
                          w=[("convh", 0, 3)])
                else:
                    def f_tail(e, ct=ct):
                        return e.tensor_copy(tails[:, ct * 3:ct * 3 + 3], convh[:, 1024:1027])
                    kb.op("vector", f_tail, r=[("convh", 1024, 1027)],
                          w=[("tails", ct * 3, ct * 3 + 3)])
                XP = ("convh", 0, 1027)
                def f_c0(e, ct=ct):
                    return e.tensor_scalar(convh[:, 1027:1027 + 1024], convh[:, 0:1024],
                                           wcol[:, ct * 4:ct * 4 + 1], None, ALU.mult)
                kb.op("vector", f_c0, r=[XP, "wcol"], w=[("convh", 1027, 2051)])
                for k in range(1, 4):
                    def f_ck(e, ct=ct, k=k):
                        return e.scalar_tensor_tensor(
                            convh[:, 1027:1027 + 1024], convh[:, k:k + 1024],
                            wcol[:, ct * 4 + k:ct * 4 + k + 1],
                            convh[:, 1027:1027 + 1024], ALU.mult, ALU.add)
                    kb.op("vector", f_ck, r=[XP, "wcol", ("convh", 1027, 2051)],
                          w=[("convh", 1027, 2051)])
                if h == 0:
                    def f_si(e, ct=ct):
                        return e.activation(big_a[:, ct * 1024:(ct + 1) * 1024],
                                            convh[:, 1027:1027 + 1024],
                                            AF.Silu, bias=cbcol[:, ct:ct + 1])
                    kb.op("scalar", f_si, r=[("convh", 1027, 2051), "cbcol"],
                          w=[("big_a", ct * 1024, (ct + 1) * 1024)])
                    def f_sp(e, ct=ct):
                        return e.dma_start(out=d_xconv[ct][:, 0:1024],
                                           in_=big_a[:, ct * 1024:(ct + 1) * 1024])
                    kb.op("sync", f_sp, kind="dma", r=[("big_a", ct * 1024, (ct + 1) * 1024)],
                          w=[("d_xconv", ct * 2, ct * 2 + 1)])
                else:
                    def f_si(e, ct=ct):
                        return e.activation(big_b[:, 24064:25088],
                                            convh[:, 1027:1027 + 1024],
                                            AF.Silu, bias=cbcol[:, ct:ct + 1])
                    kb.op("scalar", f_si, r=[("convh", 1027, 2051), "cbcol"],
                          w=[("big_b", 24064, 25088)])
                    def f_sp(e, ct=ct):
                        return e.dma_start(out=d_xconv[ct][:, 1024:2048],
                                           in_=big_b[:, 24064:25088])
                    kb.op("sync", f_sp, kind="dma", r=[("big_b", 24064, 25088)],
                          w=[("d_xconv", ct * 2 + 1, ct * 2 + 2)])
            else:
                if h == 0:
                    def f_spz(e, ct=ct):
                        return e.dma_start(out=d_sz[ct - 16][:, 0:1024],
                                           in_=sm[:, 2048:3072])
                    kb.op("sync", f_spz, kind="dma", r=[("sm", 2048, 3072)],
                          w=[("d_sz", (ct - 16) * 2, (ct - 16) * 2 + 1)])
                else:
                    def f_spz(e, ct=ct):
                        return e.dma_start(out=d_sz[ct - 16][:, 1024:2048],
                                           in_=big_b[:, 24064:25088])
                    kb.op("sync", f_spz, kind="dma", r=[("big_b", 24064, 25088)],
                          w=[("d_sz", (ct - 16) * 2 + 1, (ct - 16) * 2 + 2)])
            if cti + nring - 1 < 32 and cti + nring - 1 >= nring:
                nct = ct_order[cti + nring - 1]
                sl = (cti + nring - 1) % nring
                def f_ldw(e, nct=nct, sl=sl, rbase=rbase):
                    return e.dma_start(
                        out=big_b[:, rbase + sl * 1024:rbase + sl * 1024 + 1024],
                        in_=dw_in[:, nct].rearrange("p dt c -> p (dt c)"))
                kb.op("gpsimd", f_ldw, kind="dma",
                      w=[("big_b", rbase + sl * 1024, rbase + sl * 1024 + 1024)])

        WX_ALL = ("big_b", 0, 16 * 1056)

        REP_STG = {0: 24064, 1: 23040}
        SEL0 = 18432

        repc = [0]

        def rep_unit(which, s):
            bb = 0 if which == 0 else 16384
            for lc2 in range(2):
                repc[0] += 1
                ps, pk = (psC, "psC") if repc[0] % 2 == 0 else (psD, "psD")
                def f_mm(e, which=which, s=s, lc2=lc2, ps=ps):
                    return e.matmul(
                        ps[:, 0:512],
                        big_b[0:16, SEL0 + s * 128:SEL0 + (s + 1) * 128],
                        big_b[0:16, REP_STG[which] + lc2 * 512:REP_STG[which] + lc2 * 512 + 512],
                        start=True, stop=True)
                kb.op("tensor", f_mm,
                      r=[("big_b", SEL0 + s * 128, SEL0 + (s + 1) * 128),
                         ("big_b", REP_STG[which] + lc2 * 512,
                          REP_STG[which] + lc2 * 512 + 512)],
                      w=[pk])
                def f_ev(e, bb=bb, s=s, lc2=lc2, ps=ps):
                    return e.copy(
                        big_a[:, bb + s * HF + lc2 * 512:bb + s * HF + lc2 * 512 + 512],
                        ps[:, 0:512])
                kb.op("scalar", f_ev, r=[pk],
                      w=[("big_a", bb + s * HF + lc2 * 512,
                          bb + s * HF + lc2 * 512 + 512)])

        def emit_rx(ct):
            def f_rx(e, ct=ct):
                return e.dma_start(out=big_a[:, ct * 1024:(ct + 1) * 1024],
                                   in_=d_xconv[ct][:, 1024:2048])
            kb.op("sync" if ct % 2 == 0 else "gpsimd", f_rx, kind="dma",
                  r=[("d_xconv", ct * 2 + 1, ct * 2 + 2)],
                  w=[("big_a", ct * 1024, (ct + 1) * 1024)])

        def emit_p3a_half(h, skip_reloads=False):
            if h == 1 and not skip_reloads:
                for ct in range(16):
                    emit_rx(ct)
            kb.op("gpsimd", lambda e: e.dma_start(out=big_b[0:16, SEL0:SEL0 + 2048],
                                                  in_=dsel), kind="dma",
                  w=[("big_b", SEL0, SEL0 + 2048)])
            for which in (1, 0):
                col0 = 1024 if which == 0 else 1040
                ps, pk = (psD, "psD") if which == 0 else (psC, "psC")
                for lc2 in range(2):
                    for ct in range(16):
                        def f_mm(e, col0=col0, lc2=lc2, ct=ct, ps=ps):
                            return e.matmul(
                                ps[0:16, 0:512],
                                big_b[:, ct * 1056 + col0:ct * 1056 + col0 + 16],
                                big_a[:, ct * 1024 + lc2 * 512:ct * 1024 + lc2 * 512 + 512],
                                start=(ct == 0), stop=(ct == 15))
                        kb.op("tensor", f_mm,
                              r=[WX_ALL, ("big_a", ct * 1024 + lc2 * 512,
                                          ct * 1024 + lc2 * 512 + 512)], w=[pk])
                    def f_st(e, ps=ps, which=which, lc2=lc2):
                        return e.copy(
                            big_b[0:16, REP_STG[which] + lc2 * 512:REP_STG[which] + lc2 * 512 + 512],
                            ps[0:16, 0:512])
                    kb.op("scalar", f_st, r=[pk],
                          w=[("big_b", REP_STG[which] + lc2 * 512,
                              REP_STG[which] + lc2 * 512 + 512)])
            emit_p3b_loads(h, 0)
            base = 8192 if h == 0 else 0
            for dp in range(8):
                for lc2 in range(2):
                    ps, pk = next_ps()
                    for ct in range(16):
                        def f_mm(e, dp=dp, lc2=lc2, ct=ct, ps=ps):
                            return e.matmul(
                                ps[:, 0:512],
                                big_b[:, ct * 1056 + dp * 128:ct * 1056 + dp * 128 + 128],
                                big_a[:, ct * 1024 + lc2 * 512:ct * 1024 + lc2 * 512 + 512],
                                start=(ct == 0), stop=(ct == 15))
                        kb.op("tensor", f_mm,
                              r=[WX_ALL, ("big_a", ct * 1024 + lc2 * 512,
                                          ct * 1024 + lc2 * 512 + 512)], w=[pk])
                    def f_ev(e, dp=dp, lc2=lc2, ps=ps, base=base):
                        return e.copy(
                            arena[:, base + dp * 1024 + lc2 * 512:base + dp * 1024 + lc2 * 512 + 512],
                            ps[:, 0:512])
                    kb.op("scalar", f_ev, r=[pk],
                          w=[("arena", base + dp * 1024 + lc2 * 512,
                              base + dp * 1024 + lc2 * 512 + 512)])
            emit_p3b(h, 0, do_loads=False)
            for s in (0, 1):
                rep_unit(0, s)
                rep_unit(1, s)

        HLn, L0n, L1n, L2n = 16896, 17920, 18944, 19968
        WDTn, YGn = 20992, 23040
        HK = ("big_b", HLn, HLn + 1024)
        LK = [("big_b", c, c + 1024) for c in (L0n, L1n, L2n)]
        HF = 1024


        def emit_p3b_loads(half, it):
            par = it % 2
            WDL = ("big_b", WDTn + par * 1024, WDTn + (par + 1) * 1024)
            def f_lw(e, it=it, par=par):
                return e.dma_start(
                    out=big_b[:, WDTn + par * 1024:WDTn + (par + 1) * 1024],
                    in_=dw_dt[:, it].rearrange("p dt i -> p (dt i)"))
            kb.op("gpsimd", f_lw, kind="dma", w=[WDL])
            def f_lxc(e, it=it, half=half, par=par):
                return e.dma_start(out=sm[:, (2 + par) * 1024:(3 + par) * 1024],
                                   in_=d_xconv[it][:, half * HF:(half + 1) * HF])
            kb.op("sync", f_lxc, kind="dma", r=[("d_xconv", it * 2 + half, it * 2 + half + 1)],
                  w=[("sm", (2 + par) * 1024, (3 + par) * 1024)])
            def f_lsz(e, it=it, half=half, par=par):
                return e.dma_start(out=sm[:, (4 + par) * 1024:(5 + par) * 1024],
                                   in_=d_sz[it][:, half * HF:(half + 1) * HF])
            kb.op("sync", f_lsz, kind="dma", r=[("d_sz", it * 2 + half, it * 2 + half + 1)],
                  w=[("sm", (4 + par) * 1024, (5 + par) * 1024)])

        def emit_p3b(half, it, do_loads=True):
            par = it % 2
            WDL = ("big_b", WDTn + par * 1024, WDTn + (par + 1) * 1024)
            if do_loads:
                emit_p3b_loads(half, it)
            XCL = ("sm", (2 + par) * 1024, (3 + par) * 1024)
            SZL = ("sm", (4 + par) * 1024, (5 + par) * 1024)
            DTB = T3 + par * 1024
            DTK = ("bigf", DTB, DTB + 1024)
            for lc2 in range(2):
                ps, pk = next_ps()
                base_h = 8192 if half == 0 else 0
                for dp in range(8):
                    def f_mm(e, lc2=lc2, dp=dp, ps=ps, par=par, base_h=base_h):
                        return e.matmul(
                            ps[:, 0:512],
                            big_b[:, WDTn + par * 1024 + dp * 128:WDTn + par * 1024 + dp * 128 + 128],
                            arena[:, base_h + dp * 1024 + lc2 * 512:base_h + dp * 1024 + lc2 * 512 + 512],
                            start=(dp == 0), stop=(dp == 7))
                    kb.op("tensor", f_mm, r=[WDL, ("arena", base_h, base_h + 8192)], w=[pk])
                R_ = ("bigf", T2 + 1024 + lc2 * 512, T2 + 1024 + lc2 * 512 + 512)
                A_ = ("bigf", T0 + 1024 + lc2 * 512, T0 + 1024 + lc2 * 512 + 512)
                def f_r(e, it=it, lc2=lc2, ps=ps):
                    return e.activation(
                        bigf[:, T2 + 1024 + lc2 * 512:T2 + 1024 + lc2 * 512 + 512],
                        ps[:, 0:512], AF.Relu, bias=bdtcol[:, it:it + 1])
                kb.op("scalar", f_r, r=[pk, "bdtcol"], w=[R_])
                def f_a(e, it=it, lc2=lc2, ps=ps):
                    return e.activation(
                        bigf[:, T0 + 1024 + lc2 * 512:T0 + 1024 + lc2 * 512 + 512],
                        ps[:, 0:512], AF.Abs, bias=bdtcol[:, it:it + 1])
                kb.op("scalar", f_a, r=[pk, "bdtcol"], w=[A_])
                def f_e(e, lc2=lc2):
                    return e.activation(
                        bigf[:, T0 + 1024 + lc2 * 512:T0 + 1024 + lc2 * 512 + 512],
                        bigf[:, T0 + 1024 + lc2 * 512:T0 + 1024 + lc2 * 512 + 512],
                        AF.Exp, scale=-1.0)
                kb.op("scalar", f_e, r=[A_], w=[A_])
                def f_l(e, lc2=lc2):
                    return e.activation(
                        bigf[:, T0 + 1024 + lc2 * 512:T0 + 1024 + lc2 * 512 + 512],
                        bigf[:, T0 + 1024 + lc2 * 512:T0 + 1024 + lc2 * 512 + 512],
                        AF.Ln, bias=cpool[:, 1:2])
                kb.op("scalar", f_l, r=[A_, "cpool"], w=[A_])
                def f_d(e, lc2=lc2, DTB=DTB):
                    return e.tensor_add(bigf[:, DTB + lc2 * 512:DTB + lc2 * 512 + 512],
                                        bigf[:, T2 + 1024 + lc2 * 512:T2 + 1024 + lc2 * 512 + 512],
                                        bigf[:, T0 + 1024 + lc2 * 512:T0 + 1024 + lc2 * 512 + 512])
                kb.op("vector", f_d, r=[R_, A_],
                      w=[("bigf", DTB + lc2 * 512, DTB + lc2 * 512 + 512)])
            def f_dtx(e, DTB=DTB, par=par):
                return e.tensor_mul(sm[:, par * 1024:(par + 1) * 1024],
                                    bigf[:, DTB:DTB + 1024],
                                    sm[:, (2 + par) * 1024:(3 + par) * 1024])
            kb.op("vector", f_dtx, r=[DTK, XCL], w=[("sm", par * 1024, (par + 1) * 1024)])

        def emit_p4(half, it, lead_reps=False, hooks=None):
            par = it % 2
            DTB = T3 + par * 1024
            DTK = ("bigf", DTB, DTB + 1024)
            DTXL = ("sm", par * 1024, (par + 1) * 1024)
            XCL = ("sm", (2 + par) * 1024, (3 + par) * 1024)
            SZL = ("sm", (4 + par) * 1024, (5 + par) * 1024)
            YA = ("bigf", T2, T2 + 1024)
            for s in range(NS):
                sp = s % 2
                DA = ("bigf", T1 + sp * 1024, T1 + (sp + 1) * 1024)
                def f_da(e, it=it, s=s, sp=sp, DTB=DTB):
                    return e.activation(bigf[:, T1 + sp * 1024:T1 + (sp + 1) * 1024],
                                        bigf[:, DTB:DTB + 1024], AF.Exp,
                                        scale=pa[:, it * 16 + s:it * 16 + s + 1])
                kb.op("scalar", f_da, r=[DTK, ("pa", 0, 256)], w=[DA])
                DBU = ("sm", 6144, 7168)
                def f_dbu(e, s=s, par=par):
                    return e.tensor_mul(sm[:, 6144:7168],
                                        sm[:, par * 1024:(par + 1) * 1024],
                                        big_a[:, s * HF:(s + 1) * HF])
                kb.op("vector", f_dbu, r=[DTXL, ("big_a", s * HF, (s + 1) * HF)],
                      w=[DBU])
                if half == 0:
                    def f_scan(e, sp=sp):
                        return e.tensor_tensor_scan(
                            big_b[:, HLn:HLn + 1024],
                            bigf[:, T1 + sp * 1024:T1 + (sp + 1) * 1024],
                            sm[:, 6144:7168],
                            0.0, ALU.mult, ALU.add)
                    kb.op("vector", f_scan, r=[DA, DBU], w=[HK])
                    ic = it * 16 + s
                    def f_hc(e, ic=ic):
                        return e.tensor_copy(hcol[:, ic:ic + 1],
                                             big_b[:, HLn + 1023:HLn + 1024])
                    kb.op("vector", f_hc, r=[HK], w=[("hcol", ic, ic + 1)])
                else:
                    ic = it * 16 + s
                    def f_scan(e, sp=sp, ic=ic):
                        return e.tensor_tensor_scan(
                            big_b[:, HLn:HLn + 1024],
                            bigf[:, T1 + sp * 1024:T1 + (sp + 1) * 1024],
                            sm[:, 6144:7168],
                            hcol[:, ic:ic + 1], ALU.mult, ALU.add)
                    kb.op("vector", f_scan, r=[DA, DBU, ("hcol", ic, ic + 1)], w=[HK])
                leaf = L0n if s % 2 == 0 else L1n
                def f_ch(e, s=s, leaf=leaf):
                    return e.tensor_mul(big_b[:, leaf:leaf + 1024],
                                        big_b[:, HLn:HLn + 1024],
                                        big_a[:, 16384 + s * HF:16384 + (s + 1) * HF])
                kb.op("vector", f_ch,
                      r=[HK, ("big_a", 16384 + s * HF, 16384 + (s + 1) * HF)],
                      w=[("big_b", leaf, leaf + 1024)])
                if s % 4 == 1:
                    def f_pair(e):
                        return e.tensor_add(big_b[:, L2n:L2n + 1024],
                                            big_b[:, L0n:L0n + 1024],
                                            big_b[:, L1n:L1n + 1024])
                    kb.op("vector", f_pair, r=[LK[0], LK[1]], w=[LK[2]])
                if s % 4 == 3:
                    def f_pair1(e):
                        return e.tensor_add(big_b[:, L0n:L0n + 1024],
                                            big_b[:, L0n:L0n + 1024],
                                            big_b[:, L1n:L1n + 1024])
                    kb.op("vector", f_pair1, r=[LK[0], LK[1]], w=[LK[0]])
                    def f_quad(e):
                        return e.tensor_add(big_b[:, L0n:L0n + 1024],
                                            big_b[:, L2n:L2n + 1024],
                                            big_b[:, L0n:L0n + 1024])
                    kb.op("vector", f_quad, r=[LK[2], LK[0]], w=[LK[0]])
                    if s == 3:
                        kb.op("vector", lambda e: e.tensor_copy(bigf[:, T2:T2 + 1024],
                                                                big_b[:, L0n:L0n + 1024]),
                              r=[LK[0]], w=[YA])
                    else:
                        kb.op("vector", lambda e: e.tensor_add(bigf[:, T2:T2 + 1024],
                                                               bigf[:, T2:T2 + 1024],
                                                               big_b[:, L0n:L0n + 1024]),
                              r=[LK[0], YA], w=[YA])
                if lead_reps and s + 2 < NS:
                    rep_unit(0, s + 2)
                    rep_unit(1, s + 2)
                if hooks and s in hooks:
                    for fn in hooks[s]:
                        fn()
            YF = ("bigf", T0, T0 + 1024)
            def f_yf(e, par=par):
                return e.tensor_add(bigf[:, T0:T0 + 1024], bigf[:, T2:T2 + 1024],
                                    sm[:, (2 + par) * 1024:(3 + par) * 1024])
            kb.op("vector", f_yf, r=[YA, XCL], w=[YF])
            def f_yg(e, par=par):
                return e.tensor_mul(big_b[:, YGn:YGn + 1024], bigf[:, T0:T0 + 1024],
                                    sm[:, (4 + par) * 1024:(5 + par) * 1024])
            kb.op("vector", f_yg, r=[YF, SZL], w=[("big_b", YGn, YGn + 1024)])
            def f_spyg(e, it=it, half=half):
                return e.dma_start(out=d_yg[it][:, half * HF:(half + 1) * HF],
                                   in_=big_b[:, YGn:YGn + 1024])
            kb.op("sync", f_spyg, kind="dma", r=[("big_b", YGn, YGn + 1024)],
                  w=[("d_yg", it * 2 + half, it * 2 + half + 1)])

        ct_order = []
        for k in range(16):
            ct_order += [k, 16 + k]
        emit_p1_ldx(0, 0)
        for j in range(8):
            if j + 1 < 8:
                emit_p1_ldx(0, j + 1)
            emit_p1_lt(0, j, do_load=False)
        kb.op("vector", lambda e: e.memset(convh[:, 0:3], 0.0), w=[("convh", 0, 3)])
        for cti in range(32):
            emit_p2_ct(0, cti, ct_order)
            for _ in range(2):
                if weff_steps:
                    weff_steps.pop(0)()
        while weff_steps:
            weff_steps.pop(0)()
        for ct in range(16):
            def f_lwx(e, ct=ct):
                return e.dma_start(out=big_b[:, ct * 1056:(ct + 1) * 1056],
                                   in_=dw_x[ct * 128:(ct + 1) * 128, :])
            kb.op("gpsimd", f_lwx, kind="dma",
                  w=[("big_b", ct * 1056, (ct + 1) * 1056)])
        emit_p3a_half(0)
        for it in range(NT):
            if it + 1 < NT:
                emit_p3b_loads(0, it + 1)
            if it < 8:
                emit_p1_ldx(1, it)
            hooks = {}
            if it + 1 < NT:
                hooks.setdefault(7, []).append(
                    lambda it=it: emit_p3b(0, it + 1, do_loads=False))
            if it < 8:
                hooks.setdefault(3, []).append(
                    lambda it=it: emit_p1_lt(1, it, do_load=False, part="stats"))
                hooks.setdefault(8, []).append(
                    lambda it=it: emit_p1_apply(1, it))
            else:
                for k, cti in enumerate(range((it - 8) * 4, (it - 8) * 4 + 4)):
                    hooks.setdefault(1 + k * 4, []).append(
                        lambda cti=cti: emit_p2_ct(1, cti, ct_order))
            if it == 15:
                for ct in range(16):
                    hooks.setdefault(ct, []).append(
                        lambda ct=ct: emit_rx(ct))
            emit_p4(0, it, lead_reps=(it == 0), hooks=hooks)
        def emit_mix_early_loads(lt):
            ab = 8192 + (lt % 4) * 2048
            for it in range(NT):
                def f_ly(e, lt=lt, it=it, ab=ab):
                    return e.dma_start(
                        out=arena[:, ab + it * 128:ab + (it + 1) * 128],
                        in_=d_yg[it][:, lt * 128:(lt + 1) * 128])
                kb.op("gpsimd", f_ly, kind="dma", r=[("d_yg", it * 2, it * 2 + 1)],
                      w=[("arena", ab + it * 128, ab + (it + 1) * 128)])

        def emit_mix_early(lt):
            ab = 8192 + (lt % 4) * 2048
            for n2 in range(2):
                ps, pk = (psC, "psC") if n2 == 0 else (psD, "psD")
                for it in range(NT):
                    def f_mm(e, n2=n2, it=it, ps=ps, ab=ab):
                        return e.matmul(
                            ps[:, 0:512],
                            arena[:, ab + it * 128:ab + (it + 1) * 128],
                            big_b[:, it * 1024 + n2 * 512:it * 1024 + n2 * 512 + 512],
                            start=(it == 0), stop=(it == 15))
                    kb.op("tensor", f_mm,
                          r=[("arena", ab, ab + 2048), ("big_b", 0, 16384)], w=[pk])
                def f_ev(e, n2=n2, ps=ps):
                    return e.copy(big_b[:, 16384 + n2 * 512:16384 + n2 * 512 + 512],
                                  ps[:, 0:512])
                kb.op("scalar", f_ev, r=[pk],
                      w=[("big_b", 16384 + n2 * 512, 16384 + n2 * 512 + 512)])
            def f_sp(e, lt=lt):
                return e.dma_start(out=d_mix[lt], in_=big_b[:, 16384:17408])
            kb.op("sync", f_sp, kind="dma", r=[("big_b", 16384, 17408)],
                  w=[("d_mix", lt, lt + 1)])

        def emit_lyg(it):
            def f_lyg(e, it=it):
                return e.dma_start(out=big_a[:, it * L + 1024:(it + 1) * L],
                                   in_=d_yg[it][:, 1024:2048])
            kb.op("sync", f_lyg, kind="dma", r=[("d_yg", it * 2 + 1, it * 2 + 2)],
                  w=[("big_a", it * L + 1024, (it + 1) * L)])

        def emit_weff_lane(k):
            def f_lw(e, k=k):
                return e.dma_start(out=sm[:, k * 1024:(k + 1) * 1024],
                                   in_=d_weff[:, k * 1024:(k + 1) * 1024])
            kb.op("gpsimd", f_lw, kind="dma",
                  r=[("d_weff", k * 1024, (k + 1) * 1024)],
                  w=[("sm", k * 1024, (k + 1) * 1024)])

        emit_p3a_half(1, skip_reloads=True)
        for it in range(NT):
            if it + 1 < NT:
                emit_p3b_loads(1, it + 1)
            if it < 8:
                for k in (it * 2, it * 2 + 1):
                    def f_lwo(e, k=k):
                        return e.dma_start(out=big_b[:, k * 1024:(k + 1) * 1024],
                                           in_=dw_out[k * 128:(k + 1) * 128, :])
                    kb.op("gpsimd", f_lwo, kind="dma",
                          w=[("big_b", k * 1024, (k + 1) * 1024)])
            if it >= 7 and it < 15:
                emit_mix_early_loads(it - 7)
            hooks = {}
            if it + 1 < NT:
                hooks.setdefault(7, []).append(
                    lambda it=it: emit_p3b(1, it + 1, do_loads=False))
            if it >= 8:
                hooks.setdefault(2, []).append(
                    lambda it=it: emit_mix_early(it - 8))
            if it == 15:
                for jt in range(7):
                    hooks.setdefault(2 * jt + 2, []).append(
                        lambda jt=jt: emit_lyg(jt))
                for jt in range(8, 15):
                    hooks.setdefault(2 * (jt - 8) + 2, []).append(
                        lambda jt=jt: emit_lyg(jt))
                for k in (0, 2, 4):
                    hooks.setdefault(3 + k, []).append(
                        lambda k=k: emit_weff_lane(k))
            emit_p4(1, it, lead_reps=(it == 0), hooks=hooks)

        for lt in (0, 1):
            mb0 = T2 + (lt % 2) * 1024
            def f_lm0(e, lt=lt, mb0=mb0):
                return e.dma_start(out=bigf[:, mb0:mb0 + 1024], in_=d_mix[lt])
            kb.op("gpsimd", f_lm0, kind="dma", r=[("d_mix", lt, lt + 1)],
                  w=[("bigf", mb0, mb0 + 1024)])
        for it in (7, 15):
            emit_lyg(it)
        for k in (1, 3, 5, 6):
            emit_weff_lane(k)
        kb.op("gpsimd", lambda e: e.dma_start(out=big_b[:, 24064:25088],
                                              in_=d_weff[:, 7168:8192]),
              kind="dma", r=[("d_weff", 7168, 8192)], w=[("big_b", 24064, 25088)])
        YG_ALL = ("big_a", 0, 16 * L)
        WO_ALL = ("big_b", 0, 16 * 1024)
        WEFF = ("sm", 0, 7168)

        kb.op("sync", lambda e: e.dma_start(out=bigf[0:8, T1:T1 + 1024], in_=dffn_b),
              kind="dma", w=[("bigf", T1, T1 + 1024)])
        kb.op("vector", lambda e: e.tensor_scalar(bigf[0:8, T1:T1 + 1024],
                                                  bigf[0:8, T1:T1 + 1024],
                                                  maskt[:], None, ALU.mult),
              r=[("bigf", T1, T1 + 1024), "maskt"], w=[("bigf", T1, T1 + 1024)])
        kb.op("vector", lambda e: e.tensor_copy(big_b[0:8, 16384:16384 + 1024],
                                                bigf[0:8, T1:T1 + 1024]),
              r=[("bigf", T1, T1 + 1024)], w=[("big_b", 16384, 17408)])
        for n2 in range(2):
            def f_mm(e, n2=n2):
                return e.matmul(psE[0:1, 0:512], ones8[:],
                                big_b[0:8, 16384 + n2 * 512:16384 + n2 * 512 + 512],
                                start=True, stop=True)
            kb.op("tensor", f_mm, r=[("big_b", 16384, 17408), "ones8"], w=["psE"])
            def f_ev(e, n2=n2):
                return e.tensor_copy(beffrow[:, n2 * 512:n2 * 512 + 512], psE[0:1, 0:512])
            kb.op("vector", f_ev, r=["psE"], w=[("big_b", 20480 + n2 * 512, 20480 + n2 * 512 + 512)])
        for n2 in range(2):
            def f_mm(e, n2=n2):
                return e.matmul(psE[:, 0:512], ones1,
                                beffrow[:, n2 * 512:n2 * 512 + 512], start=True,
                                stop=True)
            kb.op("tensor", f_mm, r=[("big_b", 20480, 21504), ("big_b", 27264, 27392)], w=["psE"])
            def f_ev(e, n2=n2):
                return e.tensor_copy(
                    convh[:, n2 * 512:n2 * 512 + 512], psE[:, 0:512])
            kb.op("vector", f_ev, r=["psE"],
                  w=[("convh", n2 * 512, n2 * 512 + 512)])
        BEFF = ("convh", 0, 1024)

        statn = sb("statn", [128, 128], F32)

        def ln_ops(src_lo, dst_ap_fn, dst_w, sb_, sq_lo):
            SRC = ("bigf", src_lo, src_lo + 1024)
            SQ5 = ("bigf", sq_lo, sq_lo + 1024)
            c = lambda a, b: ("statn", sb_ + a, sb_ + b)
            s_ = statn
            kb.op("vector", lambda e: e.reduce_sum(s_[:, sb_:sb_ + 1],
                                                   bigf[:, src_lo:src_lo + 1024],
                                                   axis=AX.X), r=[SRC], w=[c(0, 1)])
            kb.op("scalar", lambda e: e.activation(bigf[:, sq_lo:sq_lo + 1024],
                                                   bigf[:, src_lo:src_lo + 1024],
                                                   AF.Square,
                                                   accum_out=s_[:, sb_ + 1:sb_ + 2]),
                  r=[SRC], w=[SQ5, c(1, 2)])
            kb.op("vector", lambda e: e.tensor_scalar(s_[:, sb_ + 2:sb_ + 3],
                                                      s_[:, sb_:sb_ + 1], 1.0 / D,
                                                      None, ALU.mult),
                  r=[c(0, 1)], w=[c(2, 3)])
            kb.op("vector", lambda e: e.tensor_scalar(s_[:, sb_ + 3:sb_ + 4],
                                                      s_[:, sb_ + 1:sb_ + 2], 1.0 / D,
                                                      None, ALU.mult),
                  r=[c(1, 2)], w=[c(3, 4)])
            kb.op("vector", lambda e: e.tensor_mul(s_[:, sb_ + 4:sb_ + 5],
                                                   s_[:, sb_ + 2:sb_ + 3],
                                                   s_[:, sb_ + 2:sb_ + 3]),
                  r=[c(2, 3)], w=[c(4, 5)])
            kb.op("vector", lambda e: e.tensor_sub(s_[:, sb_ + 5:sb_ + 6],
                                                   s_[:, sb_ + 3:sb_ + 4],
                                                   s_[:, sb_ + 4:sb_ + 5]),
                  r=[c(3, 5)], w=[c(5, 6)])
            kb.op("scalar", lambda e: e.activation(s_[:, sb_ + 6:sb_ + 7],
                                                   s_[:, sb_ + 5:sb_ + 6], AF.Ln,
                                                   bias=cpool[:, 0:1]),
                  r=[c(5, 6), "cpool"], w=[c(6, 7)])
            kb.op("scalar", lambda e: e.activation(s_[:, sb_ + 7:sb_ + 8],
                                                   s_[:, sb_ + 6:sb_ + 7], AF.Exp,
                                                   scale=-0.5),
                  r=[c(6, 7)], w=[c(7, 8)])
            kb.op("vector", lambda e: e.tensor_scalar(s_[:, sb_ + 8:sb_ + 9],
                                                      s_[:, sb_ + 2:sb_ + 3],
                                                      s_[:, sb_ + 7:sb_ + 8], -1.0,
                                                      ALU.mult, ALU.mult),
                  r=[c(2, 3), c(7, 8)], w=[c(8, 9)])
            def f_dst(e):
                return e.activation(dst_ap_fn(), bigf[:, src_lo:src_lo + 1024],
                                    AF.Identity, bias=s_[:, sb_ + 8:sb_ + 9],
                                    scale=s_[:, sb_ + 7:sb_ + 8])
            kb.op("scalar", f_dst, r=[SRC, c(7, 9)], w=[dst_w])


        def emit_mix(lt):
            mb = T2 + (lt % 2) * 1024
            def f_lm(e, lt=lt, mb=mb):
                return e.dma_start(out=bigf[:, mb:mb + 1024], in_=d_mix[lt])
            kb.op("gpsimd", f_lm, kind="dma", r=[("d_mix", lt, lt + 1)],
                  w=[("bigf", mb, mb + 1024)])

        def emit_mix_late(lt):
            for n2 in range(2):
                ps, pk = (psA, "psA") if n2 == 0 else (psB, "psB")
                for it in range(NT):
                    def f_mm(e, lt=lt, n2=n2, it=it, ps=ps):
                        return e.matmul(
                            ps[:, 0:512],
                            big_a[:, it * L + lt * 128:it * L + lt * 128 + 128],
                            big_b[:, it * 1024 + n2 * 512:it * 1024 + n2 * 512 + 512],
                            start=(it == 0), stop=(it == 15))
                    kb.op("tensor", f_mm, r=[YG_ALL, WO_ALL], w=[pk])
                def f_ev(e, n2=n2, ps=ps):
                    return e.copy(big_b[:, 18432 + n2 * 512:18432 + n2 * 512 + 512],
                                  ps[:, 0:512])
                kb.op("scalar", f_ev, r=[pk],
                      w=[("big_b", 18432 + n2 * 512, 18432 + n2 * 512 + 512)])
            def f_sp(e, lt=lt):
                return e.dma_start(out=d_mix[lt], in_=big_b[:, 18432:19456])
            kb.op("sync", f_sp, kind="dma", r=[("big_b", 18432, 19456)],
                  w=[("d_mix", lt, lt + 1)])

        def emit_tail_a(lt):
            p = lt % 2
            mb = T2 + p * 1024
            xs = p * 1024
            x2s = 2048 + p * 1024
            sqs = 6180 + p * 1024
            l2s = 16384 if p == 0 else 25088
            l2t = 17408 if p == 0 else 26112
            def f_ldx(e, lt=lt, xs=xs):
                return e.dma_start(out=bigf[:, xs:xs + 1024],
                                   in_=dx[lt * 128:(lt + 1) * 128, :])
            kb.op("sync", f_ldx, kind="dma", w=[("bigf", xs, xs + 1024)])
            XT5 = ("bigf", xs, xs + 1024)

            sb0 = (lt % 4) * 32
            ln_ops(mb, lambda sqs=sqs: bigf[:, sqs:sqs + 1024],
                   ("bigf", sqs, sqs + 1024), sb0, sq_lo=sqs)
            kb.op("vector", lambda e, xs=xs, x2s=x2s, sqs=sqs: e.tensor_add(
                                                   bigf[:, x2s:x2s + 1024],
                                                   bigf[:, xs:xs + 1024],
                                                   bigf[:, sqs:sqs + 1024]),
                  r=[("bigf", sqs, sqs + 1024), XT5], w=[("bigf", x2s, x2s + 1024)])
            X2 = ("bigf", x2s, x2s + 1024)
            ln_ops(x2s, lambda l2s=l2s: big_b[:, l2s:l2s + 1024],
                   ("big_b", l2s, l2s + 1024), sb0 + 10, sq_lo=sqs)
            for dt_ in range(8):
                bank, bk = (psT, "psT") if dt_ % 2 == 0 else (psU, "psU")
                def f_tr(e, dt_=dt_, bank=bank, l2s=l2s):
                    return e.transpose(bank[:, 0:128],
                                       big_b[:, l2s + dt_ * 128:l2s + dt_ * 128 + 128],
                                       identb)
                kb.op("tensor", f_tr, r=[("big_b", l2s, l2s + 1024), ("big_b", 27136, 27264)],
                      w=[bk])
                def f_cp(e, dt_=dt_, bank=bank, l2t=l2t):
                    return e.copy(
                        big_b[:, l2t + dt_ * 128:l2t + dt_ * 128 + 128],
                        bank[:, 0:128])
                kb.op("scalar", f_cp, r=[bk],
                      w=[("big_b", l2t + dt_ * 128, l2t + dt_ * 128 + 128)])
            LN2T = ("big_b", l2t, l2t + 1024)
            for n2 in range(2):
                if p == 0:
                    ps, pk = (psC, "psC") if n2 == 0 else (psD, "psD")
                else:
                    ps, pk = (psE, "psE") if n2 == 0 else (psV, "psV")
                for dt_ in range(8):
                    if dt_ < 7:
                        rhs_ap = sm[:, dt_ * 1024 + n2 * 512:dt_ * 1024 + n2 * 512 + 512]
                        rhs_k = WEFF
                    else:
                        rhs_ap = big_b[:, 24064 + n2 * 512:24064 + n2 * 512 + 512]
                        rhs_k = ("big_b", 24064, 25088)
                    def f_mm(e, n2=n2, dt_=dt_, ps=ps, rhs_ap=rhs_ap, l2t=l2t):
                        return e.matmul(
                            ps[:, 0:512],
                            big_b[:, l2t + dt_ * 128:l2t + dt_ * 128 + 128],
                            rhs_ap,
                            start=(dt_ == 0), stop=(dt_ == 7))
                    kb.op("tensor", f_mm, r=[LN2T, rhs_k], w=[pk])

        def emit_tail_b(lt):
            p = lt % 2
            xs = p * 1024
            x2s = 2048 + p * 1024
            sqs = 6180 + p * 1024
            sb0 = (lt % 4) * 32
            X2 = ("bigf", x2s, x2s + 1024)
            for n2 in range(2):
                if p == 0:
                    ps, pk = (psC, "psC") if n2 == 0 else (psD, "psD")
                else:
                    ps, pk = (psE, "psE") if n2 == 0 else (psV, "psV")
                def f_ev(e, n2=n2, ps=ps, xs=xs):
                    return e.tensor_add(
                        bigf[:, xs + n2 * 512:xs + n2 * 512 + 512], ps[:, 0:512],
                        convh[:, n2 * 512:n2 * 512 + 512])
                kb.op("vector", f_ev, r=[pk, BEFF],
                      w=[("bigf", xs + n2 * 512, xs + n2 * 512 + 512)])
            ln_ops(xs, lambda sqs=sqs: bigf[:, sqs:sqs + 1024],
                   ("bigf", sqs, sqs + 1024), sb0 + 20, sq_lo=sqs)
            kb.op("vector", lambda e, sqs=sqs, x2s=x2s: e.tensor_add(
                                                   bigf[:, sqs:sqs + 1024],
                                                   bigf[:, x2s:x2s + 1024],
                                                   bigf[:, sqs:sqs + 1024]),
                  r=[("bigf", sqs, sqs + 1024), X2], w=[("bigf", sqs, sqs + 1024)])
            def f_out(e, lt=lt, sqs=sqs):
                return e.dma_start(out=dy[lt * 128:(lt + 1) * 128, :],
                                   in_=bigf[:, sqs:sqs + 1024])
            kb.op("sync", f_out, kind="dma", r=[("bigf", sqs, sqs + 1024)],
                  w=[("dy", lt, lt + 1)], sem="dout")

        emit_tail_a(0)
        for lt in range(NT):
            emit_tail_b(lt)
            if lt < 8:
                emit_mix_late(lt + 8)
            if lt + 2 < NT:
                emit_mix(lt + 2)
            if lt + 1 < NT:
                emit_tail_a(lt + 1)

        counters = kb.resolve()
        with nc.Block() as block:
            kb.emit(block, sems, final_waits=[("dout", counters.get("dout", 0))])

    return nc


_NC_CACHE = [None]


def kernel(**inputs):
    x = np.ascontiguousarray(inputs["L0_attn_inputs"], dtype=np.float32)
    W_in = np.ascontiguousarray(inputs["L0_attn_W_in"], dtype=np.float32)
    conv_w = np.ascontiguousarray(inputs["L0_attn_conv_w"], dtype=np.float32)
    conv_b = np.ascontiguousarray(inputs["L0_attn_conv_b"], dtype=np.float32)
    W_x = np.ascontiguousarray(inputs["L0_attn_W_x"], dtype=np.float32)
    W_dt = np.ascontiguousarray(inputs["L0_attn_W_dt"], dtype=np.float32)
    b_dt = np.ascontiguousarray(inputs["L0_attn_b_dt"], dtype=np.float32)
    A_log = np.ascontiguousarray(inputs["L0_attn_A_log"], dtype=np.float32)
    W_out = np.ascontiguousarray(inputs["L0_attn_W_out"], dtype=np.float32)
    ffn_W = np.ascontiguousarray(inputs["L0_ffn_W"], dtype=np.float32)
    ffn_b = np.ascontiguousarray(inputs["L0_ffn_b"], dtype=np.float32)
    logits = np.ascontiguousarray(inputs["L0_ffn_logits"], dtype=np.float32)
    u = np.ascontiguousarray(inputs["L0_ffn_u"], dtype=np.float32)

    W_in2 = np.ascontiguousarray(
        W_in.reshape(8, 128, 32, 128).transpose(1, 2, 0, 3))
    W_dt2 = np.ascontiguousarray(
        W_dt.reshape(8, 128, 16, 128).transpose(1, 2, 0, 3))
    conv_wt = np.ascontiguousarray(
        conv_w.reshape(4, 16, 128).transpose(2, 1, 0).reshape(128, 64))
    conv_b2 = np.ascontiguousarray(conv_b.reshape(16, 128).T)
    b_dt2 = np.ascontiguousarray(b_dt.reshape(16, 128).T)
    a_log2 = np.ascontiguousarray(
        A_log.reshape(16, 128, 16).transpose(1, 0, 2).reshape(128, 256))
    logits1 = np.ascontiguousarray(logits.reshape(1, 8))
    u1 = np.ascontiguousarray(u.reshape(1, 8))
    ident = np.eye(128, dtype=np.float32)
    sel16 = np.zeros((16, 2048), dtype=np.float32)
    for s in range(16):
        sel16[s, s * 128:(s + 1) * 128] = 1.0

    if _NC_CACHE[0] is None:
        _NC_CACHE[0] = build_nc()
    nc = _NC_CACHE[0]

    shared = {"w_in": W_in2, "w_x": W_x, "w_dt": W_dt2, "w_out": W_out,
              "ffn_w": ffn_W, "ffn_b": ffn_b, "conv_wt": conv_wt,
              "conv_b2": conv_b2, "b_dt2": b_dt2, "a_log2": a_log2,
              "logits1": logits1, "u1": u1, "ident": ident, "sel16": sel16}
    in_maps = [dict(shared, x=np.ascontiguousarray(x[b])) for b in range(NCORES)]
    res = run_bass_kernel_spmd(nc, in_maps, core_ids=list(range(NCORES)))
    return np.stack([np.asarray(res.results[b]["y"], dtype=np.float32)
                     for b in range(NCORES)], axis=0)
